# version 23
# baseline (speedup 1.0000x reference)
import math
import os
import contextlib
import numpy as np
import ml_dtypes
import concourse.bass as bass
import concourse.mybir as mybir
from concourse.bass_utils import run_bass_kernel_spmd

F32 = mybir.dt.float32
BF16 = mybir.dt.bfloat16
F16 = mybir.dt.float16
I32 = mybir.dt.int32
ALU = mybir.AluOpType
AF = mybir.ActivationFunctionType
AX = mybir.AxisListType

D = 2048
S = 2048
H = 16
DH = 128
IFF = 5632
NT = 16
OT = 8
NHC = 16
EPS = 1e-5
SM_SCALE = 1.0 / math.sqrt(DH)
P126 = float(2.0 ** 126)

CHUNKS = [(cc // 8, cc % 8, (128 if cc < 40 else 64)) for cc in range(48)]


def act_slot(cc):
    if cc < 40:
        return cc, 0, 128
    j = cc - 40
    return 40 + (j % 4), 64 * (j // 4), 64


class _RecInstr:
    __slots__ = ("incs",)

    def __init__(self):
        self.incs = []

    def then_inc(self, sem, n):
        self.incs.append((sem, n))
        return self


class _Rec:
    __slots__ = ("_log",)

    def __init__(self, log):
        self._log = log

    def __getattr__(self, name):
        def call(*a, **kw):
            ri = _RecInstr()
            self._log.append((name, a, kw, ri))
            return ri
        return call


class Build:
    def __init__(self, debug=False, upto="d"):
        self.upto = upto
        self.debug = debug
        self.nc = bass.Bass("TRN2", target_bir_lowering=False, debug=False)
        self.es = contextlib.ExitStack()
        self.rlog = {e: [] for e in ("sync", "tensor", "vector", "scalar", "gpsimd")}

    def sbuf(self, st, name, shape, d):
        return st.enter_context(self.nc.sbuf_tensor(name, shape, d))

    def ch(self, name):
        class Ch:
            pass
        c = Ch()
        c.h = self.es.enter_context(self.nc.semaphore(name))
        c.n = 0
        return c

    def on(self, eng, f):
        f(_Rec(self.rlog[eng]))

    def flush(self):
        for eng, log in self.rlog.items():
            if not log:
                continue
            ops = log[:]
            log.clear()

            def body(e, ops=ops):
                for name, a, kw, ri in ops:
                    inst = getattr(e, name)(*a, **kw)
                    for sem, n in ri.incs:
                        inst = inst.then_inc(sem, n)
            getattr(self.blk, eng)(body)

    def build(self):
        nc = self.nc
        es = self.es
        dt = nc.dram_tensor
        self.x_in = dt("x", [S, D], F32, kind="ExternalInput").ap()
        self.wq = dt("wq", [D, D], F32, kind="ExternalInput").ap()
        self.wk = dt("wk", [D, D], F32, kind="ExternalInput").ap()
        self.wv = dt("wv", [D, D], F32, kind="ExternalInput").ap()
        self.wo = dt("wo", [D, D], F32, kind="ExternalInput").ap()
        self.g_cols = dt("g_cols", [128, 16], F32, kind="ExternalInput").ap()
        self.lnb_cols = dt("lnb_cols", [128, 16], F32, kind="ExternalInput").ap()
        self.bo_cols = dt("bo_cols", [128, 16], F32, kind="ExternalInput").ap()
        self.b_rows = dt("b_rows", [4, D], F32, kind="ExternalInput").ap()
        self.gup_t = dt("gup_t", [256, 11264], I32, kind="ExternalInput").ap()
        self.gur = dt("gur", [128, 2, 22528], mybir.dt.int16, kind="ExternalInput").ap()
        self.down_t = dt("down_t", [704, D], I32, kind="ExternalInput").ap()
        self.dnr = dt("dnr", [128, 6, 2 * D], mybir.dt.int16, kind="ExternalInput").ap()
        self.bmask = dt("bmask", [128, 16, 128], BF16, kind="ExternalInput").ap()
        self.identf = dt("identf", [128, 128], F32, kind="ExternalInput").ap()
        self.ones_col = dt("ones_col", [128, 1], BF16, kind="ExternalInput").ap()
        self.ones_row = dt("ones_row", [1, 128], BF16, kind="ExternalInput").ap()
        self.m_sign = dt("m_sign", [128, 1], I32, kind="ExternalInput").ap()
        self.identb = dt("identb", [128, 128], BF16, kind="ExternalInput").ap()
        self.out_t = dt("out", [OT, 128, D], F32, kind="ExternalOutput").ap()
        self.kscr = dt("kscr", [4, S, 512], BF16, kind="Internal").ap()
        self.qscr = dt("qscr", [4, 1024, 512], BF16, kind="Internal").ap()
        self.x2scr = dt("x2scr", [OT, 128, D], F32, kind="Internal").ap()

        self.dbg = {}
        if self.debug:
            for name, shape, ddt in [
                ("d_xgT", [128, NHC, S], F32), ("d_V", [128, NT, D], F32),
                ("d_K", [4, S, 512], BF16), ("d_Q", [4, 1024, 512], BF16),
                ("d_AVn", [128, H, 512], F32), ("d_x2", [OT, 128, D], F32),
                ("d_act", [128, 44, 1024], F32), ("d_negb", [1, 16], F32),
                ("d_xfT", [128, NHC, 1024], F32), ("d_wgu", [128, 8, 512], F32),
                ("d_stats", [128, 32], F32), ("d_xh", [128, 4096], F32),
                ("d_ybf", [128, OT, D], F32), ("d_dd", [128, 4, 1024], F32),
                ("d_bp", [1, D], F32), ("d_bstage", [1, 1536], F32),
                ("d_ss", [128, 64], F32),
            ]:
                self.dbg[name] = dt(name, shape, ddt, kind="ExternalOutput").ap()

        sb = self.sbuf
        self.ident_s = sb(es, "ident_s", [128, 128], F32)
        self.ones_c = sb(es, "ones_c", [128, 1], BF16)
        self.ones_r = sb(es, "ones_r", [1, 128], BF16)
        self.msign_s = sb(es, "msign_s", [128, 1], I32)
        self.g_s = sb(es, "g_s", [128, 16], F32)
        self.lnb_s = sb(es, "lnb_s", [128, 16], F32)
        self.bo_s = sb(es, "bo_s", [128, 16], F32)
        self.lnb_bf = sb(es, "lnb_bf", [128, 16], BF16)
        self.stats = sb(es, "stats", [128, 32], F32)
        self.kn2 = sb(es, "kn2", [128, 16], F32)
        self.qn2 = sb(es, "qn2", [128, 16], F32)
        self.negb_rep = sb(es, "negb_rep", [128, 16], F32)
        self.bstat = sb(es, "bstat", [128, 4], F32)
        self.nrow = sb(es, "nrow", [16, 128], F32)
        self.rowscr = sb(es, "rowscr", [1, 512], F32)
        self.rowbf = sb(es, "rowbf", [1, 512], BF16)
        self.junkb = sb(es, "junkb", [128, 4096], BF16)
        self.junkc = sb(es, "junkc", [128, 512], BF16)
        self.ss = sb(es, "ss", [128, 64], F32)

        self.vreg = sb(es, "vreg", [128, 32768], BF16)
        self.vreg32 = self.vreg[:].bitcast(F32)
        self.V_t = self.vreg[:].rearrange("p (t d) -> p t d", t=NT)
        self.xraw = [self.vreg32[:, 0:2048], self.vreg32[:, 2048:4096]]
        self.xh32 = [self.vreg32[:, 4096:6144], self.vreg32[:, 6144:8192]]

        self.cL = self.ch("cL")
        self.cC = self.ch("cC")
        self.cG = self.ch("cG")
        self.cP = self.ch("cP")
        self.cV = self.ch("cV")
        self.cA = self.ch("cA")

        self.blk = es.enter_context(nc.Block())
        self.pa = es.enter_context(nc.psum_tensor("pa", [128, 1024], F32))
        self.pb = es.enter_context(nc.psum_tensor("pb", [128, 1024], F32))
        self.pcs = es.enter_context(nc.psum_tensor("pcs", [128, 1024], F32))
        self.pd = es.enter_context(nc.psum_tensor("pd", [128, 1024], F32))
        self.tp_slot = [self.pcs[:, 0:128], self.pcs[:, 512:640],
                        self.pd[:, 0:128], self.pd[:, 512:640]]
        self.tp_ctr = 0
        self.tp_tok = []
        self.tp_ev = []

        self.tok_ssv = {}
        self.tok_outv = {}
        self._last_oev = None
        order = ["p0", "a", "b0", "b1", "b2", "b3", "b", "c", "d0", "dgu", "dmm", "d"]
        stop = order.index(self.upto)
        self.phase_p0()
        if stop >= 1:
            self.phase_a()
        if stop >= 2:
            self.phase_b(min(stop - 2, 4))
        elif stop >= 1:
            self.flush()
            self.esAB.close()
        stop = max(stop - 4, 0)
        if stop >= 3:
            self.phase_c()
        if stop >= 4:
            self.phase_d(["d0", "dgu", "dmm", "d"][min(stop - 4, 3)])
        self.finish()
        self.flush()
        es.close()
        return self.nc, self.dbg

    def phase_p0(self):
        cC, cV = self.cC, self.cV

        def p0_sync(q):
            for dst, src in [(self.ident_s, self.identf),
                             (self.ones_c, self.ones_col), (self.ones_r, self.ones_row),
                             (self.msign_s, self.m_sign),
                             (self.g_s, self.g_cols), (self.lnb_s, self.lnb_cols),
                             (self.bo_s, self.bo_cols)]:
                q.dma_start(out=dst[:], in_=src[:]).then_inc(cC.h, 16)
                cC.n += 16
        self.on("sync", p0_sync)
        tok_consts = cC.n

        def p0_v(v):
            v.wait_ge(cC.h, tok_consts)
            v.tensor_copy(out=self.lnb_bf[:], in_=self.lnb_s[:])
            v.memset(self.kn2[:], 0.0)
            v.memset(self.qn2[:], 0.0)
            v.memset(self.ss[:], 0.0).then_inc(cV.h, 1)
            cV.n += 1
        self.on("vector", p0_v)
        self.tok_p0v = cV.n

    def pe_transpose(self, t, src_ap):
        cP, cV = self.cP, self.cV
        k = self.tp_ctr
        if k >= 4:
            t.wait_ge(cV.h, self.tp_ev[k - 4])
        t.transpose(self.tp_slot[k % 4], src_ap, self.ident_s[:]).then_inc(cP.h, 1)
        cP.n += 1
        self.tp_tok.append(cP.n)
        self.tp_ctr += 1
        return k

    def tp_evac_done(self, k, tok):
        assert len(self.tp_ev) == k
        self.tp_ev.append(tok)

    def emit_ln(self, tile_ap, c0, dual, tok_ready_ch, tok_ready):
        cL, cA, cV = self.cL, self.cA, self.cV
        st = lambda i: self.stats[:, c0 + i:c0 + i + 1]

        pred_prod = cV.n + 5

        jb = (c0 // 16) * 2048
        box = {}

        def sq(s):
            s.wait_ge(tok_ready_ch.h, tok_ready)
            s.activation(out=self.junkb[:, jb:jb + 1024], in_=tile_ap[:, 0:1024],
                         func=AF.Square, accum_out=st(0)).then_inc(cA.h, 1)
            cA.n += 1
            s.activation(out=self.junkb[:, jb + 1024:jb + 2048], in_=tile_ap[:, 1024:2048],
                         func=AF.Square, accum_out=st(1)).then_inc(cA.h, 1)
            cA.n += 1
            s.wait_ge(cA.h, cA.n)
            s.activation(out=st(3), in_=st(0), func=AF.Square).then_inc(cA.h, 1)
            cA.n += 1
            box["tok_sq"] = cA.n
            s.wait_ge(cV.h, pred_prod)
            s.activation(out=st(12), in_=st(8), func=AF.Ln)
            s.nop(cycle_cnt=1, nofuse=True).then_inc(cA.h, 1)
            cA.n += 1
            s.wait_ge(cA.h, cA.n)
            s.activation(out=st(13), in_=st(12), func=AF.Exp, scale=-0.5)
            s.nop(cycle_cnt=1, nofuse=True).then_inc(cA.h, 1)
            cA.n += 1
            s.wait_ge(cA.h, cA.n)
            s.activation(out=st(2), in_=st(13), func=AF.Exp).then_inc(cA.h, 1)
            cA.n += 1
        self.on("scalar", sq)
        tok_G = cA.n
        tok_sq = box["tok_sq"]

        def v1(v):
            v.wait_ge(tok_ready_ch.h, tok_ready)
            v.reduce_sum(out=st(4), in_=tile_ap, axis=AX.X)
            v.engine_nop().then_inc(cV.h, 1)
            cV.n += 1
            v.wait_ge(cV.h, cV.n)
            v.wait_ge(cA.h, tok_sq)
            v.tensor_tensor(out=st(5), in0=st(0), in1=st(1), op=ALU.add)
            v.tensor_scalar(out=st(6), in0=st(4), scalar1=1.0 / D, scalar2=None, op0=ALU.mult)
            v.engine_nop().then_inc(cV.h, 1)
            cV.n += 1
            v.wait_ge(cV.h, cV.n)
            v.tensor_tensor(out=st(7), in0=st(6), in1=st(6), op=ALU.mult)
            v.tensor_scalar(out=st(5), in0=st(5), scalar1=1.0 / D, scalar2=None, op0=ALU.mult)
            v.engine_nop().then_inc(cV.h, 1)
            cV.n += 1
            v.wait_ge(cV.h, cV.n)
            v.tensor_tensor(out=st(5), in0=st(5), in1=st(7), op=ALU.subtract)
            v.engine_nop().then_inc(cV.h, 1)
            cV.n += 1
            v.wait_ge(cV.h, cV.n)
            if dual:
                v.tensor_scalar(out=st(8), in0=st(5), scalar1=1.0 + EPS,
                                scalar2=EPS + EPS * EPS, op0=ALU.mult,
                                op1=ALU.add).then_inc(cV.h, 1)
            else:
                v.tensor_scalar(out=st(8), in0=st(5), scalar1=EPS, scalar2=None,
                                op0=ALU.add).then_inc(cV.h, 1)
            cV.n += 1
        self.on("vector", v1)
        assert cV.n == pred_prod
        return tok_G

    def phase_a(self):
        cL, cG, cP, cV, cA = self.cL, self.cG, self.cP, self.cV, self.cA
        nc = self.nc
        self.esAB = contextlib.ExitStack()
        self.xgT = self.sbuf(self.esAB, "xgT", [128, NHC, S], BF16)
        self.bprow_bf = self.sbuf(self.esAB, "bprow_bf", [1, 3, D], BF16)
        self.wqt = [self.sbuf(self.esAB, f"wquart{i}", [128, NHC, 512], BF16)
                    for i in range(2)]
        self.stage = self.sbuf(self.esAB, "stage", [128, 2, 512], BF16)

        tok_x = [0] * NT
        tok_xh = [0] * NT
        tok_tp_last = [0] * NT
        self.tok_xgT_tile = [0] * NT

        for tt in range(NT):
            pi = tt % 2
            c0 = 16 * pi

            def a_sync(q, tt=tt, pi=pi):
                if tt >= 2:
                    q.wait_ge(cV.h, tok_xh[tt - 2])
                q.wait_ge(cL.h, cL.n)
                q.dma_start(out=self.xraw[pi],
                            in_=self.x_in[tt * 128:(tt + 1) * 128, :]).then_inc(cL.h, 16)
                cL.n += 16
                tok_x[tt] = cL.n
            self.on("sync", a_sync)

            tok_G = self.emit_ln(self.xraw[pi], c0, True, cL, tok_x[tt])

            def a_v2(v, tt=tt, pi=pi, c0=c0, tok_G=tok_G):
                v.wait_ge(cA.h, tok_G)
                if tt == 0:
                    v.wait_ge(cV.h, self.tok_p0v)
                if tt >= 2:
                    v.wait_ge(cP.h, tok_tp_last[tt - 2])
                v.tensor_scalar(out=self.xh32[pi], in0=self.xraw[pi],
                                scalar1=self.stats[:, c0 + 6:c0 + 7],
                                scalar2=self.stats[:, c0 + 13:c0 + 14],
                                op0=ALU.subtract, op1=ALU.mult).then_inc(cV.h, 1)
                cV.n += 1
                tok_xh[tt] = cV.n
            self.on("vector", a_v2)

            for hc in range(NHC):
                kbox = []

                def a_pe(t, tt=tt, pi=pi, hc=hc, kbox=kbox):
                    if hc == 0:
                        t.wait_ge(cV.h, tok_xh[tt])
                    kbox.append(self.pe_transpose(
                        t, self.xh32[pi][:, hc * 128:(hc + 1) * 128]))
                    tok_tp_last[tt] = cP.n
                self.on("tensor", a_pe)

                def a_v3(v, tt=tt, hc=hc, kbox=kbox):
                    k = kbox[0]
                    v.wait_ge(cP.h, self.tp_tok[k])
                    v.tensor_scalar(out=self.xgT[:, hc, tt * 128:(tt + 1) * 128],
                                    in0=self.tp_slot[k % 4],
                                    scalar1=self.g_s[:, hc:hc + 1],
                                    scalar2=self.lnb_s[:, hc:hc + 1],
                                    op0=ALU.mult, op1=ALU.add).then_inc(cV.h, 1)
                    cV.n += 1
                    self.tp_evac_done(k, cV.n)
                self.on("vector", a_v3)
            self.tok_xgT_tile[tt] = cV.n

        self.tok_xgT_done = cV.n

        if self.debug:
            def a_dbg(g):
                g.wait_ge(cV.h, self.tok_xgT_done)
                g.dma_start(out=self.dbg["d_xgT"][:], in_=self.xgT[:]).then_inc(cG.h, 16)
                g.dma_start(out=self.dbg["d_stats"][:], in_=self.stats[:]).then_inc(cG.h, 16)
                g.dma_start(out=self.dbg["d_xh"][:],
                            in_=self.vreg32[:, 4096:8192]).then_inc(cG.h, 16)
                cG.n += 48
            self.on("gpsimd", a_dbg)

    def phase_b(self, sub=4):
        cL, cG, cP, cV, cA = self.cL, self.cG, self.cP, self.cV, self.cA
        pa, pb, pcs, pd = self.pa, self.pb, self.pcs, self.pd
        b_slots = [pa[:, 0:512], pa[:, 512:1024], pb[:, 0:512], pb[:, 512:1024]]
        mats = [("k", self.wk, NT, 1), ("v", self.wv, NT, 2), ("q", self.wq, OT, 0)]
        nmats = min(sub + 1, 3)
        mats = mats[:nmats]
        ALLQ = [(mi, qo) for mi in range(nmats) for qo in range(4)]

        tok_wq = {}
        tok_bp = {}
        self.tok_mm = tok_mm = {}
        tok_evac = {}
        self.tok_sp = tok_sp = {}
        tok_n2 = {}
        slot_hist = []
        stage_hist = []

        def prefetch_w(g, idx):
            mi, qo = ALLQ[idx]
            wt = mats[mi][1]
            if idx >= 2:
                pm, pq = ALLQ[idx - 2]
                g.wait_ge(cP.h, tok_mm[(pm, pq, mats[pm][2] - 1)])
            g.dma_start(out=self.wqt[idx % 2][:],
                        in_=wt[:, qo * 512:(qo + 1) * 512].rearrange("(c p) o -> p c o", p=128)
                        ).then_inc(cG.h, 16)
            cG.n += 16
            tok_wq[(mi, qo)] = cG.n

        def brow_cast(g):
            for mi2 in range(nmats):
                bi2 = mats[mi2][3]
                g.dma_start(out=self.bprow_bf[0:1, mi2, :],
                            in_=self.b_rows[bi2:bi2 + 1, :]).then_inc(cG.h, 16)
                cG.n += 16
        self.on("gpsimd", brow_cast)
        tok_brow = cG.n

        self.on("gpsimd", lambda g: prefetch_w(g, 0))
        self.on("gpsimd", lambda g: prefetch_w(g, 1))

        slot_ctr = 0
        for mi in range(nmats):
            name, wt, nts, bi = mats[mi]
            for qo in range(4):
                idx = mi * 4 + qo
                buf = self.wqt[idx % 2]
                if idx + 1 < 4 * nmats:
                    self.on("gpsimd", lambda g, i=idx + 1: prefetch_w(g, i))

                for ts in range(nts):
                    key = (mi, qo, ts)
                    slot = slot_ctr % 4
                    slot_ctr += 1
                    prev_slot = None
                    for k2, s2 in reversed(slot_hist):
                        if s2 == slot:
                            prev_slot = k2
                            break
                    slot_hist.append((key, slot))
                    prev_stage = None
                    if mi != 1:
                        for k2 in reversed(stage_hist):
                            if k2[2] % 2 == ts % 2:
                                prev_stage = k2
                                break
                        stage_hist.append(key)

                    def b_pe(t, key=key, slot=slot, prev_slot=prev_slot, buf=buf):
                        mi, qo, ts = key
                        t.wait_ge(cG.h, tok_wq[(mi, qo)])
                        t.wait_ge(cG.h, tok_brow)
                        t.wait_ge(cV.h, self.tok_xgT_tile[ts])
                        if prev_slot is not None:
                            t.wait_ge(cA.h, tok_evac[prev_slot])
                        ps = b_slots[slot]
                        for hc in range(NHC):
                            t.matmul(ps, self.xgT[:, hc, ts * 128:(ts + 1) * 128],
                                     buf[:, hc, :], start=(hc == 0), stop=False)
                        t.matmul(ps, self.ones_r[:],
                                 self.bprow_bf[0:1, mi, qo * 512:(qo + 1) * 512],
                                 start=False, stop=True).then_inc(cP.h, 1)
                        cP.n += 1
                        tok_mm[key] = cP.n
                    self.on("tensor", b_pe)

                    def b_evac(s, key=key, slot=slot, prev_stage=prev_stage):
                        mi, qo, ts = key
                        s.wait_ge(cP.h, tok_mm[key])
                        ps = b_slots[slot]
                        if mi == 1:
                            s.copy(out=self.V_t[:, ts, qo * 512:(qo + 1) * 512],
                                   in_=ps).then_inc(cA.h, 1)
                        else:
                            if prev_stage is not None:
                                s.wait_ge(cL.h, tok_sp[prev_stage])
                                s.wait_ge(cV.h, tok_n2[prev_stage])
                            s.copy(out=self.stage[:, ts % 2, :], in_=ps).then_inc(cA.h, 1)
                        cA.n += 1
                        tok_evac[key] = cA.n
                    self.on("scalar", b_evac)

                    if mi != 1:
                        def b_v(v, key=key):
                            mi, qo, ts = key
                            dst = self.kn2 if mi == 0 else self.qn2
                            v.wait_ge(cA.h, tok_evac[key])
                            v.tensor_tensor(out=self.junkc[:, 0:512],
                                            in0=self.stage[:, ts % 2, :],
                                            in1=self.stage[:, ts % 2, :], op=ALU.mult)
                            v.engine_nop().then_inc(cV.h, 1)
                            cV.n += 1
                            v.wait_ge(cV.h, cV.n)
                            v.reduce_sum(
                                out=self.bstat[:].rearrange("p (c o) -> p c o", o=1),
                                in_=self.junkc[:, 0:512].rearrange("p (c f) -> p c f", f=128),
                                axis=AX.X)
                            v.engine_nop().then_inc(cV.h, 1)
                            cV.n += 1
                            v.wait_ge(cV.h, cV.n)
                            v.tensor_tensor(out=dst[:, qo * 4:qo * 4 + 4],
                                            in0=dst[:, qo * 4:qo * 4 + 4],
                                            in1=self.bstat[:],
                                            op=ALU.max).then_inc(cV.h, 1)
                            cV.n += 1
                            tok_n2[key] = cV.n
                        self.on("vector", b_v)

                        def b_spill(q, key=key):
                            mi, qo, ts = key
                            q.wait_ge(cA.h, tok_evac[key])
                            scr = self.kscr if mi == 0 else self.qscr
                            q.wait_ge(cL.h, cL.n)
                            q.dma_start(out=scr[qo, ts * 128:(ts + 1) * 128, :],
                                        in_=self.stage[:, ts % 2, :]).then_inc(cL.h, 16)
                            cL.n += 16
                            tok_sp[key] = cL.n
                        self.on("sync", b_spill)

        tok_lastn2 = cV.n
        self.tok_b_pe_done = cP.n
        self.tok_sp_last = max(tok_sp.values())
        if sub < 4:
            if self.debug:
                def b_dbg0(g):
                    g.wait_ge(cP.h, self.tok_b_pe_done)
                    g.wait_ge(cL.h, self.tok_sp_last)
                    if nmats >= 2:
                        g.wait_ge(cA.h, tok_evac[(1, 3, NT - 1)])
                        g.dma_start(out=self.dbg["d_V"][:], in_=self.V_t[:]).then_inc(cG.h, 16)
                        cG.n += 16
                    g.dma_start(out=self.dbg["d_K"][:], in_=self.kscr[:]).then_inc(cG.h, 16)
                    g.dma_start(out=self.dbg["d_bp"][:],
                                in_=self.bprow_bf[0:1, 0, :]).then_inc(cG.h, 16)
                    cG.n += 32
                    if nmats >= 3:
                        g.dma_start(out=self.dbg["d_Q"][:], in_=self.qscr[:]).then_inc(cG.h, 16)
                        cG.n += 16
                self.on("gpsimd", b_dbg0)
            self.flush()
            self.esAB.close()
            return
        tok_evac_last = tok_evac[(2, 3, OT - 1)]

        def bnd_settle(v):
            v.engine_nop().then_inc(cV.h, 1)
            cV.n += 1
            v.wait_ge(cV.h, cV.n)
            v.engine_nop().then_inc(cV.h, 1)
            cV.n += 1
        self.on("vector", bnd_settle)
        tok_lastn2 = cV.n

        def bnd_pe1(t):
            t.wait_ge(cV.h, tok_lastn2)
            t.wait_ge(cA.h, tok_evac_last)
            t.transpose(pcs[0:16, 0:128], self.kn2[:], self.ident_s[:])
            t.transpose(pd[0:16, 0:128], self.qn2[:], self.ident_s[:]).then_inc(cP.h, 1)
            cP.n += 1
        self.on("tensor", bnd_pe1)
        tok_btp = cP.n

        def bnd_v1(v):
            v.wait_ge(cP.h, tok_btp)
            v.reduce_max(out=self.nrow[0:16, 0:1], in_=pcs[0:16, 0:128], axis=AX.X)
            v.reduce_max(out=self.nrow[0:16, 1:2], in_=pd[0:16, 0:128], axis=AX.X)
            v.engine_nop().then_inc(cV.h, 1)
            cV.n += 1
            v.wait_ge(cV.h, cV.n)
            v.tensor_tensor(out=self.nrow[0:16, 2:3], in0=self.nrow[0:16, 0:1],
                            in1=self.nrow[0:16, 1:2], op=ALU.mult)
            v.engine_nop().then_inc(cV.h, 1)
            cV.n += 1
            v.wait_ge(cV.h, cV.n)
            v.engine_nop().then_inc(cV.h, 1)
            cV.n += 1
        self.on("vector", bnd_v1)
        tok_bv1 = cV.n

        def bnd_a1(s):
            s.wait_ge(cV.h, tok_bv1)
            s.activation(out=self.nrow[0:16, 4:5], in_=self.nrow[0:16, 2:3], func=AF.Ln)
            s.nop(cycle_cnt=1, nofuse=True).then_inc(cA.h, 1)
            cA.n += 1
            s.wait_ge(cA.h, cA.n)
            s.activation(out=self.nrow[0:16, 5:6], in_=self.nrow[0:16, 4:5], func=AF.Exp,
                         scale=0.5)
            s.nop(cycle_cnt=1, nofuse=True).then_inc(cA.h, 1)
            cA.n += 1
            s.wait_ge(cA.h, cA.n)
            s.activation(out=self.nrow[0:16, 12:13], in_=self.nrow[0:16, 5:6],
                         func=AF.Ln).then_inc(cA.h, 1)
            cA.n += 1
        self.on("scalar", bnd_a1)
        tok_ba1 = cA.n

        def bnd_v2(v):
            v.wait_ge(cA.h, tok_ba1)
            v.tensor_scalar(out=self.nrow[0:16, 8:9], in0=self.nrow[0:16, 5:6],
                            scalar1=-1.02 * SM_SCALE, scalar2=None, op0=ALU.mult)
            v.engine_nop().then_inc(cV.h, 1)
            cV.n += 1
            v.wait_ge(cV.h, cV.n)
            v.engine_nop().then_inc(cV.h, 1)
            cV.n += 1
        self.on("vector", bnd_v2)
        tok_bv2 = cV.n

        def bnd_pe2(t):
            t.wait_ge(cV.h, tok_bv2)
            t.transpose(pcs[0:1, 128:144], self.nrow[0:16, 8:9],
                        self.ident_s[0:16, 0:16]).then_inc(cP.h, 1)
            cP.n += 1
        self.on("tensor", bnd_pe2)
        tok_bt2 = cP.n

        def bnd_v3(v):
            v.wait_ge(cP.h, tok_bt2)
            v.tensor_copy(out=self.rowscr[0:1, 0:16], in_=pcs[0:1, 128:144])
            v.engine_nop().then_inc(cV.h, 1)
            cV.n += 1
            v.wait_ge(cV.h, cV.n)
            v.tensor_copy(out=self.rowbf[0:1, 0:16], in_=self.rowscr[0:1, 0:16])
            v.engine_nop().then_inc(cV.h, 1)
            cV.n += 1
            v.wait_ge(cV.h, cV.n)
            v.engine_nop().then_inc(cV.h, 1)
            cV.n += 1
        self.on("vector", bnd_v3)
        tok_bv3 = cV.n

        def bnd_pe3(t):
            t.wait_ge(cV.h, tok_bv3)
            t.matmul(pcs[:, 128:144], self.ones_r[:], self.rowbf[0:1, 0:16],
                     start=True, stop=True).then_inc(cP.h, 1)
            cP.n += 1
        self.on("tensor", bnd_pe3)
        tok_brep = cP.n

        def bnd_a2(s):
            s.wait_ge(cP.h, tok_brep)
            s.copy(out=self.negb_rep[:], in_=pcs[:, 128:144]).then_inc(cA.h, 1)
            cA.n += 1
        self.on("scalar", bnd_a2)
        self.tok_negb = cA.n

        if self.debug:
            def b_dbg(g):
                g.wait_ge(cA.h, self.tok_negb)
                g.dma_start(out=self.dbg["d_V"][:], in_=self.V_t[:]).then_inc(cG.h, 16)
                g.wait_ge(cL.h, tok_sp[(0, 3, NT - 1)])
                g.wait_ge(cL.h, tok_sp[(2, 3, OT - 1)])
                g.dma_start(out=self.dbg["d_K"][:], in_=self.kscr[:]).then_inc(cG.h, 16)
                g.dma_start(out=self.dbg["d_Q"][:], in_=self.qscr[:]).then_inc(cG.h, 16)
                cG.n += 48
            self.on("gpsimd", b_dbg)

            def b_dbg2(q):
                q.wait_ge(cA.h, self.tok_negb)
                q.wait_ge(cL.h, cL.n)
                q.dma_start(out=self.dbg["d_negb"][:],
                            in_=self.rowscr[0:1, 0:16]).then_inc(cL.h, 16)
                cL.n += 16
            self.on("sync", b_dbg2)

        self.flush()
        self.esAB.close()

    def phase_c(self):
        cL, cG, cP, cV, cA = self.cL, self.cG, self.cP, self.cV, self.cA
        pa, pb, pcs, pd = self.pa, self.pb, self.pcs, self.pd
        esC = contextlib.ExitStack()
        woq = [self.sbuf(esC, f"woq{i}", [128, NHC, 512], BF16) for i in range(2)]
        self.tri_s = self.sbuf(esC, "tri_s", [128, 16, 128], BF16)
        borow32 = self.sbuf(esC, "borow32", [1, D], F32)
        identb_s = self.sbuf(esC, "identb_s", [128, 128], BF16)
        borow_bf = self.sbuf(esC, "borow_bf", [1, D], BF16)
        ones512 = self.sbuf(esC, "ones512", [1, 512], BF16)

        def c_triload(q):
            q.wait_ge(cP.h, self.tok_b_pe_done)
            q.wait_ge(cL.h, self.tok_sp_last)
            q.wait_ge(cL.h, cL.n)
            q.dma_start(out=self.tri_s[:], in_=self.bmask[:]).then_inc(cL.h, 16)
            q.wait_ge(cL.h, cL.n)
            q.dma_start(out=borow32[:], in_=self.b_rows[3:4, :]).then_inc(cL.h, 16)
            q.wait_ge(cL.h, cL.n)
            q.dma_start(out=identb_s[:], in_=self.identb[:]).then_inc(cL.h, 16)
            cL.n += 48
        self.on("sync", c_triload)
        tok_tri = cL.n

        def c_boprep(v):
            v.wait_ge(cL.h, tok_tri)
            v.memset(ones512[:], 1.0)
            v.tensor_copy(out=borow_bf[:], in_=borow32[:])
            v.engine_nop().then_inc(cV.h, 1)
            cV.n += 1
            v.wait_ge(cV.h, cV.n)
            v.engine_nop().then_inc(cV.h, 1)
            cV.n += 1
        self.on("vector", c_boprep)
        tok_bo = cV.n
        PT = self.sbuf(esC, "PT", [128, 16, 512], BF16)
        AVn = self.sbuf(esC, "AVn", [128, H, 512], BF16)
        KT = self.sbuf(esC, "KT", [128, 2, 2048], BF16)
        QT = self.sbuf(esC, "QT", [128, 2, 512], BF16)
        X2h = self.sbuf(esC, "X2h", [128, 4, 2048], F32)
        ostage = self.sbuf(esC, "ostage", [128, 2, 512], F32)
        reprec = self.sbuf(esC, "reprec", [128, 512], F32)

        tok_sp = self.tok_sp
        tok_kt = [0, 0]
        tok_qt = [0, 0]
        tok_ktfree = [0, 0]
        tok_x2l = {}
        tok_x2sp = {}
        tok_wol = {}
        tok_wo_used = {}
        tok_oadd = {}
        self.tok_x2sp = tok_x2sp

        def kt_valid(kt, qh):
            imin = kt if kt < 8 else kt - 8
            loc = 128 * imin - 512 * qh
            if loc >= 512:
                return None
            return max(loc, 0), (0 <= loc)

        for qh in range(2):
            def c_xload(q, qh=qh):
                if qh == 0:
                    q.wait_ge(cA.h, self.tok_negb)
                else:
                    q.wait_ge(cL.h, tok_x2sp[(0, 3)])
                for ii in range(4):
                    q.dma_start(out=X2h[:, ii, :],
                                in_=self.x_in[(4 * qh + ii) * 128:(4 * qh + ii + 1) * 128, :]
                                ).then_inc(cL.h, 16)
                    cL.n += 16
                    tok_x2l[(qh, ii)] = cL.n
            self.on("sync", c_xload)

            tok_avn = {}
            slot_exp = {0: None, 1: None, 2: None, 3: None}
            for h in range(H):
                def c_load(q, h=h, qh=qh):
                    q.wait_ge(cP.h, tok_ktfree[h % 2])
                    q.wait_ge(cL.h, tok_sp[(0, h // 4, NT - 1)])
                    q.dma_start_transpose(
                        out=KT[:, h % 2, :],
                        in_=self.kscr[h // 4, :, (h % 4) * 128:(h % 4 + 1) * 128]
                        ).then_inc(cL.h, 16)
                    cL.n += 16
                    tok_kt[h % 2] = cL.n
                    q.wait_ge(cL.h, tok_sp[(2, h // 4, OT - 1)])
                    q.dma_start_transpose(
                        out=QT[:, h % 2, :],
                        in_=self.qscr[h // 4, 512 * qh:512 * (qh + 1),
                                      (h % 4) * 128:(h % 4 + 1) * 128]).then_inc(cL.h, 16)
                    cL.n += 16
                    tok_qt[h % 2] = cL.n
                self.on("sync", c_load)

                kts = [kt for kt in range(16) if kt_valid(kt, qh) is not None]
                n_kt = len(kts)
                pe_base = cP.n
                pred_sc = {}
                pred_rav = {}
                NS = 4
                ev_list = [("S", i) for i in range(min(NS, n_kt))]
                for i in range(n_kt):
                    ev_list.append(("R", i))
                    if i + NS < n_kt:
                        ev_list.append(("S", i + NS))
                for off, (kind, i) in enumerate(ev_list):
                    if kind == "S":
                        pred_sc[i] = pe_base + off + 1
                    else:
                        pred_rav[i] = pe_base + off + 1
                exp_base = cA.n
                tok_exp = {kt: exp_base + i + 1 for i, kt in enumerate(kts)}

                score_slots = [pa[:, 0:512], pa[:, 512:1024], pb[:, 0:512], pb[:, 512:1024]]

                def c_exp_all(s, h=h, qh=qh, kts=kts, pred_sc=dict(pred_sc)):
                    for i, kt in enumerate(kts):
                        loc, _ = kt_valid(kt, qh)
                        sslot = score_slots[i % 4]
                        s.wait_ge(cP.h, pred_sc[i])
                        s.activation(out=PT[:, kt, loc:512], in_=sslot[:, loc:512],
                                     func=AF.Exp, bias=self.negb_rep[:, h:h + 1],
                                     scale=SM_SCALE).then_inc(cA.h, 1)
                        cA.n += 1
                self.on("scalar", c_exp_all)
                assert cA.n == exp_base + n_kt

                def c_pe_all(t, h=h, qh=qh, kts=kts, ev_list=list(ev_list)):
                    for kind, i in ev_list:
                        kt = kts[i]
                        loc, msk = kt_valid(kt, qh)
                        if kind == "S":
                            sslot = score_slots[i % 4]
                            t.wait_ge(cL.h, tok_kt[h % 2])
                            t.wait_ge(cL.h, tok_qt[h % 2])
                            if i >= 4:
                                t.wait_ge(cA.h, tok_exp[kts[i - 4]])
                            elif slot_exp[i % 4] is not None:
                                t.wait_ge(cA.h, slot_exp[i % 4])
                            if (qh, h) == (1, 0) and i < 4 and self._last_oev:
                                t.wait_ge(cA.h, self._last_oev)
                            inst = t.matmul(sslot[:, loc:512],
                                            KT[:, h % 2, kt * 128:(kt + 1) * 128],
                                            QT[:, h % 2, loc:512], start=True,
                                            stop=not msk)
                            if msk:
                                if (qh, h) == (0, 0):
                                    t.wait_ge(cL.h, tok_tri)
                                inst = t.matmul(sslot[:, loc:loc + 128],
                                                self.tri_s[:, kt, :], identb_s[:],
                                                start=False, stop=True)
                            inst.then_inc(cP.h, 1)
                            cP.n += 1
                            assert cP.n == pred_sc[i]
                        else:
                            first = i == 0
                            last = i == n_kt - 1
                            t.wait_ge(cA.h, tok_exp[kt])
                            av = pcs[:, 0:512] if h % 2 == 0 else pcs[:, 512:1024]
                            if first:
                                if h >= 2:
                                    t.wait_ge(cV.h, tok_avn[h - 2])
                                elif len(self.tp_ev):
                                    t.wait_ge(cV.h, self.tp_ev[-1])
                            t.matmul(pd[0:1, loc:512], self.ones_c[:], PT[:, kt, loc:512],
                                     start=first, stop=last)
                            t.matmul(av[:, loc:512],
                                     self.V_t[:, kt, h * 128:(h + 1) * 128],
                                     PT[:, kt, loc:512], start=first,
                                     stop=last).then_inc(cP.h, 1)
                            cP.n += 1
                            assert cP.n == pred_rav[i]
                self.on("tensor", c_pe_all)
                for i in range(n_kt):
                    slot_exp[i % 4] = tok_exp[kts[i]]

                tok_rav = cP.n
                tok_ktfree[h % 2] = cP.n

                def c_recip(s, tok_rav=tok_rav):
                    s.wait_ge(cP.h, tok_rav)
                    s.activation(out=self.rowscr[0:1, :], in_=pd[0:1, 0:512],
                                 func=AF.Ln)
                    s.nop(cycle_cnt=1, nofuse=True).then_inc(cA.h, 1)
                    cA.n += 1
                    s.wait_ge(cA.h, cA.n)
                    s.activation(out=self.rowbf[0:1, :], in_=self.rowscr[0:1, :],
                                 func=AF.Exp, scale=-1.0).then_inc(cA.h, 1)
                    cA.n += 1
                self.on("scalar", c_recip)
                tok_rc = cA.n

                def c_rep(t, tok_rc=tok_rc):
                    t.wait_ge(cA.h, tok_rc)
                    t.matmul(pd[:, 512:1024], self.ones_r[:], self.rowbf[0:1, :],
                             start=True, stop=True).then_inc(cP.h, 1)
                    cP.n += 1
                self.on("tensor", c_rep)
                tok_rep = cP.n

                def c_repc(s, tok_rep=tok_rep):
                    s.wait_ge(cP.h, tok_rep)
                    s.copy(out=reprec[:], in_=pd[:, 512:1024]).then_inc(cA.h, 1)
                    cA.n += 1
                self.on("scalar", c_repc)
                tok_repc = cA.n

                def c_avn(v, h=h, tok_repc=tok_repc):
                    v.wait_ge(cA.h, tok_repc)
                    av = pcs[:, 0:512] if h % 2 == 0 else pcs[:, 512:1024]
                    v.scalar_tensor_tensor(out=AVn[:, h, :], in0=av, scalar=1.0,
                                           in1=reprec[:], op0=ALU.mult,
                                           op1=ALU.mult).then_inc(cV.h, 1)
                    cV.n += 1
                    tok_avn[h] = cV.n
                self.on("vector", c_avn)
                tok_av_free = tok_avn[h]

            if self.debug and qh == 0:
                def c_dbg(g):
                    g.wait_ge(cV.h, tok_avn[H - 1])
                    g.dma_start(out=self.dbg["d_AVn"][:], in_=AVn[:]).then_inc(cG.h, 16)
                    cG.n += 16
                self.on("gpsimd", c_dbg)

            tok_opj = {}
            tok_oev = {}
            for oq in range(4):
                widx = qh * 4 + oq

                def c_wol(g, qh=qh, oq=oq, widx=widx):
                    if widx >= 2:
                        g.wait_ge(cP.h, tok_wo_used.get(widx - 2, 0))
                    else:
                        g.wait_ge(cP.h, self.tok_b_pe_done)
                        g.wait_ge(cL.h, self.tok_sp_last)
                    g.dma_start(out=woq[oq % 2][:],
                                in_=self.wo[:, oq * 512:(oq + 1) * 512].rearrange(
                                    "(c p) o -> p c o", p=128)).then_inc(cG.h, 16)
                    cG.n += 16
                    tok_wol[widx] = cG.n
                self.on("gpsimd", c_wol)

                for oc4 in range(4):
                    oc = oq * 4 + oc4
                    oslot = pb[:, 0:512] if oc % 2 == 0 else pb[:, 512:1024]

                    def c_opj(t, qh=qh, oq=oq, oc4=oc4, oc=oc, oslot=oslot, widx=widx):
                        t.wait_ge(cG.h, tok_wol[widx])
                        t.wait_ge(cV.h, tok_avn[H - 1])
                        if oc >= 2:
                            t.wait_ge(cA.h, tok_oev[oc - 2])
                        t.wait_ge(cV.h, tok_bo)
                        for hc in range(H):
                            t.matmul(oslot,
                                     woq[oq % 2][:, hc, oc4 * 128:(oc4 + 1) * 128],
                                     AVn[:, hc, :], start=(hc == 0), stop=False)
                        inst = t.matmul(oslot, borow_bf[0:1, oc * 128:(oc + 1) * 128],
                                        ones512[:], start=False, stop=True)
                        inst.then_inc(cP.h, 1)
                        cP.n += 1
                        tok_opj[oc] = cP.n
                        tok_wo_used[widx] = cP.n
                    self.on("tensor", c_opj)

                    def c_oev(s, qh=qh, oc=oc, oslot=oslot):
                        s.wait_ge(cP.h, tok_opj[oc])
                        if oc >= 2:
                            s.wait_ge(cV.h, tok_oadd[(qh, oc - 2)])
                        s.copy(out=ostage[:, oc % 2, :], in_=oslot).then_inc(cA.h, 1)
                        cA.n += 1
                        tok_oev[oc] = cA.n
                    self.on("scalar", c_oev)

                    slots = []

                    def c_otp(t, qh=qh, oc=oc, slots=slots):
                        t.wait_ge(cA.h, tok_oev[oc])
                        for ii in range(4):
                            k = self.pe_transpose(
                                t, ostage[:, oc % 2, ii * 128:(ii + 1) * 128])
                            slots.append(k)
                    self.on("tensor", c_otp)

                    def c_oadd(v, qh=qh, oc=oc, slots=slots):
                        for ii in range(4):
                            k = slots[ii]
                            v.wait_ge(cP.h, self.tp_tok[k])
                            v.wait_ge(cL.h, tok_x2l[(qh, ii)])
                            v.scalar_tensor_tensor(
                                out=X2h[:, ii, oc * 128:(oc + 1) * 128],
                                in0=self.tp_slot[k % 4], scalar=1.0,
                                in1=X2h[:, ii, oc * 128:(oc + 1) * 128],
                                op0=ALU.mult, op1=ALU.add).then_inc(cV.h, 1)
                            cV.n += 1
                            self.tp_evac_done(k, cV.n)
                        tok_oadd[(qh, oc)] = cV.n
                    self.on("vector", c_oadd)

            self._last_oev = cA.n

            def c_x2spill(q, qh=qh):
                q.wait_ge(cV.h, tok_oadd[(qh, 15)])
                for ii in range(4):
                    q.dma_start(out=self.x2scr[4 * qh + ii],
                                in_=X2h[:, ii, :]).then_inc(cL.h, 16)
                    cL.n += 16
                    tok_x2sp[(qh, ii)] = cL.n
            self.on("sync", c_x2spill)

        if self.debug:
            def c_dbg2(q):
                for i in range(OT):
                    q.wait_ge(cL.h, tok_x2sp[(i // 4, i % 4)])
                    q.dma_start(out=self.dbg["d_x2"][i],
                                in_=self.x2scr[i]).then_inc(cL.h, 16)
                    cL.n += 16
            self.on("sync", c_dbg2)

        self.flush()
        esC.close()

    def phase_d(self, sub="d"):
        cL, cG, cP, cV, cA = self.cL, self.cG, self.cP, self.cV, self.cA
        pa, pb, pcs, pd = self.pa, self.pb, self.pcs, self.pd
        nc = self.nc
        tok_x2sp = self.tok_x2sp

        esD = contextlib.ExitStack()
        act = self.sbuf(esD, "act", [128, 44, 1024], BF16)
        esD0 = contextlib.ExitStack()
        xfT = self.sbuf(esD0, "xfT", [128, NHC, 1024], BF16)

        vb = self.vreg
        x2t = [self.vreg32[:, 0:2048], self.vreg32[:, 2048:4096]]
        xf32 = [self.vreg32[:, 4096:6144], self.vreg32[:, 6144:8192]]
        dec_b = [vb[:, 0:8192].rearrange("p (s o) -> p s o", s=8),
                 vb[:, 8192:16384].rearrange("p (s o) -> p s o", s=8)]
        pk = vb[:, 16384:20480].bitcast(I32).rearrange(
            "p (b w o) -> p b w o", b=2, w=2)
        silu4 = vb[:, 28672:32768].bitcast(F32).rearrange("p (s o) -> p s o", s=4)
        srep = vb[:, 25600:27648].rearrange("p (b w o) -> p b w o", b=2, w=2)
        pint = vb[:, 27648:28160].bitcast(I32)
        p1s = vb[:, 28160:28672].bitcast(I32)

        tok_x2 = [0] * OT
        tok_xf = [0] * OT
        tok_tp_last = [0] * OT
        for i in range(OT):
            pi = i % 2
            c0 = 16 * pi

            def d0_sync(q, i=i, pi=pi):
                q.wait_ge(cL.h, tok_x2sp[(i // 4, i % 4)])
                if i >= 2:
                    q.wait_ge(cV.h, tok_xf[i - 2])
                q.dma_start(out=x2t[pi], in_=self.x2scr[i]).then_inc(cL.h, 16)
                cL.n += 16
                tok_x2[i] = cL.n
            self.on("sync", d0_sync)

            tok_G = self.emit_ln(x2t[pi], c0, False, cL, tok_x2[i])

            def d0_v2(v, i=i, pi=pi, c0=c0, tok_G=tok_G):
                v.wait_ge(cA.h, tok_G)
                if i >= 2:
                    v.wait_ge(cP.h, tok_tp_last[i - 2])
                v.tensor_scalar(out=xf32[pi], in0=x2t[pi],
                                scalar1=self.stats[:, c0 + 6:c0 + 7],
                                scalar2=self.stats[:, c0 + 13:c0 + 14],
                                op0=ALU.subtract, op1=ALU.mult).then_inc(cV.h, 1)
                cV.n += 1
                tok_xf[i] = cV.n
            self.on("vector", d0_v2)

            for idx in range(NHC):
                kbox = []

                def d0_pe(t, i=i, pi=pi, idx=idx, kbox=kbox):
                    if idx == 0:
                        t.wait_ge(cV.h, tok_xf[i])
                    src = xf32[pi].rearrange("p (a q j) -> p a j q", a=2, j=8)
                    kbox.append(self.pe_transpose(t, src[:, idx // 8, idx % 8, :]))
                    tok_tp_last[i] = cP.n
                self.on("tensor", d0_pe)

                def d0_v3(v, i=i, idx=idx, kbox=kbox):
                    k = kbox[0]
                    v.wait_ge(cP.h, self.tp_tok[k])
                    v.tensor_copy(out=xfT[:, idx, i * 128:(i + 1) * 128],
                                  in_=self.tp_slot[k % 4]).then_inc(cV.h, 1)
                    cV.n += 1
                    self.tp_evac_done(k, cV.n)
                self.on("vector", d0_v3)

        tok_xfT_done = cV.n

        if self.debug:
            def d0_dbg(g):
                g.wait_ge(cV.h, tok_xfT_done)
                g.dma_start(out=self.dbg["d_xfT"][:], in_=xfT[:]).then_inc(cG.h, 16)
                cG.n += 16
            self.on("gpsimd", d0_dbg)

        if sub == "d0":
            self.flush()
            esD0.close()
            esD.close()
            return
        pair_off = []
        off = 0
        for cc in range(48):
            w = CHUNKS[cc][2]
            pair_off.append((off, w))
            off += 2 * w
        assert off == 11264

        tok_pk = [0] * 48
        tok_gsc = [0] * 48
        tok_dec = [0] * 48
        tok_gumm = {}
        tok_actw = {}
        tok_actw_pred = {}
        gu_slots = [pa[:, 0:512], pa[:, 512:1024], pb[:, 0:512], pb[:, 512:1024]]
        gu_ctr = [0]
        gu_slot_hist = {}

        def gu_prefetch(g, cc):
            o0, w = pair_off[cc]
            span = 2 * w
            bi = cc % 2
            if cc >= 2:
                g.wait_ge(cV.h, tok_dec[cc - 2])
            else:
                g.wait_ge(cV.h, tok_xfT_done)
            g.dma_start(out=pk[:, bi, :, 0:span],
                        in_=self.gup_t[:, o0:o0 + span].rearrange(
                            "(c p) o -> p c o", p=128)).then_inc(cG.h, 16)
            cG.n += 16
            tok_pk[cc] = cG.n
            g.dma_start(out=srep[:, bi, :, 0:2 * span],
                        in_=self.gur[:, :, 2 * o0:2 * o0 + 2 * span].bitcast(F16)
                        ).then_inc(cG.h, 16)
            cG.n += 16
            tok_gsc[cc] = cG.n

        self.on("gpsimd", lambda g: gu_prefetch(g, 0))
        self.on("gpsimd", lambda g: gu_prefetch(g, 1))

        def emit_gu_dec(cc):
            bi = cc % 2
            o0, w = pair_off[cc]
            span = 2 * w
            dec = dec_b[bi]

            def gu_dec(v, cc=cc, bi=bi, span=span, dec=dec):
                v.wait_ge(cG.h, tok_gsc[cc])
                if cc >= 2:
                    v.wait_ge(cP.h, tok_gumm[(cc - 2, 1, 1)])
                for Wp in range(2):
                    for jp in range(4):
                        sh1 = 4 * jp - 6
                        if sh1 >= 0:
                            v.tensor_scalar(out=p1s[:, 0:span], in0=pk[:, bi, Wp, 0:span],
                                            scalar1=sh1, scalar2=0x01C001C0,
                                            op0=ALU.logical_shift_right, op1=ALU.bitwise_and)
                        else:
                            v.tensor_scalar(out=p1s[:, 0:span], in0=pk[:, bi, Wp, 0:span],
                                            scalar1=-sh1, scalar2=0x01C001C0,
                                            op0=ALU.logical_shift_left, op1=ALU.bitwise_and)
                        sh2 = 4 * jp - 12
                        v.tensor_scalar(out=pint[:, 0:span], in0=pk[:, bi, Wp, 0:span],
                                        scalar1=-sh2, scalar2=self.msign_s[:, 0:1],
                                        op0=ALU.logical_shift_left, op1=ALU.bitwise_and)
                        v.tensor_tensor(out=pint[:, 0:span], in0=p1s[:, 0:span],
                                        in1=pint[:, 0:span], op=ALU.bitwise_or)
                        v.scalar_tensor_tensor(out=dec[:, Wp * 4 + jp, 0:2 * span],
                                               in0=pint[:, 0:span].bitcast(BF16),
                                               scalar=P126,
                                               in1=srep[:, bi, Wp, 0:2 * span],
                                               op0=ALU.mult, op1=ALU.mult)
                v.engine_nop().then_inc(cV.h, 1)
                cV.n += 1
                tok_dec[cc] = cV.n
            self.on("vector", gu_dec)
            if cc + 2 < 48:
                self.on("gpsimd", lambda g, c=cc + 2: gu_prefetch(g, c))

        def emit_gu_mm(cc):
            W_d, j_d, wdt = CHUNKS[cc]
            bi = cc % 2
            dec = dec_b[bi]
            sl, po, _ = act_slot(cc)
            for qhf in range(2):
                for half in range(2):
                    k = gu_ctr[0]
                    slot = gu_slots[k % 4]
                    prevk = gu_slot_hist.get(k % 4)
                    gu_slot_hist[k % 4] = (cc, qhf, half)
                    gu_ctr[0] += 1

                    def gu_mm(t, cc=cc, qhf=qhf, half=half, slot=slot, prevk=prevk,
                              wdt=wdt, po=po, dec=dec):
                        t.wait_ge(cV.h, tok_dec[cc])
                        if prevk is not None:
                            t.wait_ge(cV.h, tok_actw[prevk[:2]])
                        decv = dec.rearrange("p s (o two) -> p s two o", two=2)
                        out = slot[po:po + wdt, :] if po else slot[0:wdt, :]
                        tp = (0, po) if po else None
                        for idx in range(NHC):
                            Wp, j = idx // 8, idx % 8
                            lhs = decv[:, Wp * 4 + (j % 4), j // 4,
                                       half * wdt:(half + 1) * wdt]
                            inst = t.matmul(out, lhs, xfT[:, idx, qhf * 512:(qhf + 1) * 512],
                                            start=(idx == 0), stop=(idx == NHC - 1),
                                            tile_position=tp)
                        inst.then_inc(cP.h, 1)
                        cP.n += 1
                        tok_gumm[(cc, qhf, half)] = cP.n
                    self.on("tensor", gu_mm)

                ss4 = (2 * cc + qhf) % 4

                def gu_silu(s, cc=cc, qhf=qhf, wdt=wdt, po=po, ss4=ss4):
                    gk = gu_ctr[0] - 2
                    gslot = gu_slots[gk % 4]
                    s.wait_ge(cP.h, tok_gumm[(cc, qhf, 0)])
                    if (cc, qhf) in tok_actw_pred:
                        s.wait_ge(cV.h, tok_actw_pred[(cc, qhf)])
                    src = gslot[po:po + wdt, :] if po else gslot[0:wdt, :]
                    dst = silu4[po:po + wdt, ss4, :] if po else silu4[0:wdt, ss4, :]
                    s.activation(out=dst, in_=src, func=AF.Silu).then_inc(cA.h, 1)
                    cA.n += 1
                self.on("scalar", gu_silu)
                tok_silu = cA.n

                def gu_act(v, cc=cc, qhf=qhf, wdt=wdt, po=po, sl=sl, tok_silu=tok_silu,
                           ss4=ss4):
                    uk = gu_ctr[0] - 1
                    uslot = gu_slots[uk % 4]
                    v.wait_ge(cA.h, tok_silu)
                    v.wait_ge(cP.h, tok_gumm[(cc, qhf, 1)])
                    src_u = uslot[po:po + wdt, :] if po else uslot[0:wdt, :]
                    src_s = silu4[po:po + wdt, ss4, :] if po else silu4[0:wdt, ss4, :]
                    dsta = act[po:po + wdt, sl, qhf * 512:(qhf + 1) * 512] if po else \
                        act[0:wdt, sl, qhf * 512:(qhf + 1) * 512]
                    v.scalar_tensor_tensor(out=dsta, in0=src_u, scalar=1.0, in1=src_s,
                                           op0=ALU.mult, op1=ALU.mult).then_inc(cV.h, 1)
                    cV.n += 1
                    tok_actw[(cc, qhf)] = cV.n
                self.on("vector", gu_act)
                nxt = (cc + 2, qhf) if qhf == 0 else (cc + 2, 1)
                tok_actw_pred[nxt] = tok_actw[(cc, qhf)]

        emit_gu_dec(0)
        for cc in range(48):
            if cc + 1 < 48:
                emit_gu_dec(cc + 1)
            emit_gu_mm(cc)

        tok_act_done = cV.n

        if self.debug:
            def gu_dbg(g):
                g.wait_ge(cV.h, tok_act_done)
                g.dma_start(out=self.dbg["d_act"][:], in_=act[:]).then_inc(cG.h, 16)
                cG.n += 16
            self.on("gpsimd", gu_dbg)

        self.flush()
        esD0.close()
        if sub == "dgu":
            esD.close()
            return

        dd3 = self.sbuf(esD, "dd3", [128, 3, 4, 1024], BF16)
        pintd = self.sbuf(esD, "pintd", [128, 512], I32)
        p1d = self.sbuf(esD, "p1d", [128, 512], I32)
        pkdw = self.sbuf(esD, "pkdw", [128, 2, 512], I32)
        srepd = self.sbuf(esD, "srepd", [128, 2, 1024], BF16)
        dd5c = self.sbuf(esD, "dd5c", [128, 4, 512], BF16)
        x2dn = self.vreg32[:, 0:2048]
        y_bf = self.vreg[:, 4096:20480].rearrange("p (t d) -> p t d", t=OT)

        y8 = [pa[:, 0:512], pa[:, 512:1024], pb[:, 0:512], pb[:, 512:1024],
              pcs[:, 0:512], pcs[:, 512:1024], pd[:, 0:512], pd[:, 512:1024]]
        tok_pkw = [0] * 24
        tok_dsc = [0] * 24
        tok_ddec = [0] * 24
        tok_wmm = [0] * 24
        tok_ymm = {}
        tok_yev = {}

        def dn_prefetch(g, k):
            hg, W = divmod(k, 6)
            if k >= 2:
                g.wait_ge(cV.h, tok_ddec[k - 2])
            else:
                g.wait_ge(cV.h, tok_act_done)
            if W < 5:
                g.dma_start(out=pkdw[:, k % 2, :],
                            in_=self.down_t[128 * W:128 * (W + 1), hg * 512:(hg + 1) * 512]
                            ).then_inc(cG.h, 16)
                cG.n += 16
            else:
                g.dma_start(out=pkdw[0:64, k % 2, :],
                            in_=self.down_t[640:704, hg * 512:(hg + 1) * 512]
                            ).then_inc(cG.h, 16)
                g.dma_start(out=pkdw[64:128, k % 2, :],
                            in_=self.down_t[640:704, hg * 512:(hg + 1) * 512]
                            ).then_inc(cG.h, 16)
                cG.n += 32
            tok_pkw[k] = cG.n
            g.dma_start(out=srepd[:, k % 2, :],
                        in_=self.dnr[:, W, hg * 1024:(hg + 1) * 1024].bitcast(F16)
                        ).then_inc(cG.h, 16)
            cG.n += 16
            tok_dsc[k] = cG.n

        self.on("gpsimd", lambda g: dn_prefetch(g, 0))
        self.on("gpsimd", lambda g: dn_prefetch(g, 1))

        def emit_ss(hg):
            for ts in range(OT):
                def dn_ss(v, hg=hg, ts=ts):
                    sc_ = 32 + 4 * (ts % 4)
                    ac_ = 4 * ts
                    v.wait_ge(cA.h, tok_yev[(hg, ts)])
                    yv = y_bf[:, ts, hg * 512:(hg + 1) * 512]
                    v.tensor_tensor(out=self.junkc[:, 0:512], in0=yv, in1=yv, op=ALU.mult)
                    v.engine_nop().then_inc(cV.h, 1)
                    cV.n += 1
                    v.wait_ge(cV.h, cV.n)
                    v.reduce_sum(out=self.ss[:, sc_:sc_ + 1],
                                 in_=self.junkc[:, 0:512], axis=AX.X)
                    v.engine_nop().then_inc(cV.h, 1)
                    cV.n += 1
                    v.wait_ge(cV.h, cV.n)
                    v.tensor_tensor(out=self.ss[:, ac_:ac_ + 1], in0=self.ss[:, ac_:ac_ + 1],
                                    in1=self.ss[:, sc_:sc_ + 1],
                                    op=ALU.add).then_inc(cV.h, 1)
                    cV.n += 1
                    self.tok_ssv[(hg, ts)] = cV.n
                self.on("vector", dn_ss)

        def emit_dn_dec(k):
            W = k % 6

            def dn_dec(v, k=k, W=W):
                v.wait_ge(cG.h, tok_dsc[k])
                if k >= 3:
                    v.wait_ge(cP.h, tok_wmm[k - 3])
                src = pkdw[:, k % 2, :]
                for jp in range(4):
                    sh1 = 4 * jp - 6
                    if sh1 >= 0:
                        v.tensor_scalar(out=p1d[:], in0=src, scalar1=sh1,
                                        scalar2=0x01C001C0,
                                        op0=ALU.logical_shift_right, op1=ALU.bitwise_and)
                    else:
                        v.tensor_scalar(out=p1d[:], in0=src, scalar1=-sh1,
                                        scalar2=0x01C001C0,
                                        op0=ALU.logical_shift_left, op1=ALU.bitwise_and)
                    sh2 = 4 * jp - 12
                    v.tensor_scalar(out=pintd[:], in0=src, scalar1=-sh2,
                                    scalar2=self.msign_s[:, 0:1],
                                    op0=ALU.logical_shift_left, op1=ALU.bitwise_and)
                    v.tensor_tensor(out=pintd[:], in0=p1d[:],
                                    in1=pintd[:], op=ALU.bitwise_or)
                    v.scalar_tensor_tensor(out=dd3[:, k % 3, jp, :],
                                           in0=pintd[:].bitcast(BF16), scalar=P126,
                                           in1=srepd[:, k % 2, :],
                                           op0=ALU.mult, op1=ALU.mult)
                    if W == 5:
                        dv = dd3[:, k % 3, jp, :].rearrange("p (h two) -> p two h", two=2)
                        v.tensor_copy(out=dd5c[0:64, jp, :], in_=dv[0:64, 0, :])
                        v.tensor_copy(out=dd5c[64:128, jp, :], in_=dv[64:128, 1, :])
                v.engine_nop().then_inc(cV.h, 1)
                cV.n += 1
                tok_ddec[k] = cV.n
            self.on("vector", dn_dec)
            if k + 2 < 24:
                self.on("gpsimd", lambda g, k2=k + 2: dn_prefetch(g, k2))

        def emit_dn_mm(k):
            hg, W = divmod(k, 6)

            def dn_mm(t, k=k, hg=hg, W=W):
                t.wait_ge(cV.h, tok_ddec[k])
                inst = None
                for ts in range(OT):
                    slot = y8[ts]
                    if W == 0:
                        if hg == 0:
                            t.wait_ge(cV.h, tok_act_done)
                        else:
                            t.wait_ge(cA.h, tok_yev[(hg - 1, ts)])
                    if W < 5:
                        for j in range(8):
                            cc = 8 * W + j
                            ddv = dd3[:, k % 3, j % 4, :].rearrange(
                                "p (o two) -> p two o", two=2)
                            inst = t.matmul(slot, act[0:128, cc, ts * 128:(ts + 1) * 128],
                                            ddv[:, j // 4, :],
                                            start=(W == 0 and j == 0), stop=False)
                    else:
                        for m in range(4):
                            inst = t.matmul(slot, act[0:128, 40 + m, ts * 128:(ts + 1) * 128],
                                            dd5c[:, m, :], start=False, stop=(m == 3))
                        inst.then_inc(cP.h, 1)
                        cP.n += 1
                        tok_ymm[(hg, ts)] = cP.n
                if W < 5:
                    inst.then_inc(cP.h, 1)
                    cP.n += 1
                tok_wmm[k] = cP.n
            self.on("tensor", dn_mm)

            if W == 5:
                for ts in range(OT):
                    def dn_yev(s, hg=hg, ts=ts):
                        s.wait_ge(cP.h, tok_ymm[(hg, ts)])
                        s.copy(out=y_bf[:, ts, hg * 512:(hg + 1) * 512],
                               in_=y8[ts]).then_inc(cA.h, 1)
                        cA.n += 1
                        tok_yev[(hg, ts)] = cA.n
                    self.on("scalar", dn_yev)
                emit_ss(hg)

        emit_dn_dec(0)
        for k in range(24):
            if k + 1 < 24:
                emit_dn_dec(k + 1)
            emit_dn_mm(k)
        tok_ssv_last = cV.n
        if self.debug:
            def dall_dbg(g):
                g.wait_ge(cV.h, tok_ssv_last)
                g.dma_start(out=self.dbg["d_ybf"][:],
                            in_=y_bf[:, :, :]).then_inc(cG.h, 16)
                for q_ in range(4):
                    g.dma_start(out=self.dbg["d_dd"][:, q_, :],
                                in_=dd3[:, 23 % 3, q_, :]).then_inc(cG.h, 16)
                g.dma_start(out=self.dbg["d_ss"][:], in_=self.ss[:]).then_inc(cG.h, 16)
                cG.n += 96
            self.on("gpsimd", dall_dbg)
        if sub == "dmm":
            if self.debug:
                def dmm_dbg(g):
                    g.wait_ge(cV.h, tok_ssv_last)
                    g.dma_start(out=self.dbg["d_ybf"][:],
                                in_=y_bf[:, :, :]).then_inc(cG.h, 16)
                    cG.n += 16
                self.on("gpsimd", dmm_dbg)
            self.flush()
            esD.close()
            return

        tok_x2d = [0] * OT
        tok_outs = [0] * OT
        for ts in range(OT):
            def ep_sync(q, ts=ts):
                q.wait_ge(cV.h, self.tok_ssv[(3, ts)])
                if ts >= 1:
                    q.wait_ge(cL.h, tok_outs[ts - 1])
                q.dma_start(out=x2dn, in_=self.x2scr[ts]).then_inc(cL.h, 16)
                cL.n += 16
                tok_x2d[ts] = cL.n
            self.on("sync", ep_sync)

            def ep_rms_v(v, ts=ts):
                p_ = 48 + 4 * (ts % 2)
                v.tensor_scalar(out=self.ss[:, p_:p_ + 1], in0=self.ss[:, 4 * ts:4 * ts + 1],
                                scalar1=1.0 / D, scalar2=1e-6, op0=ALU.mult,
                                op1=ALU.add).then_inc(cV.h, 1)
                cV.n += 1
            self.on("vector", ep_rms_v)
            tok_ms = cV.n

            def ep_rms_a(s, ts=ts, tok_ms=tok_ms):
                b_ = 56 + 4 * (ts % 2)
                m_ = 48 + 4 * (ts % 2)
                s.wait_ge(cV.h, tok_ms)
                s.activation(out=self.ss[:, b_:b_ + 1], in_=self.ss[:, m_:m_ + 1],
                             func=AF.Ln)
                s.nop(cycle_cnt=1, nofuse=True).then_inc(cA.h, 1)
                cA.n += 1
                s.wait_ge(cA.h, cA.n)
                s.activation(out=self.ss[:, b_ + 1:b_ + 2], in_=self.ss[:, b_:b_ + 1],
                             func=AF.Exp, scale=-0.5)
                s.nop(cycle_cnt=1, nofuse=True).then_inc(cA.h, 1)
                cA.n += 1
                s.wait_ge(cA.h, cA.n)
                s.activation(out=self.ss[:, b_ + 2:b_ + 3], in_=self.ss[:, b_ + 1:b_ + 2],
                             func=AF.Exp).then_inc(cA.h, 1)
            self.on("scalar", ep_rms_a)
            cA.n += 1
            tok_rr = cA.n

            def ep_out_v(v, ts=ts, tok_rr=tok_rr):
                v.wait_ge(cA.h, tok_rr)
                v.wait_ge(cL.h, tok_x2d[ts])
                v.scalar_tensor_tensor(out=x2dn, in0=y_bf[:, ts, :],
                                       scalar=self.ss[:, 57 + 4 * (ts % 2):58 + 4 * (ts % 2)],
                                       in1=x2dn, op0=ALU.mult,
                                       op1=ALU.add).then_inc(cV.h, 1)
                cV.n += 1
                self.tok_outv[ts] = cV.n
            self.on("vector", ep_out_v)

            def ep_store(q, ts=ts):
                q.wait_ge(cV.h, self.tok_outv[ts])
                q.dma_start(out=self.out_t[ts], in_=x2dn).then_inc(cL.h, 16)
                cL.n += 16
                tok_outs[ts] = cL.n
            self.on("sync", ep_store)

        self.tok_final = cL.n
        self.flush()
        esD.close()

    def finish(self):
        cL, cG = self.cL, self.cG
        tokL, tokG = cL.n, cG.n

        def fin(g):
            g.wait_ge(cL.h, tokL)
            g.wait_ge(cG.h, tokG)
        self.on("gpsimd", fin)


def _host_prep(inputs):
    X = np.ascontiguousarray(np.asarray(inputs["L0_attn_X"], dtype=np.float32))
    pr = {}
    pr["wq"] = np.ascontiguousarray(np.asarray(inputs["L0_attn_Wq"], np.float32))
    pr["wk"] = np.ascontiguousarray(np.asarray(inputs["L0_attn_Wk"], np.float32))
    pr["wv"] = np.ascontiguousarray(np.asarray(inputs["L0_attn_Wv"], np.float32))
    pr["wo"] = np.ascontiguousarray(np.asarray(inputs["L0_attn_Wo"], np.float32))
    g = np.asarray(inputs["L0_attn_ln_g"], np.float32)
    lb = np.asarray(inputs["L0_attn_ln_b"], np.float32)
    bo = np.asarray(inputs["L0_attn_bo"], np.float32)
    pr["g_cols"] = np.ascontiguousarray(g.reshape(16, 128).T)
    pr["lnb_cols"] = np.ascontiguousarray(lb.reshape(16, 128).T)
    pr["bo_cols"] = np.ascontiguousarray(bo.reshape(16, 128).T)
    pr["b_rows"] = np.ascontiguousarray(np.stack([
        np.asarray(inputs["L0_attn_bq"], np.float32),
        np.asarray(inputs["L0_attn_bk"], np.float32),
        np.asarray(inputs["L0_attn_bv"], np.float32),
        bo]))

    rows = []
    for cc in range(48):
        W, j, w = CHUNKS[cc]
        ilist = [8 * wd + j for wd in range(128 * W, 128 * W + w)]
        rows.extend(ilist)
        rows.extend([IFF + i for i in ilist])
    rows = np.array(rows)
    gup = np.asarray(inputs["L0_ffn_gate_up_packed"])
    pr["gup_t"] = np.ascontiguousarray(gup[rows].T)
    sc = np.asarray(inputs["L0_ffn_gate_up_scales"])
    scp = np.repeat(sc[:, rows], 2, axis=1)
    gur = np.broadcast_to(scp.reshape(2, 8, 1, 22528),
                          (2, 8, 16, 22528)).reshape(2, 128, 22528)
    pr["gur"] = np.ascontiguousarray(gur.transpose(1, 0, 2)).view(np.int16)
    dn = np.asarray(inputs["L0_ffn_down_packed"])
    pr["down_t"] = np.ascontiguousarray(dn.T)
    ds = np.asarray(inputs["L0_ffn_down_scales"])
    dsp = np.zeros((48, D), ds.dtype)
    dsp[:44] = ds
    dsd = np.repeat(dsp, 2, axis=1)
    base = dsd.reshape(6, 8, 2 * D)
    dnr = np.repeat(base, 16, axis=1)
    dnr = dnr.copy()
    dnr[5] = np.tile(np.repeat(base[5][:4], 16, axis=0), (2, 1))
    pr["dnr"] = np.ascontiguousarray(dnr.transpose(1, 0, 2)).view(np.int16)

    pr["identf"] = np.eye(128, dtype=np.float32)
    pr["ones_col"] = np.ones((128, 1), ml_dtypes.bfloat16)
    pr["ones_row"] = np.ones((1, 128), ml_dtypes.bfloat16)
    pr["m_sign"] = np.full((128, 1), np.uint32(0x80008000)).view(np.int32)
    pr["identb"] = np.eye(128, dtype=ml_dtypes.bfloat16)
    return X, pr


def _bmask(par):
    m = np.zeros((16, 128, 128), np.float32)
    kq = np.where(np.arange(128)[:, None] <= np.arange(128)[None, :], 0.0, -1e30)
    for kt in range(8):
        m[kt] = kq
    if par == 0:
        m[8:] = -1e30
    return np.ascontiguousarray(m.transpose(2, 0, 1).astype(ml_dtypes.bfloat16))


def kernel(**inputs):
    X, pr = _host_prep(inputs)
    nc, _ = Build(debug=False).build()
    in_maps = []
    for c in range(8):
        b, par = c // 2, c & 1
        permtok = np.concatenate(
            [np.arange(128) + 128 * (par + 2 * i) for i in range(8)]
            + [np.arange(128) + 128 * ((1 - par) + 2 * i) for i in range(8)])
        im = dict(pr)
        im["x"] = np.ascontiguousarray(X[b][permtok])
        im["bmask"] = _bmask(par)
        in_maps.append(im)
    res = run_bass_kernel_spmd(nc, in_maps, core_ids=list(range(8)))
    out = np.empty((4, S, D), np.float32)
    for c in range(8):
        b, par = c // 2, c & 1
        o = np.asarray(res.results[c]["out"]).reshape(OT, 128, D)
        for i in range(OT):
            out[b, 128 * (par + 2 * i):128 * (par + 2 * i) + 128] = o[i]
    return out


# version 24
# speedup vs baseline: 1.0500x; 1.0500x over previous
import math
import os
import contextlib
import numpy as np
import ml_dtypes
import concourse.bass as bass
import concourse.mybir as mybir
from concourse.bass_utils import run_bass_kernel_spmd

F32 = mybir.dt.float32
BF16 = mybir.dt.bfloat16
F16 = mybir.dt.float16
I32 = mybir.dt.int32
ALU = mybir.AluOpType
AF = mybir.ActivationFunctionType
AX = mybir.AxisListType

D = 2048
S = 2048
H = 16
DH = 128
IFF = 5632
NT = 16
OT = 8
NHC = 16
EPS = 1e-5
SM_SCALE = 1.0 / math.sqrt(DH)
P126 = float(2.0 ** 126)

CHUNKS = [(cc // 8, cc % 8, (128 if cc < 40 else 64)) for cc in range(48)]


def act_slot(cc):
    if cc < 40:
        return cc, 0, 128
    j = cc - 40
    return 40 + (j % 4), 64 * (j // 4), 64


class _RecInstr:
    __slots__ = ("incs",)

    def __init__(self):
        self.incs = []

    def then_inc(self, sem, n):
        self.incs.append((sem, n))
        return self


class _Rec:
    __slots__ = ("_log",)

    def __init__(self, log):
        self._log = log

    def __getattr__(self, name):
        def call(*a, **kw):
            ri = _RecInstr()
            self._log.append((name, a, kw, ri))
            return ri
        return call


class Build:
    def __init__(self, debug=False, upto="d"):
        self.upto = upto
        self.debug = debug
        self.nc = bass.Bass("TRN2", target_bir_lowering=False, debug=False)
        self.es = contextlib.ExitStack()
        self.rlog = {e: [] for e in ("sync", "tensor", "vector", "scalar", "gpsimd")}

    def sbuf(self, st, name, shape, d):
        return st.enter_context(self.nc.sbuf_tensor(name, shape, d))

    def ch(self, name):
        class Ch:
            pass
        c = Ch()
        c.h = self.es.enter_context(self.nc.semaphore(name))
        c.n = 0
        return c

    def on(self, eng, f):
        f(_Rec(self.rlog[eng]))

    def flush(self):
        for eng, log in self.rlog.items():
            if not log:
                continue
            ops = log[:]
            log.clear()

            def body(e, ops=ops):
                for name, a, kw, ri in ops:
                    inst = getattr(e, name)(*a, **kw)
                    for sem, n in ri.incs:
                        inst = inst.then_inc(sem, n)
            getattr(self.blk, eng)(body)

    def build(self):
        nc = self.nc
        es = self.es
        dt = nc.dram_tensor
        self.x_in = dt("x", [S, D], F32, kind="ExternalInput").ap()
        self.wq = dt("wq", [D, D], F32, kind="ExternalInput").ap()
        self.wk = dt("wk", [D, D], F32, kind="ExternalInput").ap()
        self.wv = dt("wv", [D, D], F32, kind="ExternalInput").ap()
        self.wo = dt("wo", [D, D], F32, kind="ExternalInput").ap()
        self.g_cols = dt("g_cols", [128, 16], F32, kind="ExternalInput").ap()
        self.lnb_cols = dt("lnb_cols", [128, 16], F32, kind="ExternalInput").ap()
        self.bo_cols = dt("bo_cols", [128, 16], F32, kind="ExternalInput").ap()
        self.b_rows = dt("b_rows", [4, D], F32, kind="ExternalInput").ap()
        self.gup_t = dt("gup_t", [256, 11264], I32, kind="ExternalInput").ap()
        self.gur = dt("gur", [128, 2, 22528], mybir.dt.int16, kind="ExternalInput").ap()
        self.down_t = dt("down_t", [704, D], I32, kind="ExternalInput").ap()
        self.dnr = dt("dnr", [128, 6, 2 * D], mybir.dt.int16, kind="ExternalInput").ap()
        self.bmask = dt("bmask", [128, 16, 128], BF16, kind="ExternalInput").ap()
        self.identf = dt("identf", [128, 128], F32, kind="ExternalInput").ap()
        self.ones_col = dt("ones_col", [128, 1], BF16, kind="ExternalInput").ap()
        self.ones_row = dt("ones_row", [1, 128], BF16, kind="ExternalInput").ap()
        self.m_sign = dt("m_sign", [128, 1], I32, kind="ExternalInput").ap()
        self.identb = dt("identb", [128, 128], BF16, kind="ExternalInput").ap()
        self.out_t = dt("out", [OT, 128, D], F32, kind="ExternalOutput").ap()
        self.kscr = dt("kscr", [4, S, 512], BF16, kind="Internal").ap()
        self.qscr = dt("qscr", [4, 1024, 512], BF16, kind="Internal").ap()
        self.x2scr = dt("x2scr", [OT, 128, D], F32, kind="Internal").ap()

        self.dbg = {}
        if self.debug:
            for name, shape, ddt in [
                ("d_xgT", [128, NHC, S], F32), ("d_V", [128, NT, D], F32),
                ("d_K", [4, S, 512], BF16), ("d_Q", [4, 1024, 512], BF16),
                ("d_AVn", [128, H, 512], F32), ("d_x2", [OT, 128, D], F32),
                ("d_act", [128, 44, 1024], F32), ("d_negb", [1, 16], F32),
                ("d_xfT", [128, NHC, 1024], F32), ("d_wgu", [128, 8, 512], F32),
                ("d_stats", [128, 32], F32), ("d_xh", [128, 4096], F32),
                ("d_ybf", [128, OT, D], F32), ("d_dd", [128, 4, 1024], F32),
                ("d_bp", [1, D], F32), ("d_bstage", [1, 1536], F32),
                ("d_ss", [128, 64], F32),
            ]:
                self.dbg[name] = dt(name, shape, ddt, kind="ExternalOutput").ap()

        sb = self.sbuf
        self.ident_s = sb(es, "ident_s", [128, 128], F32)
        self.ones_c = sb(es, "ones_c", [128, 1], BF16)
        self.ones_r = sb(es, "ones_r", [1, 128], BF16)
        self.msign_s = sb(es, "msign_s", [128, 1], I32)
        self.g_s = sb(es, "g_s", [128, 16], F32)
        self.lnb_s = sb(es, "lnb_s", [128, 16], F32)
        self.bo_s = sb(es, "bo_s", [128, 16], F32)
        self.lnb_bf = sb(es, "lnb_bf", [128, 16], BF16)
        self.stats = sb(es, "stats", [128, 32], F32)
        self.kn2 = sb(es, "kn2", [128, 16], F32)
        self.qn2 = sb(es, "qn2", [128, 16], F32)
        self.negb_rep = sb(es, "negb_rep", [128, 16], F32)
        self.bstat = sb(es, "bstat", [128, 4], F32)
        self.nrow = sb(es, "nrow", [16, 128], F32)
        self.rowscr = sb(es, "rowscr", [1, 512], F32)
        self.rowbf = sb(es, "rowbf", [1, 512], BF16)
        self.junkb = sb(es, "junkb", [128, 4096], BF16)
        self.junkc = sb(es, "junkc", [128, 512], BF16)
        self.ss = sb(es, "ss", [128, 64], F32)

        self.vreg = sb(es, "vreg", [128, 32768], BF16)
        self.vreg32 = self.vreg[:].bitcast(F32)
        self.V_t = self.vreg[:].rearrange("p (t d) -> p t d", t=NT)
        self.xraw = [self.vreg32[:, 0:2048], self.vreg32[:, 2048:4096]]
        self.xh32 = [self.vreg32[:, 4096:6144], self.vreg32[:, 6144:8192]]

        self.cL = self.ch("cL")
        self.cC = self.ch("cC")
        self.cG = self.ch("cG")
        self.cP = self.ch("cP")
        self.cV = self.ch("cV")
        self.cA = self.ch("cA")

        self.blk = es.enter_context(nc.Block())
        self.pa = es.enter_context(nc.psum_tensor("pa", [128, 1024], F32))
        self.pb = es.enter_context(nc.psum_tensor("pb", [128, 1024], F32))
        self.pcs = es.enter_context(nc.psum_tensor("pcs", [128, 1024], F32))
        self.pd = es.enter_context(nc.psum_tensor("pd", [128, 1024], F32))
        self.tp_slot = [self.pcs[:, 0:128], self.pcs[:, 512:640],
                        self.pd[:, 0:128], self.pd[:, 512:640]]
        self.tp_ctr = 0
        self.tp_tok = []
        self.tp_ev = []

        self.tok_ssv = {}
        self.tok_outv = {}
        self._last_oev = None
        order = ["p0", "a", "b0", "b1", "b2", "b3", "b", "c", "d0", "dgu", "dmm", "d"]
        stop = order.index(self.upto)
        self.phase_p0()
        if stop >= 1:
            self.phase_a()
        if stop >= 2:
            self.phase_b(min(stop - 2, 4))
        elif stop >= 1:
            self.flush()
            self.esAB.close()
        stop = max(stop - 4, 0)
        if stop >= 3:
            self.phase_c()
        if stop >= 4:
            self.phase_d(["d0", "dgu", "dmm", "d"][min(stop - 4, 3)])
        self.finish()
        self.flush()
        es.close()
        return self.nc, self.dbg

    def phase_p0(self):
        cC, cV = self.cC, self.cV

        def p0_sync(q):
            for dst, src in [(self.ident_s, self.identf),
                             (self.ones_c, self.ones_col), (self.ones_r, self.ones_row),
                             (self.msign_s, self.m_sign),
                             (self.g_s, self.g_cols), (self.lnb_s, self.lnb_cols),
                             (self.bo_s, self.bo_cols)]:
                q.dma_start(out=dst[:], in_=src[:]).then_inc(cC.h, 16)
                cC.n += 16
        self.on("sync", p0_sync)
        tok_consts = cC.n

        def p0_v(v):
            v.wait_ge(cC.h, tok_consts)
            v.tensor_copy(out=self.lnb_bf[:], in_=self.lnb_s[:])
            v.memset(self.kn2[:], 0.0)
            v.memset(self.qn2[:], 0.0)
            v.memset(self.ss[:], 0.0).then_inc(cV.h, 1)
            cV.n += 1
        self.on("vector", p0_v)
        self.tok_p0v = cV.n

    def pe_transpose(self, t, src_ap):
        cP, cV = self.cP, self.cV
        k = self.tp_ctr
        if k >= 4:
            t.wait_ge(cV.h, self.tp_ev[k - 4])
        t.transpose(self.tp_slot[k % 4], src_ap, self.ident_s[:]).then_inc(cP.h, 1)
        cP.n += 1
        self.tp_tok.append(cP.n)
        self.tp_ctr += 1
        return k

    def tp_evac_done(self, k, tok):
        assert len(self.tp_ev) == k
        self.tp_ev.append(tok)

    def emit_ln(self, tile_ap, c0, dual, tok_ready_ch, tok_ready):
        cL, cA, cV = self.cL, self.cA, self.cV
        st = lambda i: self.stats[:, c0 + i:c0 + i + 1]

        pred_prod = cV.n + 5

        jb = (c0 // 16) * 2048
        box = {}

        def sq(s):
            s.wait_ge(tok_ready_ch.h, tok_ready)
            s.activation(out=self.junkb[:, jb:jb + 1024], in_=tile_ap[:, 0:1024],
                         func=AF.Square, accum_out=st(0)).then_inc(cA.h, 1)
            cA.n += 1
            s.activation(out=self.junkb[:, jb + 1024:jb + 2048], in_=tile_ap[:, 1024:2048],
                         func=AF.Square, accum_out=st(1)).then_inc(cA.h, 1)
            cA.n += 1
            s.wait_ge(cA.h, cA.n)
            s.activation(out=st(3), in_=st(0), func=AF.Square).then_inc(cA.h, 1)
            cA.n += 1
            box["tok_sq"] = cA.n
            s.wait_ge(cV.h, pred_prod)
            s.activation(out=st(12), in_=st(8), func=AF.Ln)
            s.nop(cycle_cnt=1, nofuse=True).then_inc(cA.h, 1)
            cA.n += 1
            s.wait_ge(cA.h, cA.n)
            s.activation(out=st(13), in_=st(12), func=AF.Exp, scale=-0.5)
            s.nop(cycle_cnt=1, nofuse=True).then_inc(cA.h, 1)
            cA.n += 1
            s.wait_ge(cA.h, cA.n)
            s.activation(out=st(2), in_=st(13), func=AF.Exp).then_inc(cA.h, 1)
            cA.n += 1
        self.on("scalar", sq)
        tok_G = cA.n
        tok_sq = box["tok_sq"]

        def v1(v):
            v.wait_ge(tok_ready_ch.h, tok_ready)
            v.reduce_sum(out=st(4), in_=tile_ap, axis=AX.X)
            v.engine_nop().then_inc(cV.h, 1)
            cV.n += 1
            v.wait_ge(cV.h, cV.n)
            v.wait_ge(cA.h, tok_sq)
            v.tensor_tensor(out=st(5), in0=st(0), in1=st(1), op=ALU.add)
            v.tensor_scalar(out=st(6), in0=st(4), scalar1=1.0 / D, scalar2=None, op0=ALU.mult)
            v.engine_nop().then_inc(cV.h, 1)
            cV.n += 1
            v.wait_ge(cV.h, cV.n)
            v.tensor_tensor(out=st(7), in0=st(6), in1=st(6), op=ALU.mult)
            v.tensor_scalar(out=st(5), in0=st(5), scalar1=1.0 / D, scalar2=None, op0=ALU.mult)
            v.engine_nop().then_inc(cV.h, 1)
            cV.n += 1
            v.wait_ge(cV.h, cV.n)
            v.tensor_tensor(out=st(5), in0=st(5), in1=st(7), op=ALU.subtract)
            v.engine_nop().then_inc(cV.h, 1)
            cV.n += 1
            v.wait_ge(cV.h, cV.n)
            if dual:
                v.tensor_scalar(out=st(8), in0=st(5), scalar1=1.0 + EPS,
                                scalar2=EPS + EPS * EPS, op0=ALU.mult,
                                op1=ALU.add).then_inc(cV.h, 1)
            else:
                v.tensor_scalar(out=st(8), in0=st(5), scalar1=EPS, scalar2=None,
                                op0=ALU.add).then_inc(cV.h, 1)
            cV.n += 1
        self.on("vector", v1)
        assert cV.n == pred_prod
        return tok_G

    def phase_a(self):
        cL, cG, cP, cV, cA = self.cL, self.cG, self.cP, self.cV, self.cA
        nc = self.nc
        self.esAB = contextlib.ExitStack()
        self.xgT = self.sbuf(self.esAB, "xgT", [128, NHC, S], BF16)
        self.bprow_bf = self.sbuf(self.esAB, "bprow_bf", [1, 3, D], BF16)
        self.wqt = [self.sbuf(self.esAB, f"wquart{i}", [128, NHC, 512], BF16)
                    for i in range(2)]
        self.stage = self.sbuf(self.esAB, "stage", [128, 2, 512], BF16)

        tok_x = [0] * NT
        tok_xh = [0] * NT
        tok_tp_last = [0] * NT
        self.tok_xgT_tile = [0] * NT

        for tt in range(NT):
            pi = tt % 2
            c0 = 16 * pi

            def a_sync(q, tt=tt, pi=pi):
                if tt >= 2:
                    q.wait_ge(cV.h, tok_xh[tt - 2])
                q.wait_ge(cL.h, cL.n)
                q.dma_start(out=self.xraw[pi],
                            in_=self.x_in[tt * 128:(tt + 1) * 128, :]).then_inc(cL.h, 16)
                cL.n += 16
                tok_x[tt] = cL.n
            self.on("sync", a_sync)

            tok_G = self.emit_ln(self.xraw[pi], c0, True, cL, tok_x[tt])

            def a_v2(v, tt=tt, pi=pi, c0=c0, tok_G=tok_G):
                v.wait_ge(cA.h, tok_G)
                if tt == 0:
                    v.wait_ge(cV.h, self.tok_p0v)
                if tt >= 2:
                    v.wait_ge(cP.h, tok_tp_last[tt - 2])
                v.tensor_scalar(out=self.xh32[pi], in0=self.xraw[pi],
                                scalar1=self.stats[:, c0 + 6:c0 + 7],
                                scalar2=self.stats[:, c0 + 13:c0 + 14],
                                op0=ALU.subtract, op1=ALU.mult).then_inc(cV.h, 1)
                cV.n += 1
                tok_xh[tt] = cV.n
            self.on("vector", a_v2)

            for hc in range(NHC):
                kbox = []

                def a_pe(t, tt=tt, pi=pi, hc=hc, kbox=kbox):
                    if hc == 0:
                        t.wait_ge(cV.h, tok_xh[tt])
                    kbox.append(self.pe_transpose(
                        t, self.xh32[pi][:, hc * 128:(hc + 1) * 128]))
                    tok_tp_last[tt] = cP.n
                self.on("tensor", a_pe)

                def a_v3(v, tt=tt, hc=hc, kbox=kbox):
                    k = kbox[0]
                    v.wait_ge(cP.h, self.tp_tok[k])
                    v.tensor_scalar(out=self.xgT[:, hc, tt * 128:(tt + 1) * 128],
                                    in0=self.tp_slot[k % 4],
                                    scalar1=self.g_s[:, hc:hc + 1],
                                    scalar2=self.lnb_s[:, hc:hc + 1],
                                    op0=ALU.mult, op1=ALU.add).then_inc(cV.h, 1)
                    cV.n += 1
                    self.tp_evac_done(k, cV.n)
                self.on("vector", a_v3)
            self.tok_xgT_tile[tt] = cV.n

        self.tok_xgT_done = cV.n

        if self.debug:
            def a_dbg(g):
                g.wait_ge(cV.h, self.tok_xgT_done)
                g.dma_start(out=self.dbg["d_xgT"][:], in_=self.xgT[:]).then_inc(cG.h, 16)
                g.dma_start(out=self.dbg["d_stats"][:], in_=self.stats[:]).then_inc(cG.h, 16)
                g.dma_start(out=self.dbg["d_xh"][:],
                            in_=self.vreg32[:, 4096:8192]).then_inc(cG.h, 16)
                cG.n += 48
            self.on("gpsimd", a_dbg)

    def phase_b(self, sub=4):
        cL, cG, cP, cV, cA = self.cL, self.cG, self.cP, self.cV, self.cA
        pa, pb, pcs, pd = self.pa, self.pb, self.pcs, self.pd
        b_slots = [pa[:, 0:512], pa[:, 512:1024], pb[:, 0:512], pb[:, 512:1024]]
        mats = [("k", self.wk, NT, 1), ("v", self.wv, NT, 2), ("q", self.wq, OT, 0)]
        nmats = min(sub + 1, 3)
        mats = mats[:nmats]
        ALLQ = [(mi, qo) for mi in range(nmats) for qo in range(4)]

        tok_wq = {}
        tok_bp = {}
        self.tok_mm = tok_mm = {}
        tok_evac = {}
        self.tok_sp = tok_sp = {}
        tok_n2 = {}
        slot_hist = []
        stage_hist = []

        def prefetch_w(g, idx):
            mi, qo = ALLQ[idx]
            wt = mats[mi][1]
            if idx >= 2:
                pm, pq = ALLQ[idx - 2]
                g.wait_ge(cP.h, tok_mm[(pm, pq, mats[pm][2] - 1)])
            g.dma_start(out=self.wqt[idx % 2][:],
                        in_=wt[:, qo * 512:(qo + 1) * 512].rearrange("(c p) o -> p c o", p=128)
                        ).then_inc(cG.h, 16)
            cG.n += 16
            tok_wq[(mi, qo)] = cG.n

        def brow_cast(g):
            for mi2 in range(nmats):
                bi2 = mats[mi2][3]
                g.dma_start(out=self.bprow_bf[0:1, mi2, :],
                            in_=self.b_rows[bi2:bi2 + 1, :]).then_inc(cG.h, 16)
                cG.n += 16
        self.on("gpsimd", brow_cast)
        tok_brow = cG.n

        self.on("gpsimd", lambda g: prefetch_w(g, 0))
        self.on("gpsimd", lambda g: prefetch_w(g, 1))

        slot_ctr = 0
        for mi in range(nmats):
            name, wt, nts, bi = mats[mi]
            for qo in range(4):
                idx = mi * 4 + qo
                buf = self.wqt[idx % 2]
                if idx + 1 < 4 * nmats:
                    self.on("gpsimd", lambda g, i=idx + 1: prefetch_w(g, i))

                for ts in range(nts):
                    key = (mi, qo, ts)
                    slot = slot_ctr % 4
                    slot_ctr += 1
                    prev_slot = None
                    for k2, s2 in reversed(slot_hist):
                        if s2 == slot:
                            prev_slot = k2
                            break
                    slot_hist.append((key, slot))
                    prev_stage = None
                    if mi != 1:
                        for k2 in reversed(stage_hist):
                            if k2[2] % 2 == ts % 2:
                                prev_stage = k2
                                break
                        stage_hist.append(key)

                    def b_pe(t, key=key, slot=slot, prev_slot=prev_slot, buf=buf):
                        mi, qo, ts = key
                        t.wait_ge(cG.h, tok_wq[(mi, qo)])
                        t.wait_ge(cG.h, tok_brow)
                        t.wait_ge(cV.h, self.tok_xgT_tile[ts])
                        if prev_slot is not None:
                            t.wait_ge(cA.h, tok_evac[prev_slot])
                        ps = b_slots[slot]
                        for hc in range(NHC):
                            t.matmul(ps, self.xgT[:, hc, ts * 128:(ts + 1) * 128],
                                     buf[:, hc, :], start=(hc == 0), stop=False)
                        t.matmul(ps, self.ones_r[:],
                                 self.bprow_bf[0:1, mi, qo * 512:(qo + 1) * 512],
                                 start=False, stop=True).then_inc(cP.h, 1)
                        cP.n += 1
                        tok_mm[key] = cP.n
                    self.on("tensor", b_pe)

                    def b_evac(s, key=key, slot=slot, prev_stage=prev_stage):
                        mi, qo, ts = key
                        s.wait_ge(cP.h, tok_mm[key])
                        ps = b_slots[slot]
                        if mi == 1:
                            s.copy(out=self.V_t[:, ts, qo * 512:(qo + 1) * 512],
                                   in_=ps).then_inc(cA.h, 1)
                        else:
                            if prev_stage is not None:
                                s.wait_ge(cL.h, tok_sp[prev_stage])
                                s.wait_ge(cV.h, tok_n2[prev_stage])
                            s.copy(out=self.stage[:, ts % 2, :], in_=ps).then_inc(cA.h, 1)
                        cA.n += 1
                        tok_evac[key] = cA.n
                    self.on("scalar", b_evac)

                    if mi != 1:
                        def b_v(v, key=key):
                            mi, qo, ts = key
                            dst = self.kn2 if mi == 0 else self.qn2
                            v.wait_ge(cA.h, tok_evac[key])
                            v.tensor_tensor(out=self.junkc[:, 0:512],
                                            in0=self.stage[:, ts % 2, :],
                                            in1=self.stage[:, ts % 2, :], op=ALU.mult)
                            v.engine_nop().then_inc(cV.h, 1)
                            cV.n += 1
                            v.wait_ge(cV.h, cV.n)
                            v.reduce_sum(
                                out=self.bstat[:].rearrange("p (c o) -> p c o", o=1),
                                in_=self.junkc[:, 0:512].rearrange("p (c f) -> p c f", f=128),
                                axis=AX.X)
                            v.engine_nop().then_inc(cV.h, 1)
                            cV.n += 1
                            v.wait_ge(cV.h, cV.n)
                            v.tensor_tensor(out=dst[:, qo * 4:qo * 4 + 4],
                                            in0=dst[:, qo * 4:qo * 4 + 4],
                                            in1=self.bstat[:],
                                            op=ALU.max).then_inc(cV.h, 1)
                            cV.n += 1
                            tok_n2[key] = cV.n
                        self.on("vector", b_v)

                        def b_spill(q, key=key):
                            mi, qo, ts = key
                            q.wait_ge(cA.h, tok_evac[key])
                            scr = self.kscr if mi == 0 else self.qscr
                            q.wait_ge(cL.h, cL.n)
                            q.dma_start(out=scr[qo, ts * 128:(ts + 1) * 128, :],
                                        in_=self.stage[:, ts % 2, :]).then_inc(cL.h, 16)
                            cL.n += 16
                            tok_sp[key] = cL.n
                        self.on("sync", b_spill)

        tok_lastn2 = cV.n
        self.tok_b_pe_done = cP.n
        self.tok_sp_last = max(tok_sp.values())
        if sub < 4:
            if self.debug:
                def b_dbg0(g):
                    g.wait_ge(cP.h, self.tok_b_pe_done)
                    g.wait_ge(cL.h, self.tok_sp_last)
                    if nmats >= 2:
                        g.wait_ge(cA.h, tok_evac[(1, 3, NT - 1)])
                        g.dma_start(out=self.dbg["d_V"][:], in_=self.V_t[:]).then_inc(cG.h, 16)
                        cG.n += 16
                    g.dma_start(out=self.dbg["d_K"][:], in_=self.kscr[:]).then_inc(cG.h, 16)
                    g.dma_start(out=self.dbg["d_bp"][:],
                                in_=self.bprow_bf[0:1, 0, :]).then_inc(cG.h, 16)
                    cG.n += 32
                    if nmats >= 3:
                        g.dma_start(out=self.dbg["d_Q"][:], in_=self.qscr[:]).then_inc(cG.h, 16)
                        cG.n += 16
                self.on("gpsimd", b_dbg0)
            self.flush()
            self.esAB.close()
            return
        tok_evac_last = tok_evac[(2, 3, OT - 1)]

        def bnd_settle(v):
            v.engine_nop().then_inc(cV.h, 1)
            cV.n += 1
            v.wait_ge(cV.h, cV.n)
            v.engine_nop().then_inc(cV.h, 1)
            cV.n += 1
        self.on("vector", bnd_settle)
        tok_lastn2 = cV.n

        def bnd_pe1(t):
            t.wait_ge(cV.h, tok_lastn2)
            t.wait_ge(cA.h, tok_evac_last)
            t.transpose(pcs[0:16, 0:128], self.kn2[:], self.ident_s[:])
            t.transpose(pd[0:16, 0:128], self.qn2[:], self.ident_s[:]).then_inc(cP.h, 1)
            cP.n += 1
        self.on("tensor", bnd_pe1)
        tok_btp = cP.n

        def bnd_v1(v):
            v.wait_ge(cP.h, tok_btp)
            v.reduce_max(out=self.nrow[0:16, 0:1], in_=pcs[0:16, 0:128], axis=AX.X)
            v.reduce_max(out=self.nrow[0:16, 1:2], in_=pd[0:16, 0:128], axis=AX.X)
            v.engine_nop().then_inc(cV.h, 1)
            cV.n += 1
            v.wait_ge(cV.h, cV.n)
            v.tensor_tensor(out=self.nrow[0:16, 2:3], in0=self.nrow[0:16, 0:1],
                            in1=self.nrow[0:16, 1:2], op=ALU.mult)
            v.engine_nop().then_inc(cV.h, 1)
            cV.n += 1
            v.wait_ge(cV.h, cV.n)
            v.engine_nop().then_inc(cV.h, 1)
            cV.n += 1
        self.on("vector", bnd_v1)
        tok_bv1 = cV.n

        def bnd_a1(s):
            s.wait_ge(cV.h, tok_bv1)
            s.activation(out=self.nrow[0:16, 4:5], in_=self.nrow[0:16, 2:3], func=AF.Ln)
            s.nop(cycle_cnt=1, nofuse=True).then_inc(cA.h, 1)
            cA.n += 1
            s.wait_ge(cA.h, cA.n)
            s.activation(out=self.nrow[0:16, 5:6], in_=self.nrow[0:16, 4:5], func=AF.Exp,
                         scale=0.5)
            s.nop(cycle_cnt=1, nofuse=True).then_inc(cA.h, 1)
            cA.n += 1
            s.wait_ge(cA.h, cA.n)
            s.activation(out=self.nrow[0:16, 12:13], in_=self.nrow[0:16, 5:6],
                         func=AF.Ln).then_inc(cA.h, 1)
            cA.n += 1
        self.on("scalar", bnd_a1)
        tok_ba1 = cA.n

        def bnd_v2(v):
            v.wait_ge(cA.h, tok_ba1)
            v.tensor_scalar(out=self.nrow[0:16, 8:9], in0=self.nrow[0:16, 5:6],
                            scalar1=-1.02 * SM_SCALE, scalar2=None, op0=ALU.mult)
            v.engine_nop().then_inc(cV.h, 1)
            cV.n += 1
            v.wait_ge(cV.h, cV.n)
            v.engine_nop().then_inc(cV.h, 1)
            cV.n += 1
        self.on("vector", bnd_v2)
        tok_bv2 = cV.n

        def bnd_pe2(t):
            t.wait_ge(cV.h, tok_bv2)
            t.transpose(pcs[0:1, 128:144], self.nrow[0:16, 8:9],
                        self.ident_s[0:16, 0:16]).then_inc(cP.h, 1)
            cP.n += 1
        self.on("tensor", bnd_pe2)
        tok_bt2 = cP.n

        def bnd_v3(v):
            v.wait_ge(cP.h, tok_bt2)
            v.tensor_copy(out=self.rowscr[0:1, 0:16], in_=pcs[0:1, 128:144])
            v.engine_nop().then_inc(cV.h, 1)
            cV.n += 1
            v.wait_ge(cV.h, cV.n)
            v.tensor_copy(out=self.rowbf[0:1, 0:16], in_=self.rowscr[0:1, 0:16])
            v.engine_nop().then_inc(cV.h, 1)
            cV.n += 1
            v.wait_ge(cV.h, cV.n)
            v.engine_nop().then_inc(cV.h, 1)
            cV.n += 1
        self.on("vector", bnd_v3)
        tok_bv3 = cV.n

        def bnd_pe3(t):
            t.wait_ge(cV.h, tok_bv3)
            t.matmul(pcs[:, 128:144], self.ones_r[:], self.rowbf[0:1, 0:16],
                     start=True, stop=True).then_inc(cP.h, 1)
            cP.n += 1
        self.on("tensor", bnd_pe3)
        tok_brep = cP.n

        def bnd_a2(s):
            s.wait_ge(cP.h, tok_brep)
            s.copy(out=self.negb_rep[:], in_=pcs[:, 128:144]).then_inc(cA.h, 1)
            cA.n += 1
        self.on("scalar", bnd_a2)
        self.tok_negb = cA.n

        if self.debug:
            def b_dbg(g):
                g.wait_ge(cA.h, self.tok_negb)
                g.dma_start(out=self.dbg["d_V"][:], in_=self.V_t[:]).then_inc(cG.h, 16)
                g.wait_ge(cL.h, tok_sp[(0, 3, NT - 1)])
                g.wait_ge(cL.h, tok_sp[(2, 3, OT - 1)])
                g.dma_start(out=self.dbg["d_K"][:], in_=self.kscr[:]).then_inc(cG.h, 16)
                g.dma_start(out=self.dbg["d_Q"][:], in_=self.qscr[:]).then_inc(cG.h, 16)
                cG.n += 48
            self.on("gpsimd", b_dbg)

            def b_dbg2(q):
                q.wait_ge(cA.h, self.tok_negb)
                q.wait_ge(cL.h, cL.n)
                q.dma_start(out=self.dbg["d_negb"][:],
                            in_=self.rowscr[0:1, 0:16]).then_inc(cL.h, 16)
                cL.n += 16
            self.on("sync", b_dbg2)

        self.flush()
        self.esAB.close()

    def phase_c(self):
        cL, cG, cP, cV, cA = self.cL, self.cG, self.cP, self.cV, self.cA
        pa, pb, pcs, pd = self.pa, self.pb, self.pcs, self.pd
        esC = contextlib.ExitStack()
        woq = [self.sbuf(esC, f"woq{i}", [128, NHC, 512], BF16) for i in range(2)]
        self.tri_s = self.sbuf(esC, "tri_s", [128, 16, 128], BF16)
        borow32 = self.sbuf(esC, "borow32", [1, D], F32)
        identb_s = self.sbuf(esC, "identb_s", [128, 128], BF16)
        borow_bf = self.sbuf(esC, "borow_bf", [1, D], BF16)
        ones512 = self.sbuf(esC, "ones512", [1, 512], BF16)

        def c_triload(q):
            q.wait_ge(cP.h, self.tok_b_pe_done)
            q.wait_ge(cL.h, self.tok_sp_last)
            q.wait_ge(cL.h, cL.n)
            q.dma_start(out=self.tri_s[:], in_=self.bmask[:]).then_inc(cL.h, 16)
            q.wait_ge(cL.h, cL.n)
            q.dma_start(out=borow32[:], in_=self.b_rows[3:4, :]).then_inc(cL.h, 16)
            q.wait_ge(cL.h, cL.n)
            q.dma_start(out=identb_s[:], in_=self.identb[:]).then_inc(cL.h, 16)
            cL.n += 48
        self.on("sync", c_triload)
        tok_tri = cL.n

        def c_boprep(v):
            v.wait_ge(cL.h, tok_tri)
            v.memset(ones512[:], 1.0)
            v.tensor_copy(out=borow_bf[:], in_=borow32[:])
            v.engine_nop().then_inc(cV.h, 1)
            cV.n += 1
            v.wait_ge(cV.h, cV.n)
            v.engine_nop().then_inc(cV.h, 1)
            cV.n += 1
        self.on("vector", c_boprep)
        tok_bo = cV.n
        PT = self.sbuf(esC, "PT", [128, 16, 512], BF16)
        AVn = self.sbuf(esC, "AVn", [128, H, 512], BF16)
        KT = self.sbuf(esC, "KT", [128, 2, 2048], BF16)
        QT = self.sbuf(esC, "QT", [128, 2, 512], BF16)
        X2h = self.sbuf(esC, "X2h", [128, 4, 2048], F32)
        ostage = self.sbuf(esC, "ostage", [128, 2, 512], F32)
        reprec = self.sbuf(esC, "reprec", [128, 512], F32)

        tok_sp = self.tok_sp
        tok_kt = [0, 0]
        tok_qt = [0, 0]
        tok_ktfree = [0, 0]
        tok_x2l = {}
        tok_x2sp = {}
        tok_wol = {}
        tok_wo_used = {}
        tok_oadd = {}
        self.tok_x2sp = tok_x2sp

        def kt_valid(kt, qh):
            imin = kt if kt < 8 else kt - 8
            loc = 128 * imin - 512 * qh
            if loc >= 512:
                return None
            return max(loc, 0), (0 <= loc)

        for qh in range(2):
            def c_xload(q, qh=qh):
                if qh == 0:
                    q.wait_ge(cA.h, self.tok_negb)
                else:
                    q.wait_ge(cL.h, tok_x2sp[(0, 3)])
                for ii in range(4):
                    q.dma_start(out=X2h[:, ii, :],
                                in_=self.x_in[(4 * qh + ii) * 128:(4 * qh + ii + 1) * 128, :]
                                ).then_inc(cL.h, 16)
                    cL.n += 16
                    tok_x2l[(qh, ii)] = cL.n
            self.on("sync", c_xload)

            tok_avn = {}
            slot_exp = {0: None, 1: None, 2: None, 3: None}

            def emit_c_load(qh, h):
                def c_load(q, h=h, qh=qh):
                    q.wait_ge(cP.h, tok_ktfree[h % 2])
                    q.wait_ge(cL.h, tok_sp[(0, h // 4, NT - 1)])
                    q.dma_start_transpose(
                        out=KT[:, h % 2, :],
                        in_=self.kscr[h // 4, :, (h % 4) * 128:(h % 4 + 1) * 128]
                        ).then_inc(cL.h, 16)
                    cL.n += 16
                    tok_kt[h % 2] = cL.n
                    q.wait_ge(cL.h, tok_sp[(2, h // 4, OT - 1)])
                    q.dma_start_transpose(
                        out=QT[:, h % 2, :],
                        in_=self.qscr[h // 4, 512 * qh:512 * (qh + 1),
                                      (h % 4) * 128:(h % 4 + 1) * 128]).then_inc(cL.h, 16)
                    cL.n += 16
                    tok_qt[h % 2] = cL.n
                self.on("sync", c_load)

            for h in range(H):
                if not (qh == 1 and h < 2):
                    emit_c_load(qh, h)

                kts = [kt for kt in range(16) if kt_valid(kt, qh) is not None]
                n_kt = len(kts)
                pe_base = cP.n
                pred_sc = {}
                pred_rav = {}
                NS = 4
                ev_list = [("S", i) for i in range(min(NS, n_kt))]
                for i in range(n_kt):
                    ev_list.append(("R", i))
                    if i + NS < n_kt:
                        ev_list.append(("S", i + NS))
                for off, (kind, i) in enumerate(ev_list):
                    if kind == "S":
                        pred_sc[i] = pe_base + off + 1
                    else:
                        pred_rav[i] = pe_base + off + 1
                exp_base = cA.n
                tok_exp = {kt: exp_base + i + 1 for i, kt in enumerate(kts)}

                score_slots = [pa[:, 0:512], pa[:, 512:1024], pb[:, 0:512], pb[:, 512:1024]]

                def c_exp_all(s, h=h, qh=qh, kts=kts, pred_sc=dict(pred_sc)):
                    for i, kt in enumerate(kts):
                        loc, _ = kt_valid(kt, qh)
                        sslot = score_slots[i % 4]
                        s.wait_ge(cP.h, pred_sc[i])
                        s.activation(out=PT[:, kt, loc:512], in_=sslot[:, loc:512],
                                     func=AF.Exp, bias=self.negb_rep[:, h:h + 1],
                                     scale=SM_SCALE).then_inc(cA.h, 1)
                        cA.n += 1
                self.on("scalar", c_exp_all)
                assert cA.n == exp_base + n_kt

                def c_pe_all(t, h=h, qh=qh, kts=kts, ev_list=list(ev_list)):
                    for kind, i in ev_list:
                        kt = kts[i]
                        loc, msk = kt_valid(kt, qh)
                        if kind == "S":
                            sslot = score_slots[i % 4]
                            t.wait_ge(cL.h, tok_kt[h % 2])
                            t.wait_ge(cL.h, tok_qt[h % 2])
                            if i >= 4:
                                t.wait_ge(cA.h, tok_exp[kts[i - 4]])
                            elif slot_exp[i % 4] is not None:
                                t.wait_ge(cA.h, slot_exp[i % 4])
                            if (qh, h) == (1, 0) and i < 4 and self._last_oev:
                                t.wait_ge(cA.h, self._last_oev)
                            inst = t.matmul(sslot[:, loc:512],
                                            KT[:, h % 2, kt * 128:(kt + 1) * 128],
                                            QT[:, h % 2, loc:512], start=True,
                                            stop=not msk)
                            if msk:
                                if (qh, h) == (0, 0):
                                    t.wait_ge(cL.h, tok_tri)
                                inst = t.matmul(sslot[:, loc:loc + 128],
                                                self.tri_s[:, kt, :], identb_s[:],
                                                start=False, stop=True)
                            inst.then_inc(cP.h, 1)
                            cP.n += 1
                            assert cP.n == pred_sc[i]
                        else:
                            first = i == 0
                            last = i == n_kt - 1
                            t.wait_ge(cA.h, tok_exp[kt])
                            av = pcs[:, 0:512] if h % 2 == 0 else pcs[:, 512:1024]
                            if first:
                                if h >= 2:
                                    t.wait_ge(cV.h, tok_avn[h - 2])
                                elif len(self.tp_ev):
                                    t.wait_ge(cV.h, self.tp_ev[-1])
                            t.matmul(pd[0:1, loc:512], self.ones_c[:], PT[:, kt, loc:512],
                                     start=first, stop=last)
                            t.matmul(av[:, loc:512],
                                     self.V_t[:, kt, h * 128:(h + 1) * 128],
                                     PT[:, kt, loc:512], start=first,
                                     stop=last).then_inc(cP.h, 1)
                            cP.n += 1
                            assert cP.n == pred_rav[i]
                self.on("tensor", c_pe_all)
                for i in range(n_kt):
                    slot_exp[i % 4] = tok_exp[kts[i]]

                tok_rav = cP.n
                tok_ktfree[h % 2] = cP.n

                def c_recip(s, tok_rav=tok_rav):
                    s.wait_ge(cP.h, tok_rav)
                    s.activation(out=self.rowscr[0:1, :], in_=pd[0:1, 0:512],
                                 func=AF.Ln)
                    s.nop(cycle_cnt=1, nofuse=True).then_inc(cA.h, 1)
                    cA.n += 1
                    s.wait_ge(cA.h, cA.n)
                    s.activation(out=self.rowbf[0:1, :], in_=self.rowscr[0:1, :],
                                 func=AF.Exp, scale=-1.0).then_inc(cA.h, 1)
                    cA.n += 1
                self.on("scalar", c_recip)
                tok_rc = cA.n

                def c_rep(t, tok_rc=tok_rc):
                    t.wait_ge(cA.h, tok_rc)
                    t.matmul(pd[:, 512:1024], self.ones_r[:], self.rowbf[0:1, :],
                             start=True, stop=True).then_inc(cP.h, 1)
                    cP.n += 1
                self.on("tensor", c_rep)
                tok_rep = cP.n

                def c_repc(s, tok_rep=tok_rep):
                    s.wait_ge(cP.h, tok_rep)
                    s.copy(out=reprec[:], in_=pd[:, 512:1024]).then_inc(cA.h, 1)
                    cA.n += 1
                self.on("scalar", c_repc)
                tok_repc = cA.n

                def c_avn(v, h=h, tok_repc=tok_repc):
                    v.wait_ge(cA.h, tok_repc)
                    av = pcs[:, 0:512] if h % 2 == 0 else pcs[:, 512:1024]
                    v.scalar_tensor_tensor(out=AVn[:, h, :], in0=av, scalar=1.0,
                                           in1=reprec[:], op0=ALU.mult,
                                           op1=ALU.mult).then_inc(cV.h, 1)
                    cV.n += 1
                    tok_avn[h] = cV.n
                self.on("vector", c_avn)
                tok_av_free = tok_avn[h]

            if self.debug and qh == 0:
                def c_dbg(g):
                    g.wait_ge(cV.h, tok_avn[H - 1])
                    g.dma_start(out=self.dbg["d_AVn"][:], in_=AVn[:]).then_inc(cG.h, 16)
                    cG.n += 16
                self.on("gpsimd", c_dbg)

            if qh == 0:
                emit_c_load(1, 0)
                emit_c_load(1, 1)

            tok_opj = {}
            tok_oev = {}
            for oq in range(4):
                widx = qh * 4 + oq

                def c_wol(g, qh=qh, oq=oq, widx=widx):
                    if widx >= 2:
                        g.wait_ge(cP.h, tok_wo_used.get(widx - 2, 0))
                    else:
                        g.wait_ge(cP.h, self.tok_b_pe_done)
                        g.wait_ge(cL.h, self.tok_sp_last)
                    g.dma_start(out=woq[oq % 2][:],
                                in_=self.wo[:, oq * 512:(oq + 1) * 512].rearrange(
                                    "(c p) o -> p c o", p=128)).then_inc(cG.h, 16)
                    cG.n += 16
                    tok_wol[widx] = cG.n
                self.on("gpsimd", c_wol)

                for oc4 in range(4):
                    oc = oq * 4 + oc4
                    oslot = pb[:, 0:512] if oc % 2 == 0 else pb[:, 512:1024]

                    def c_opj(t, qh=qh, oq=oq, oc4=oc4, oc=oc, oslot=oslot, widx=widx):
                        t.wait_ge(cG.h, tok_wol[widx])
                        t.wait_ge(cV.h, tok_avn[H - 1])
                        if oc >= 2:
                            t.wait_ge(cA.h, tok_oev[oc - 2])
                        t.wait_ge(cV.h, tok_bo)
                        for hc in range(H):
                            t.matmul(oslot,
                                     woq[oq % 2][:, hc, oc4 * 128:(oc4 + 1) * 128],
                                     AVn[:, hc, :], start=(hc == 0), stop=False)
                        inst = t.matmul(oslot, borow_bf[0:1, oc * 128:(oc + 1) * 128],
                                        ones512[:], start=False, stop=True)
                        inst.then_inc(cP.h, 1)
                        cP.n += 1
                        tok_opj[oc] = cP.n
                        tok_wo_used[widx] = cP.n
                    self.on("tensor", c_opj)

                    def c_oev(s, qh=qh, oc=oc, oslot=oslot):
                        s.wait_ge(cP.h, tok_opj[oc])
                        if oc >= 2:
                            s.wait_ge(cV.h, tok_oadd[(qh, oc - 2)])
                        s.copy(out=ostage[:, oc % 2, :], in_=oslot).then_inc(cA.h, 1)
                        cA.n += 1
                        tok_oev[oc] = cA.n
                    self.on("scalar", c_oev)

                    slots = []

                    def c_otp(t, qh=qh, oc=oc, slots=slots):
                        t.wait_ge(cA.h, tok_oev[oc])
                        for ii in range(4):
                            k = self.pe_transpose(
                                t, ostage[:, oc % 2, ii * 128:(ii + 1) * 128])
                            slots.append(k)
                    self.on("tensor", c_otp)

                    def c_oadd(v, qh=qh, oc=oc, slots=slots):
                        for ii in range(4):
                            k = slots[ii]
                            v.wait_ge(cP.h, self.tp_tok[k])
                            v.wait_ge(cL.h, tok_x2l[(qh, ii)])
                            v.scalar_tensor_tensor(
                                out=X2h[:, ii, oc * 128:(oc + 1) * 128],
                                in0=self.tp_slot[k % 4], scalar=1.0,
                                in1=X2h[:, ii, oc * 128:(oc + 1) * 128],
                                op0=ALU.mult, op1=ALU.add).then_inc(cV.h, 1)
                            cV.n += 1
                            self.tp_evac_done(k, cV.n)
                        tok_oadd[(qh, oc)] = cV.n
                    self.on("vector", c_oadd)

            self._last_oev = cA.n

            def c_x2spill(q, qh=qh):
                q.wait_ge(cV.h, tok_oadd[(qh, 15)])
                for ii in range(4):
                    q.dma_start(out=self.x2scr[4 * qh + ii],
                                in_=X2h[:, ii, :]).then_inc(cL.h, 16)
                    cL.n += 16
                    tok_x2sp[(qh, ii)] = cL.n
            self.on("sync", c_x2spill)

        if self.debug:
            def c_dbg2(q):
                for i in range(OT):
                    q.wait_ge(cL.h, tok_x2sp[(i // 4, i % 4)])
                    q.dma_start(out=self.dbg["d_x2"][i],
                                in_=self.x2scr[i]).then_inc(cL.h, 16)
                    cL.n += 16
            self.on("sync", c_dbg2)

        self.flush()
        esC.close()

    def phase_d(self, sub="d"):
        cL, cG, cP, cV, cA = self.cL, self.cG, self.cP, self.cV, self.cA
        pa, pb, pcs, pd = self.pa, self.pb, self.pcs, self.pd
        nc = self.nc
        tok_x2sp = self.tok_x2sp

        esD = contextlib.ExitStack()
        act = self.sbuf(esD, "act", [128, 44, 1024], BF16)
        esD0 = contextlib.ExitStack()
        xfT = self.sbuf(esD0, "xfT", [128, NHC, 1024], BF16)

        vb = self.vreg
        x2t = [self.vreg32[:, 0:2048], self.vreg32[:, 2048:4096]]
        xf32 = [self.vreg32[:, 4096:6144], self.vreg32[:, 6144:8192]]
        dec_b = [vb[:, 0:8192].rearrange("p (s o) -> p s o", s=8),
                 vb[:, 8192:16384].rearrange("p (s o) -> p s o", s=8)]
        pk = vb[:, 16384:20480].bitcast(I32).rearrange(
            "p (b w o) -> p b w o", b=2, w=2)
        silu4 = vb[:, 28672:32768].bitcast(F32).rearrange("p (s o) -> p s o", s=4)
        srep = vb[:, 25600:27648].rearrange("p (b w o) -> p b w o", b=2, w=2)
        pint = vb[:, 27648:28160].bitcast(I32)
        p1s = vb[:, 28160:28672].bitcast(I32)

        tok_x2 = [0] * OT
        tok_xf = [0] * OT
        tok_tp_last = [0] * OT
        for i in range(OT):
            pi = i % 2
            c0 = 16 * pi

            def d0_sync(q, i=i, pi=pi):
                q.wait_ge(cL.h, tok_x2sp[(i // 4, i % 4)])
                if i >= 2:
                    q.wait_ge(cV.h, tok_xf[i - 2])
                q.dma_start(out=x2t[pi], in_=self.x2scr[i]).then_inc(cL.h, 16)
                cL.n += 16
                tok_x2[i] = cL.n
            self.on("sync", d0_sync)

            tok_G = self.emit_ln(x2t[pi], c0, False, cL, tok_x2[i])

            def d0_v2(v, i=i, pi=pi, c0=c0, tok_G=tok_G):
                v.wait_ge(cA.h, tok_G)
                if i >= 2:
                    v.wait_ge(cP.h, tok_tp_last[i - 2])
                v.tensor_scalar(out=xf32[pi], in0=x2t[pi],
                                scalar1=self.stats[:, c0 + 6:c0 + 7],
                                scalar2=self.stats[:, c0 + 13:c0 + 14],
                                op0=ALU.subtract, op1=ALU.mult).then_inc(cV.h, 1)
                cV.n += 1
                tok_xf[i] = cV.n
            self.on("vector", d0_v2)

            for idx in range(NHC):
                kbox = []

                def d0_pe(t, i=i, pi=pi, idx=idx, kbox=kbox):
                    if idx == 0:
                        t.wait_ge(cV.h, tok_xf[i])
                    src = xf32[pi].rearrange("p (a q j) -> p a j q", a=2, j=8)
                    kbox.append(self.pe_transpose(t, src[:, idx // 8, idx % 8, :]))
                    tok_tp_last[i] = cP.n
                self.on("tensor", d0_pe)

                def d0_v3(v, i=i, idx=idx, kbox=kbox):
                    k = kbox[0]
                    v.wait_ge(cP.h, self.tp_tok[k])
                    v.tensor_copy(out=xfT[:, idx, i * 128:(i + 1) * 128],
                                  in_=self.tp_slot[k % 4]).then_inc(cV.h, 1)
                    cV.n += 1
                    self.tp_evac_done(k, cV.n)
                self.on("vector", d0_v3)

        tok_xfT_done = cV.n

        if self.debug:
            def d0_dbg(g):
                g.wait_ge(cV.h, tok_xfT_done)
                g.dma_start(out=self.dbg["d_xfT"][:], in_=xfT[:]).then_inc(cG.h, 16)
                cG.n += 16
            self.on("gpsimd", d0_dbg)

        if sub == "d0":
            self.flush()
            esD0.close()
            esD.close()
            return
        pair_off = []
        off = 0
        for cc in range(48):
            w = CHUNKS[cc][2]
            pair_off.append((off, w))
            off += 2 * w
        assert off == 11264

        tok_pk = [0] * 48
        tok_gsc = [0] * 48
        tok_dec = [0] * 48
        tok_gumm = {}
        tok_actw = {}
        tok_actw_pred = {}
        gu_slots = [pa[:, 0:512], pa[:, 512:1024], pb[:, 0:512], pb[:, 512:1024]]
        gu_ctr = [0]
        gu_slot_hist = {}

        def gu_prefetch(g, cc):
            o0, w = pair_off[cc]
            span = 2 * w
            bi = cc % 2
            if cc >= 2:
                g.wait_ge(cV.h, tok_dec[cc - 2])
            else:
                g.wait_ge(cV.h, tok_xfT_done)
            g.dma_start(out=pk[:, bi, :, 0:span],
                        in_=self.gup_t[:, o0:o0 + span].rearrange(
                            "(c p) o -> p c o", p=128)).then_inc(cG.h, 16)
            cG.n += 16
            tok_pk[cc] = cG.n
            g.dma_start(out=srep[:, bi, :, 0:2 * span],
                        in_=self.gur[:, :, 2 * o0:2 * o0 + 2 * span].bitcast(F16)
                        ).then_inc(cG.h, 16)
            cG.n += 16
            tok_gsc[cc] = cG.n

        self.on("gpsimd", lambda g: gu_prefetch(g, 0))
        self.on("gpsimd", lambda g: gu_prefetch(g, 1))

        def emit_gu_dec(cc):
            bi = cc % 2
            o0, w = pair_off[cc]
            span = 2 * w
            dec = dec_b[bi]

            def gu_dec(v, cc=cc, bi=bi, span=span, dec=dec):
                v.wait_ge(cG.h, tok_gsc[cc])
                if cc >= 2:
                    v.wait_ge(cP.h, tok_gumm[(cc - 2, 1, 1)])
                for Wp in range(2):
                    for jp in range(4):
                        sh1 = 4 * jp - 6
                        if sh1 >= 0:
                            v.tensor_scalar(out=p1s[:, 0:span], in0=pk[:, bi, Wp, 0:span],
                                            scalar1=sh1, scalar2=0x01C001C0,
                                            op0=ALU.logical_shift_right, op1=ALU.bitwise_and)
                        else:
                            v.tensor_scalar(out=p1s[:, 0:span], in0=pk[:, bi, Wp, 0:span],
                                            scalar1=-sh1, scalar2=0x01C001C0,
                                            op0=ALU.logical_shift_left, op1=ALU.bitwise_and)
                        sh2 = 4 * jp - 12
                        v.tensor_scalar(out=pint[:, 0:span], in0=pk[:, bi, Wp, 0:span],
                                        scalar1=-sh2, scalar2=self.msign_s[:, 0:1],
                                        op0=ALU.logical_shift_left, op1=ALU.bitwise_and)
                        v.tensor_tensor(out=pint[:, 0:span], in0=p1s[:, 0:span],
                                        in1=pint[:, 0:span], op=ALU.bitwise_or)
                        v.scalar_tensor_tensor(out=dec[:, Wp * 4 + jp, 0:2 * span],
                                               in0=pint[:, 0:span].bitcast(BF16),
                                               scalar=P126,
                                               in1=srep[:, bi, Wp, 0:2 * span],
                                               op0=ALU.mult, op1=ALU.mult)
                v.engine_nop().then_inc(cV.h, 1)
                cV.n += 1
                tok_dec[cc] = cV.n
            self.on("vector", gu_dec)
            if cc + 2 < 48:
                self.on("gpsimd", lambda g, c=cc + 2: gu_prefetch(g, c))

        def emit_gu_mm(cc):
            W_d, j_d, wdt = CHUNKS[cc]
            bi = cc % 2
            dec = dec_b[bi]
            sl, po, _ = act_slot(cc)
            for qhf in range(2):
                for half in range(2):
                    k = gu_ctr[0]
                    slot = gu_slots[k % 4]
                    prevk = gu_slot_hist.get(k % 4)
                    gu_slot_hist[k % 4] = (cc, qhf, half)
                    gu_ctr[0] += 1

                    def gu_mm(t, cc=cc, qhf=qhf, half=half, slot=slot, prevk=prevk,
                              wdt=wdt, po=po, dec=dec):
                        t.wait_ge(cV.h, tok_dec[cc])
                        if prevk is not None:
                            t.wait_ge(cV.h, tok_actw[prevk[:2]])
                        decv = dec.rearrange("p s (o two) -> p s two o", two=2)
                        out = slot[po:po + wdt, :] if po else slot[0:wdt, :]
                        tp = (0, po) if po else None
                        for idx in range(NHC):
                            Wp, j = idx // 8, idx % 8
                            lhs = decv[:, Wp * 4 + (j % 4), j // 4,
                                       half * wdt:(half + 1) * wdt]
                            inst = t.matmul(out, lhs, xfT[:, idx, qhf * 512:(qhf + 1) * 512],
                                            start=(idx == 0), stop=(idx == NHC - 1),
                                            tile_position=tp)
                        inst.then_inc(cP.h, 1)
                        cP.n += 1
                        tok_gumm[(cc, qhf, half)] = cP.n
                    self.on("tensor", gu_mm)

                ss4 = (2 * cc + qhf) % 4

                def gu_silu(s, cc=cc, qhf=qhf, wdt=wdt, po=po, ss4=ss4):
                    gk = gu_ctr[0] - 2
                    gslot = gu_slots[gk % 4]
                    s.wait_ge(cP.h, tok_gumm[(cc, qhf, 0)])
                    if (cc, qhf) in tok_actw_pred:
                        s.wait_ge(cV.h, tok_actw_pred[(cc, qhf)])
                    src = gslot[po:po + wdt, :] if po else gslot[0:wdt, :]
                    dst = silu4[po:po + wdt, ss4, :] if po else silu4[0:wdt, ss4, :]
                    s.activation(out=dst, in_=src, func=AF.Silu).then_inc(cA.h, 1)
                    cA.n += 1
                self.on("scalar", gu_silu)
                tok_silu = cA.n

                def gu_act(v, cc=cc, qhf=qhf, wdt=wdt, po=po, sl=sl, tok_silu=tok_silu,
                           ss4=ss4):
                    uk = gu_ctr[0] - 1
                    uslot = gu_slots[uk % 4]
                    v.wait_ge(cA.h, tok_silu)
                    v.wait_ge(cP.h, tok_gumm[(cc, qhf, 1)])
                    src_u = uslot[po:po + wdt, :] if po else uslot[0:wdt, :]
                    src_s = silu4[po:po + wdt, ss4, :] if po else silu4[0:wdt, ss4, :]
                    dsta = act[po:po + wdt, sl, qhf * 512:(qhf + 1) * 512] if po else \
                        act[0:wdt, sl, qhf * 512:(qhf + 1) * 512]
                    v.scalar_tensor_tensor(out=dsta, in0=src_u, scalar=1.0, in1=src_s,
                                           op0=ALU.mult, op1=ALU.mult).then_inc(cV.h, 1)
                    cV.n += 1
                    tok_actw[(cc, qhf)] = cV.n
                self.on("vector", gu_act)
                nxt = (cc + 2, qhf) if qhf == 0 else (cc + 2, 1)
                tok_actw_pred[nxt] = tok_actw[(cc, qhf)]

        emit_gu_dec(0)
        for cc in range(48):
            if cc + 1 < 48:
                emit_gu_dec(cc + 1)
            emit_gu_mm(cc)

        tok_act_done = cV.n

        if self.debug:
            def gu_dbg(g):
                g.wait_ge(cV.h, tok_act_done)
                g.dma_start(out=self.dbg["d_act"][:], in_=act[:]).then_inc(cG.h, 16)
                cG.n += 16
            self.on("gpsimd", gu_dbg)

        self.flush()
        esD0.close()
        if sub == "dgu":
            esD.close()
            return

        dd3 = self.sbuf(esD, "dd3", [128, 3, 4, 1024], BF16)
        pintd = self.sbuf(esD, "pintd", [128, 512], I32)
        p1d = self.sbuf(esD, "p1d", [128, 512], I32)
        pkdw = self.sbuf(esD, "pkdw", [128, 2, 512], I32)
        srepd = self.sbuf(esD, "srepd", [128, 2, 1024], BF16)
        dd5c = self.sbuf(esD, "dd5c", [128, 4, 512], BF16)
        x2dn = self.vreg32[:, 0:2048]
        y_bf = self.vreg[:, 4096:20480].rearrange("p (t d) -> p t d", t=OT)

        y8 = [pa[:, 0:512], pa[:, 512:1024], pb[:, 0:512], pb[:, 512:1024],
              pcs[:, 0:512], pcs[:, 512:1024], pd[:, 0:512], pd[:, 512:1024]]
        tok_pkw = [0] * 24
        tok_dsc = [0] * 24
        tok_ddec = [0] * 24
        tok_wmm = [0] * 24
        tok_ymm = {}
        tok_yev = {}

        def dn_prefetch(g, k):
            hg, W = divmod(k, 6)
            if k >= 2:
                g.wait_ge(cV.h, tok_ddec[k - 2])
            else:
                g.wait_ge(cV.h, tok_act_done)
            if W < 5:
                g.dma_start(out=pkdw[:, k % 2, :],
                            in_=self.down_t[128 * W:128 * (W + 1), hg * 512:(hg + 1) * 512]
                            ).then_inc(cG.h, 16)
                cG.n += 16
            else:
                g.dma_start(out=pkdw[0:64, k % 2, :],
                            in_=self.down_t[640:704, hg * 512:(hg + 1) * 512]
                            ).then_inc(cG.h, 16)
                g.dma_start(out=pkdw[64:128, k % 2, :],
                            in_=self.down_t[640:704, hg * 512:(hg + 1) * 512]
                            ).then_inc(cG.h, 16)
                cG.n += 32
            tok_pkw[k] = cG.n
            g.dma_start(out=srepd[:, k % 2, :],
                        in_=self.dnr[:, W, hg * 1024:(hg + 1) * 1024].bitcast(F16)
                        ).then_inc(cG.h, 16)
            cG.n += 16
            tok_dsc[k] = cG.n

        self.on("gpsimd", lambda g: dn_prefetch(g, 0))
        self.on("gpsimd", lambda g: dn_prefetch(g, 1))

        def emit_ss(hg):
            for ts in range(OT):
                def dn_ss(v, hg=hg, ts=ts):
                    sc_ = 32 + 4 * (ts % 4)
                    ac_ = 4 * ts
                    v.wait_ge(cA.h, tok_yev[(hg, ts)])
                    yv = y_bf[:, ts, hg * 512:(hg + 1) * 512]
                    v.tensor_tensor(out=self.junkc[:, 0:512], in0=yv, in1=yv, op=ALU.mult)
                    v.engine_nop().then_inc(cV.h, 1)
                    cV.n += 1
                    v.wait_ge(cV.h, cV.n)
                    v.reduce_sum(out=self.ss[:, sc_:sc_ + 1],
                                 in_=self.junkc[:, 0:512], axis=AX.X)
                    v.engine_nop().then_inc(cV.h, 1)
                    cV.n += 1
                    v.wait_ge(cV.h, cV.n)
                    v.tensor_tensor(out=self.ss[:, ac_:ac_ + 1], in0=self.ss[:, ac_:ac_ + 1],
                                    in1=self.ss[:, sc_:sc_ + 1],
                                    op=ALU.add).then_inc(cV.h, 1)
                    cV.n += 1
                    self.tok_ssv[(hg, ts)] = cV.n
                self.on("vector", dn_ss)

        def emit_dn_dec(k):
            W = k % 6

            def dn_dec(v, k=k, W=W):
                v.wait_ge(cG.h, tok_dsc[k])
                if k >= 3:
                    v.wait_ge(cP.h, tok_wmm[k - 3])
                src = pkdw[:, k % 2, :]
                for jp in range(4):
                    sh1 = 4 * jp - 6
                    if sh1 >= 0:
                        v.tensor_scalar(out=p1d[:], in0=src, scalar1=sh1,
                                        scalar2=0x01C001C0,
                                        op0=ALU.logical_shift_right, op1=ALU.bitwise_and)
                    else:
                        v.tensor_scalar(out=p1d[:], in0=src, scalar1=-sh1,
                                        scalar2=0x01C001C0,
                                        op0=ALU.logical_shift_left, op1=ALU.bitwise_and)
                    sh2 = 4 * jp - 12
                    v.tensor_scalar(out=pintd[:], in0=src, scalar1=-sh2,
                                    scalar2=self.msign_s[:, 0:1],
                                    op0=ALU.logical_shift_left, op1=ALU.bitwise_and)
                    v.tensor_tensor(out=pintd[:], in0=p1d[:],
                                    in1=pintd[:], op=ALU.bitwise_or)
                    v.scalar_tensor_tensor(out=dd3[:, k % 3, jp, :],
                                           in0=pintd[:].bitcast(BF16), scalar=P126,
                                           in1=srepd[:, k % 2, :],
                                           op0=ALU.mult, op1=ALU.mult)
                    if W == 5:
                        dv = dd3[:, k % 3, jp, :].rearrange("p (h two) -> p two h", two=2)
                        v.tensor_copy(out=dd5c[0:64, jp, :], in_=dv[0:64, 0, :])
                        v.tensor_copy(out=dd5c[64:128, jp, :], in_=dv[64:128, 1, :])
                v.engine_nop().then_inc(cV.h, 1)
                cV.n += 1
                tok_ddec[k] = cV.n
            self.on("vector", dn_dec)
            if k + 2 < 24:
                self.on("gpsimd", lambda g, k2=k + 2: dn_prefetch(g, k2))

        def emit_dn_mm(k):
            hg, W = divmod(k, 6)

            def dn_mm(t, k=k, hg=hg, W=W):
                t.wait_ge(cV.h, tok_ddec[k])
                inst = None
                for ts in range(OT):
                    slot = y8[ts]
                    if W == 0:
                        if hg == 0:
                            t.wait_ge(cV.h, tok_act_done)
                        else:
                            t.wait_ge(cA.h, tok_yev[(hg - 1, ts)])
                    if W < 5:
                        for j in range(8):
                            cc = 8 * W + j
                            ddv = dd3[:, k % 3, j % 4, :].rearrange(
                                "p (o two) -> p two o", two=2)
                            inst = t.matmul(slot, act[0:128, cc, ts * 128:(ts + 1) * 128],
                                            ddv[:, j // 4, :],
                                            start=(W == 0 and j == 0), stop=False)
                    else:
                        for m in range(4):
                            inst = t.matmul(slot, act[0:128, 40 + m, ts * 128:(ts + 1) * 128],
                                            dd5c[:, m, :], start=False, stop=(m == 3))
                        inst.then_inc(cP.h, 1)
                        cP.n += 1
                        tok_ymm[(hg, ts)] = cP.n
                if W < 5:
                    inst.then_inc(cP.h, 1)
                    cP.n += 1
                tok_wmm[k] = cP.n
            self.on("tensor", dn_mm)

            if W == 5:
                for ts in range(OT):
                    def dn_yev(s, hg=hg, ts=ts):
                        s.wait_ge(cP.h, tok_ymm[(hg, ts)])
                        s.copy(out=y_bf[:, ts, hg * 512:(hg + 1) * 512],
                               in_=y8[ts]).then_inc(cA.h, 1)
                        cA.n += 1
                        tok_yev[(hg, ts)] = cA.n
                    self.on("scalar", dn_yev)
                emit_ss(hg)

        emit_dn_dec(0)
        for k in range(24):
            if k + 1 < 24:
                emit_dn_dec(k + 1)
            emit_dn_mm(k)
        tok_ssv_last = cV.n
        if self.debug:
            def dall_dbg(g):
                g.wait_ge(cV.h, tok_ssv_last)
                g.dma_start(out=self.dbg["d_ybf"][:],
                            in_=y_bf[:, :, :]).then_inc(cG.h, 16)
                for q_ in range(4):
                    g.dma_start(out=self.dbg["d_dd"][:, q_, :],
                                in_=dd3[:, 23 % 3, q_, :]).then_inc(cG.h, 16)
                g.dma_start(out=self.dbg["d_ss"][:], in_=self.ss[:]).then_inc(cG.h, 16)
                cG.n += 96
            self.on("gpsimd", dall_dbg)
        if sub == "dmm":
            if self.debug:
                def dmm_dbg(g):
                    g.wait_ge(cV.h, tok_ssv_last)
                    g.dma_start(out=self.dbg["d_ybf"][:],
                                in_=y_bf[:, :, :]).then_inc(cG.h, 16)
                    cG.n += 16
                self.on("gpsimd", dmm_dbg)
            self.flush()
            esD.close()
            return

        tok_x2d = [0] * OT
        tok_outs = [0] * OT
        for ts in range(OT):
            def ep_sync(q, ts=ts):
                q.wait_ge(cV.h, self.tok_ssv[(3, ts)])
                if ts >= 1:
                    q.wait_ge(cL.h, tok_outs[ts - 1])
                q.dma_start(out=x2dn, in_=self.x2scr[ts]).then_inc(cL.h, 16)
                cL.n += 16
                tok_x2d[ts] = cL.n
            self.on("sync", ep_sync)

            def ep_rms_v(v, ts=ts):
                p_ = 48 + 4 * (ts % 2)
                v.tensor_scalar(out=self.ss[:, p_:p_ + 1], in0=self.ss[:, 4 * ts:4 * ts + 1],
                                scalar1=1.0 / D, scalar2=1e-6, op0=ALU.mult,
                                op1=ALU.add).then_inc(cV.h, 1)
                cV.n += 1
            self.on("vector", ep_rms_v)
            tok_ms = cV.n

            def ep_rms_a(s, ts=ts, tok_ms=tok_ms):
                b_ = 56 + 4 * (ts % 2)
                m_ = 48 + 4 * (ts % 2)
                s.wait_ge(cV.h, tok_ms)
                s.activation(out=self.ss[:, b_:b_ + 1], in_=self.ss[:, m_:m_ + 1],
                             func=AF.Ln)
                s.nop(cycle_cnt=1, nofuse=True).then_inc(cA.h, 1)
                cA.n += 1
                s.wait_ge(cA.h, cA.n)
                s.activation(out=self.ss[:, b_ + 1:b_ + 2], in_=self.ss[:, b_:b_ + 1],
                             func=AF.Exp, scale=-0.5)
                s.nop(cycle_cnt=1, nofuse=True).then_inc(cA.h, 1)
                cA.n += 1
                s.wait_ge(cA.h, cA.n)
                s.activation(out=self.ss[:, b_ + 2:b_ + 3], in_=self.ss[:, b_ + 1:b_ + 2],
                             func=AF.Exp).then_inc(cA.h, 1)
            self.on("scalar", ep_rms_a)
            cA.n += 1
            tok_rr = cA.n

            def ep_out_v(v, ts=ts, tok_rr=tok_rr):
                v.wait_ge(cA.h, tok_rr)
                v.wait_ge(cL.h, tok_x2d[ts])
                v.scalar_tensor_tensor(out=x2dn, in0=y_bf[:, ts, :],
                                       scalar=self.ss[:, 57 + 4 * (ts % 2):58 + 4 * (ts % 2)],
                                       in1=x2dn, op0=ALU.mult,
                                       op1=ALU.add).then_inc(cV.h, 1)
                cV.n += 1
                self.tok_outv[ts] = cV.n
            self.on("vector", ep_out_v)

            def ep_store(q, ts=ts):
                q.wait_ge(cV.h, self.tok_outv[ts])
                q.dma_start(out=self.out_t[ts], in_=x2dn).then_inc(cL.h, 16)
                cL.n += 16
                tok_outs[ts] = cL.n
            self.on("sync", ep_store)

        self.tok_final = cL.n
        self.flush()
        esD.close()

    def finish(self):
        cL, cG = self.cL, self.cG
        tokL, tokG = cL.n, cG.n

        def fin(g):
            g.wait_ge(cL.h, tokL)
            g.wait_ge(cG.h, tokG)
        self.on("gpsimd", fin)


def _host_prep(inputs):
    X = np.ascontiguousarray(np.asarray(inputs["L0_attn_X"], dtype=np.float32))
    pr = {}
    pr["wq"] = np.ascontiguousarray(np.asarray(inputs["L0_attn_Wq"], np.float32))
    pr["wk"] = np.ascontiguousarray(np.asarray(inputs["L0_attn_Wk"], np.float32))
    pr["wv"] = np.ascontiguousarray(np.asarray(inputs["L0_attn_Wv"], np.float32))
    pr["wo"] = np.ascontiguousarray(np.asarray(inputs["L0_attn_Wo"], np.float32))
    g = np.asarray(inputs["L0_attn_ln_g"], np.float32)
    lb = np.asarray(inputs["L0_attn_ln_b"], np.float32)
    bo = np.asarray(inputs["L0_attn_bo"], np.float32)
    pr["g_cols"] = np.ascontiguousarray(g.reshape(16, 128).T)
    pr["lnb_cols"] = np.ascontiguousarray(lb.reshape(16, 128).T)
    pr["bo_cols"] = np.ascontiguousarray(bo.reshape(16, 128).T)
    pr["b_rows"] = np.ascontiguousarray(np.stack([
        np.asarray(inputs["L0_attn_bq"], np.float32),
        np.asarray(inputs["L0_attn_bk"], np.float32),
        np.asarray(inputs["L0_attn_bv"], np.float32),
        bo]))

    rows = []
    for cc in range(48):
        W, j, w = CHUNKS[cc]
        ilist = [8 * wd + j for wd in range(128 * W, 128 * W + w)]
        rows.extend(ilist)
        rows.extend([IFF + i for i in ilist])
    rows = np.array(rows)
    gup = np.asarray(inputs["L0_ffn_gate_up_packed"])
    pr["gup_t"] = np.ascontiguousarray(gup[rows].T)
    sc = np.asarray(inputs["L0_ffn_gate_up_scales"])
    scp = np.repeat(sc[:, rows], 2, axis=1)
    gur = np.broadcast_to(scp.reshape(2, 8, 1, 22528),
                          (2, 8, 16, 22528)).reshape(2, 128, 22528)
    pr["gur"] = np.ascontiguousarray(gur.transpose(1, 0, 2)).view(np.int16)
    dn = np.asarray(inputs["L0_ffn_down_packed"])
    pr["down_t"] = np.ascontiguousarray(dn.T)
    ds = np.asarray(inputs["L0_ffn_down_scales"])
    dsp = np.zeros((48, D), ds.dtype)
    dsp[:44] = ds
    dsd = np.repeat(dsp, 2, axis=1)
    base = dsd.reshape(6, 8, 2 * D)
    dnr = np.repeat(base, 16, axis=1)
    dnr = dnr.copy()
    dnr[5] = np.tile(np.repeat(base[5][:4], 16, axis=0), (2, 1))
    pr["dnr"] = np.ascontiguousarray(dnr.transpose(1, 0, 2)).view(np.int16)

    pr["identf"] = np.eye(128, dtype=np.float32)
    pr["ones_col"] = np.ones((128, 1), ml_dtypes.bfloat16)
    pr["ones_row"] = np.ones((1, 128), ml_dtypes.bfloat16)
    pr["m_sign"] = np.full((128, 1), np.uint32(0x80008000)).view(np.int32)
    pr["identb"] = np.eye(128, dtype=ml_dtypes.bfloat16)
    return X, pr


def _bmask(par):
    m = np.zeros((16, 128, 128), np.float32)
    kq = np.where(np.arange(128)[:, None] <= np.arange(128)[None, :], 0.0, -1e30)
    for kt in range(8):
        m[kt] = kq
    if par == 0:
        m[8:] = -1e30
    return np.ascontiguousarray(m.transpose(2, 0, 1).astype(ml_dtypes.bfloat16))


def kernel(**inputs):
    X, pr = _host_prep(inputs)
    nc, _ = Build(debug=False).build()
    in_maps = []
    for c in range(8):
        b, par = c // 2, c & 1
        permtok = np.concatenate(
            [np.arange(128) + 128 * (par + 2 * i) for i in range(8)]
            + [np.arange(128) + 128 * ((1 - par) + 2 * i) for i in range(8)])
        im = dict(pr)
        im["x"] = np.ascontiguousarray(X[b][permtok])
        im["bmask"] = _bmask(par)
        in_maps.append(im)
    res = run_bass_kernel_spmd(nc, in_maps, core_ids=list(range(8)))
    out = np.empty((4, S, D), np.float32)
    for c in range(8):
        b, par = c // 2, c & 1
        o = np.asarray(res.results[c]["out"]).reshape(OT, 128, D)
        for i in range(OT):
            out[b, 128 * (par + 2 * i):128 * (par + 2 * i) + 128] = o[i]
    return out


# version 25 (final)
# speedup vs baseline: 1.0563x; 1.0061x over previous
import math
import os
import contextlib
import numpy as np
import ml_dtypes
import concourse.bass as bass
import concourse.mybir as mybir
from concourse.bass_utils import run_bass_kernel_spmd

F32 = mybir.dt.float32
BF16 = mybir.dt.bfloat16
F16 = mybir.dt.float16
I32 = mybir.dt.int32
ALU = mybir.AluOpType
AF = mybir.ActivationFunctionType
AX = mybir.AxisListType

D = 2048
S = 2048
H = 16
DH = 128
IFF = 5632
NT = 16
OT = 8
NHC = 16
EPS = 1e-5
SM_SCALE = 1.0 / math.sqrt(DH)
P126 = float(2.0 ** 126)

CHUNKS = [(cc // 8, cc % 8, (128 if cc < 40 else 64)) for cc in range(48)]


def act_slot(cc):
    if cc < 40:
        return cc, 0, 128
    j = cc - 40
    return 40 + (j % 4), 64 * (j // 4), 64


class _RecInstr:
    __slots__ = ("incs",)

    def __init__(self):
        self.incs = []

    def then_inc(self, sem, n):
        self.incs.append((sem, n))
        return self


class _Rec:
    __slots__ = ("_log",)

    def __init__(self, log):
        self._log = log

    def __getattr__(self, name):
        def call(*a, **kw):
            ri = _RecInstr()
            self._log.append((name, a, kw, ri))
            return ri
        return call


class Build:
    def __init__(self, debug=False, upto="d"):
        self.upto = upto
        self.debug = debug
        self.nc = bass.Bass("TRN2", target_bir_lowering=False, debug=False)
        self.es = contextlib.ExitStack()
        self.rlog = {e: [] for e in ("sync", "tensor", "vector", "scalar", "gpsimd")}

    def sbuf(self, st, name, shape, d):
        return st.enter_context(self.nc.sbuf_tensor(name, shape, d))

    def ch(self, name):
        class Ch:
            pass
        c = Ch()
        c.h = self.es.enter_context(self.nc.semaphore(name))
        c.n = 0
        return c

    def on(self, eng, f):
        f(_Rec(self.rlog[eng]))

    def flush(self):
        for eng, log in self.rlog.items():
            if not log:
                continue
            ops = log[:]
            log.clear()

            def body(e, ops=ops):
                for name, a, kw, ri in ops:
                    inst = getattr(e, name)(*a, **kw)
                    for sem, n in ri.incs:
                        inst = inst.then_inc(sem, n)
            getattr(self.blk, eng)(body)

    def build(self):
        nc = self.nc
        es = self.es
        dt = nc.dram_tensor
        self.x_in = dt("x", [S, D], F32, kind="ExternalInput").ap()
        self.wq = dt("wq", [D, D], F32, kind="ExternalInput").ap()
        self.wk = dt("wk", [D, D], F32, kind="ExternalInput").ap()
        self.wv = dt("wv", [D, D], F32, kind="ExternalInput").ap()
        self.wo = dt("wo", [D, D], F32, kind="ExternalInput").ap()
        self.g_cols = dt("g_cols", [128, 16], F32, kind="ExternalInput").ap()
        self.lnb_cols = dt("lnb_cols", [128, 16], F32, kind="ExternalInput").ap()
        self.bo_cols = dt("bo_cols", [128, 16], F32, kind="ExternalInput").ap()
        self.b_rows = dt("b_rows", [4, D], F32, kind="ExternalInput").ap()
        self.gup_t = dt("gup_t", [256, 11264], I32, kind="ExternalInput").ap()
        self.gur = dt("gur", [128, 2, 22528], mybir.dt.int16, kind="ExternalInput").ap()
        self.down_t = dt("down_t", [704, D], I32, kind="ExternalInput").ap()
        self.dnr = dt("dnr", [128, 6, 2 * D], mybir.dt.int16, kind="ExternalInput").ap()
        self.bmask = dt("bmask", [128, 16, 128], BF16, kind="ExternalInput").ap()
        self.identf = dt("identf", [128, 128], F32, kind="ExternalInput").ap()
        self.ones_col = dt("ones_col", [128, 1], BF16, kind="ExternalInput").ap()
        self.ones_row = dt("ones_row", [1, 128], BF16, kind="ExternalInput").ap()
        self.m_sign = dt("m_sign", [128, 1], I32, kind="ExternalInput").ap()
        self.identb = dt("identb", [128, 128], BF16, kind="ExternalInput").ap()
        self.out_t = dt("out", [OT, 128, D], F32, kind="ExternalOutput").ap()
        self.kscr = dt("kscr", [4, S, 512], BF16, kind="Internal").ap()
        self.qscr = dt("qscr", [4, 1024, 512], BF16, kind="Internal").ap()
        self.x2scr = dt("x2scr", [OT, 128, D], F32, kind="Internal").ap()

        self.dbg = {}
        if self.debug:
            for name, shape, ddt in [
                ("d_xgT", [128, NHC, S], F32), ("d_V", [128, NT, D], F32),
                ("d_K", [4, S, 512], BF16), ("d_Q", [4, 1024, 512], BF16),
                ("d_AVn", [128, H, 512], F32), ("d_x2", [OT, 128, D], F32),
                ("d_act", [128, 44, 1024], F32), ("d_negb", [1, 16], F32),
                ("d_xfT", [128, NHC, 1024], F32), ("d_wgu", [128, 8, 512], F32),
                ("d_stats", [128, 32], F32), ("d_xh", [128, 4096], F32),
                ("d_ybf", [128, OT, D], F32), ("d_dd", [128, 4, 1024], F32),
                ("d_bp", [1, D], F32), ("d_bstage", [1, 1536], F32),
                ("d_ss", [128, 64], F32),
            ]:
                self.dbg[name] = dt(name, shape, ddt, kind="ExternalOutput").ap()

        sb = self.sbuf
        self.ident_s = sb(es, "ident_s", [128, 128], F32)
        self.ones_c = sb(es, "ones_c", [128, 1], BF16)
        self.ones_r = sb(es, "ones_r", [1, 128], BF16)
        self.msign_s = sb(es, "msign_s", [128, 1], I32)
        self.g_s = sb(es, "g_s", [128, 16], F32)
        self.lnb_s = sb(es, "lnb_s", [128, 16], F32)
        self.bo_s = sb(es, "bo_s", [128, 16], F32)
        self.lnb_bf = sb(es, "lnb_bf", [128, 16], BF16)
        self.stats = sb(es, "stats", [128, 32], F32)
        self.kn2 = sb(es, "kn2", [128, 16], F32)
        self.qn2 = sb(es, "qn2", [128, 16], F32)
        self.negb_rep = sb(es, "negb_rep", [128, 16], F32)
        self.bstat = sb(es, "bstat", [128, 4], F32)
        self.nrow = sb(es, "nrow", [16, 128], F32)
        self.rowscr = sb(es, "rowscr", [1, 512], F32)
        self.rowbf = sb(es, "rowbf", [1, 512], BF16)
        self.junkb = sb(es, "junkb", [128, 4096], BF16)
        self.junkc = sb(es, "junkc", [128, 512], BF16)
        self.ss = sb(es, "ss", [128, 64], F32)

        self.vreg = sb(es, "vreg", [128, 32768], BF16)
        self.vreg32 = self.vreg[:].bitcast(F32)
        self.V_t = self.vreg[:].rearrange("p (t d) -> p t d", t=NT)
        self.xraw = [self.vreg32[:, 0:2048], self.vreg32[:, 2048:4096]]
        self.xh32 = [self.vreg32[:, 4096:6144], self.vreg32[:, 6144:8192]]

        self.cL = self.ch("cL")
        self.cC = self.ch("cC")
        self.cG = self.ch("cG")
        self.cP = self.ch("cP")
        self.cV = self.ch("cV")
        self.cA = self.ch("cA")

        self.blk = es.enter_context(nc.Block())
        self.pa = es.enter_context(nc.psum_tensor("pa", [128, 1024], F32))
        self.pb = es.enter_context(nc.psum_tensor("pb", [128, 1024], F32))
        self.pcs = es.enter_context(nc.psum_tensor("pcs", [128, 1024], F32))
        self.pd = es.enter_context(nc.psum_tensor("pd", [128, 1024], F32))
        self.tp_slot = [self.pcs[:, 0:128], self.pcs[:, 512:640],
                        self.pd[:, 0:128], self.pd[:, 512:640]]
        self.tp_ctr = 0
        self.tp_tok = []
        self.tp_ev = []

        self.tok_ssv = {}
        self.tok_outv = {}
        self._last_oev = None
        order = ["p0", "a", "b0", "b1", "b2", "b3", "b", "c", "d0", "dgu", "dmm", "d"]
        stop = order.index(self.upto)
        self.phase_p0()
        if stop >= 1:
            self.phase_a()
        if stop >= 2:
            self.phase_b(min(stop - 2, 4))
        elif stop >= 1:
            self.flush()
            self.esAB.close()
        stop = max(stop - 4, 0)
        if stop >= 3:
            self.phase_c()
        if stop >= 4:
            self.phase_d(["d0", "dgu", "dmm", "d"][min(stop - 4, 3)])
        self.finish()
        self.flush()
        es.close()
        return self.nc, self.dbg

    def phase_p0(self):
        cC, cV = self.cC, self.cV

        def p0_sync(q):
            for dst, src in [(self.ident_s, self.identf),
                             (self.ones_c, self.ones_col), (self.ones_r, self.ones_row),
                             (self.msign_s, self.m_sign),
                             (self.g_s, self.g_cols), (self.lnb_s, self.lnb_cols),
                             (self.bo_s, self.bo_cols)]:
                q.dma_start(out=dst[:], in_=src[:]).then_inc(cC.h, 16)
                cC.n += 16
        self.on("sync", p0_sync)
        tok_consts = cC.n

        def p0_v(v):
            v.wait_ge(cC.h, tok_consts)
            v.tensor_copy(out=self.lnb_bf[:], in_=self.lnb_s[:])
            v.memset(self.kn2[:], 0.0)
            v.memset(self.qn2[:], 0.0)
            v.memset(self.ss[:], 0.0).then_inc(cV.h, 1)
            cV.n += 1
        self.on("vector", p0_v)
        self.tok_p0v = cV.n

    def pe_transpose(self, t, src_ap):
        cP, cV = self.cP, self.cV
        k = self.tp_ctr
        if k >= 4:
            t.wait_ge(cV.h, self.tp_ev[k - 4])
        t.transpose(self.tp_slot[k % 4], src_ap, self.ident_s[:]).then_inc(cP.h, 1)
        cP.n += 1
        self.tp_tok.append(cP.n)
        self.tp_ctr += 1
        return k

    def tp_evac_done(self, k, tok):
        assert len(self.tp_ev) == k
        self.tp_ev.append(tok)

    def emit_ln(self, tile_ap, c0, dual, tok_ready_ch, tok_ready):
        cL, cA, cV = self.cL, self.cA, self.cV
        st = lambda i: self.stats[:, c0 + i:c0 + i + 1]

        pred_prod = cV.n + 5

        jb = (c0 // 16) * 2048
        box = {}

        def sq(s):
            s.wait_ge(tok_ready_ch.h, tok_ready)
            s.activation(out=self.junkb[:, jb:jb + 1024], in_=tile_ap[:, 0:1024],
                         func=AF.Square, accum_out=st(0)).then_inc(cA.h, 1)
            cA.n += 1
            s.activation(out=self.junkb[:, jb + 1024:jb + 2048], in_=tile_ap[:, 1024:2048],
                         func=AF.Square, accum_out=st(1)).then_inc(cA.h, 1)
            cA.n += 1
            s.wait_ge(cA.h, cA.n)
            s.activation(out=st(3), in_=st(0), func=AF.Square).then_inc(cA.h, 1)
            cA.n += 1
            box["tok_sq"] = cA.n
        self.on("scalar", sq)
        tok_sq = box["tok_sq"]

        def gch(s):
            s.wait_ge(cV.h, pred_prod)
            s.activation(out=st(12), in_=st(8), func=AF.Ln)
            s.nop(cycle_cnt=1, nofuse=True).then_inc(cA.h, 1)
            cA.n += 1
            s.wait_ge(cA.h, cA.n)
            s.activation(out=st(13), in_=st(12), func=AF.Exp, scale=-0.5)
            s.nop(cycle_cnt=1, nofuse=True).then_inc(cA.h, 1)
            cA.n += 1
            s.wait_ge(cA.h, cA.n)
            s.activation(out=st(2), in_=st(13), func=AF.Exp).then_inc(cA.h, 1)
            cA.n += 1

        def emit_g():
            self.on("scalar", gch)
            return cA.n

        def v1(v):
            v.wait_ge(tok_ready_ch.h, tok_ready)
            v.reduce_sum(out=st(4), in_=tile_ap, axis=AX.X)
            v.engine_nop().then_inc(cV.h, 1)
            cV.n += 1
            v.wait_ge(cV.h, cV.n)
            v.wait_ge(cA.h, tok_sq)
            v.tensor_tensor(out=st(5), in0=st(0), in1=st(1), op=ALU.add)
            v.tensor_scalar(out=st(6), in0=st(4), scalar1=1.0 / D, scalar2=None, op0=ALU.mult)
            v.engine_nop().then_inc(cV.h, 1)
            cV.n += 1
            v.wait_ge(cV.h, cV.n)
            v.tensor_tensor(out=st(7), in0=st(6), in1=st(6), op=ALU.mult)
            v.tensor_scalar(out=st(5), in0=st(5), scalar1=1.0 / D, scalar2=None, op0=ALU.mult)
            v.engine_nop().then_inc(cV.h, 1)
            cV.n += 1
            v.wait_ge(cV.h, cV.n)
            v.tensor_tensor(out=st(5), in0=st(5), in1=st(7), op=ALU.subtract)
            v.engine_nop().then_inc(cV.h, 1)
            cV.n += 1
            v.wait_ge(cV.h, cV.n)
            if dual:
                v.tensor_scalar(out=st(8), in0=st(5), scalar1=1.0 + EPS,
                                scalar2=EPS + EPS * EPS, op0=ALU.mult,
                                op1=ALU.add).then_inc(cV.h, 1)
            else:
                v.tensor_scalar(out=st(8), in0=st(5), scalar1=EPS, scalar2=None,
                                op0=ALU.add).then_inc(cV.h, 1)
            cV.n += 1
        self.on("vector", v1)
        assert cV.n == pred_prod
        return emit_g

    def phase_a(self):
        cL, cG, cP, cV, cA = self.cL, self.cG, self.cP, self.cV, self.cA
        nc = self.nc
        self.esAB = contextlib.ExitStack()
        self.xgT = self.sbuf(self.esAB, "xgT", [128, NHC, S], BF16)
        self.bprow_bf = self.sbuf(self.esAB, "bprow_bf", [1, 3, D], BF16)
        self.wqt = [self.sbuf(self.esAB, f"wquart{i}", [128, NHC, 512], BF16)
                    for i in range(2)]
        self.stage = self.sbuf(self.esAB, "stage", [128, 2, 512], BF16)

        tok_x = [0] * NT
        tok_xh = [0] * NT
        tok_tp_last = [0] * NT
        self.tok_xgT_tile = [0] * NT

        gpend = [None] * NT

        def tile_front(tt):
            pi = tt % 2
            c0 = 16 * pi

            def a_sync(q, tt=tt, pi=pi):
                if tt >= 2:
                    q.wait_ge(cV.h, tok_xh[tt - 2])
                q.wait_ge(cL.h, cL.n)
                q.dma_start(out=self.xraw[pi],
                            in_=self.x_in[tt * 128:(tt + 1) * 128, :]).then_inc(cL.h, 16)
                cL.n += 16
                tok_x[tt] = cL.n
            self.on("sync", a_sync)
            gpend[tt] = self.emit_ln(self.xraw[pi], c0, True, cL, tok_x[tt])

        def tile_back(tt):
            pi = tt % 2
            c0 = 16 * pi
            tok_G = gpend[tt]()

            def a_v2(v, tt=tt, pi=pi, c0=c0, tok_G=tok_G):
                v.wait_ge(cA.h, tok_G)
                if tt == 0:
                    v.wait_ge(cV.h, self.tok_p0v)
                if tt >= 2:
                    v.wait_ge(cP.h, tok_tp_last[tt - 2])
                v.tensor_scalar(out=self.xh32[pi], in0=self.xraw[pi],
                                scalar1=self.stats[:, c0 + 6:c0 + 7],
                                scalar2=self.stats[:, c0 + 13:c0 + 14],
                                op0=ALU.subtract, op1=ALU.mult).then_inc(cV.h, 1)
                cV.n += 1
                tok_xh[tt] = cV.n
            self.on("vector", a_v2)

            for hc in range(NHC):
                kbox = []

                def a_pe(t, tt=tt, pi=pi, hc=hc, kbox=kbox):
                    if hc == 0:
                        t.wait_ge(cV.h, tok_xh[tt])
                    kbox.append(self.pe_transpose(
                        t, self.xh32[pi][:, hc * 128:(hc + 1) * 128]))
                    tok_tp_last[tt] = cP.n
                self.on("tensor", a_pe)

                def a_v3(v, tt=tt, hc=hc, kbox=kbox):
                    k = kbox[0]
                    v.wait_ge(cP.h, self.tp_tok[k])
                    v.tensor_scalar(out=self.xgT[:, hc, tt * 128:(tt + 1) * 128],
                                    in0=self.tp_slot[k % 4],
                                    scalar1=self.g_s[:, hc:hc + 1],
                                    scalar2=self.lnb_s[:, hc:hc + 1],
                                    op0=ALU.mult, op1=ALU.add).then_inc(cV.h, 1)
                    cV.n += 1
                    self.tp_evac_done(k, cV.n)
                self.on("vector", a_v3)
            self.tok_xgT_tile[tt] = cV.n

        tile_front(0)
        for tt in range(NT):
            if tt + 1 < NT:
                tile_front(tt + 1)
            tile_back(tt)

        self.tok_xgT_done = cV.n

        if self.debug:
            def a_dbg(g):
                g.wait_ge(cV.h, self.tok_xgT_done)
                g.dma_start(out=self.dbg["d_xgT"][:], in_=self.xgT[:]).then_inc(cG.h, 16)
                g.dma_start(out=self.dbg["d_stats"][:], in_=self.stats[:]).then_inc(cG.h, 16)
                g.dma_start(out=self.dbg["d_xh"][:],
                            in_=self.vreg32[:, 4096:8192]).then_inc(cG.h, 16)
                cG.n += 48
            self.on("gpsimd", a_dbg)

    def phase_b(self, sub=4):
        cL, cG, cP, cV, cA = self.cL, self.cG, self.cP, self.cV, self.cA
        pa, pb, pcs, pd = self.pa, self.pb, self.pcs, self.pd
        b_slots = [pa[:, 0:512], pa[:, 512:1024], pb[:, 0:512], pb[:, 512:1024]]
        mats = [("k", self.wk, NT, 1), ("v", self.wv, NT, 2), ("q", self.wq, OT, 0)]
        nmats = min(sub + 1, 3)
        mats = mats[:nmats]
        ALLQ = [(mi, qo) for mi in range(nmats) for qo in range(4)]

        tok_wq = {}
        tok_bp = {}
        self.tok_mm = tok_mm = {}
        tok_evac = {}
        self.tok_sp = tok_sp = {}
        tok_n2 = {}
        slot_hist = []
        stage_hist = []

        def prefetch_w(g, idx):
            mi, qo = ALLQ[idx]
            wt = mats[mi][1]
            if idx >= 2:
                pm, pq = ALLQ[idx - 2]
                g.wait_ge(cP.h, tok_mm[(pm, pq, mats[pm][2] - 1)])
            g.dma_start(out=self.wqt[idx % 2][:],
                        in_=wt[:, qo * 512:(qo + 1) * 512].rearrange("(c p) o -> p c o", p=128)
                        ).then_inc(cG.h, 16)
            cG.n += 16
            tok_wq[(mi, qo)] = cG.n

        def brow_cast(g):
            for mi2 in range(nmats):
                bi2 = mats[mi2][3]
                g.dma_start(out=self.bprow_bf[0:1, mi2, :],
                            in_=self.b_rows[bi2:bi2 + 1, :]).then_inc(cG.h, 16)
                cG.n += 16
        self.on("gpsimd", brow_cast)
        tok_brow = cG.n

        self.on("gpsimd", lambda g: prefetch_w(g, 0))
        self.on("gpsimd", lambda g: prefetch_w(g, 1))

        slot_ctr = 0
        for mi in range(nmats):
            name, wt, nts, bi = mats[mi]
            for qo in range(4):
                idx = mi * 4 + qo
                buf = self.wqt[idx % 2]
                if idx + 1 < 4 * nmats:
                    self.on("gpsimd", lambda g, i=idx + 1: prefetch_w(g, i))

                for ts in range(nts):
                    key = (mi, qo, ts)
                    slot = slot_ctr % 4
                    slot_ctr += 1
                    prev_slot = None
                    for k2, s2 in reversed(slot_hist):
                        if s2 == slot:
                            prev_slot = k2
                            break
                    slot_hist.append((key, slot))
                    prev_stage = None
                    if mi != 1:
                        for k2 in reversed(stage_hist):
                            if k2[2] % 2 == ts % 2:
                                prev_stage = k2
                                break
                        stage_hist.append(key)

                    def b_pe(t, key=key, slot=slot, prev_slot=prev_slot, buf=buf):
                        mi, qo, ts = key
                        t.wait_ge(cG.h, tok_wq[(mi, qo)])
                        t.wait_ge(cG.h, tok_brow)
                        t.wait_ge(cV.h, self.tok_xgT_tile[ts])
                        if prev_slot is not None:
                            t.wait_ge(cA.h, tok_evac[prev_slot])
                        ps = b_slots[slot]
                        for hc in range(NHC):
                            t.matmul(ps, self.xgT[:, hc, ts * 128:(ts + 1) * 128],
                                     buf[:, hc, :], start=(hc == 0), stop=False)
                        t.matmul(ps, self.ones_r[:],
                                 self.bprow_bf[0:1, mi, qo * 512:(qo + 1) * 512],
                                 start=False, stop=True).then_inc(cP.h, 1)
                        cP.n += 1
                        tok_mm[key] = cP.n
                    self.on("tensor", b_pe)

                    def b_evac(s, key=key, slot=slot, prev_stage=prev_stage):
                        mi, qo, ts = key
                        s.wait_ge(cP.h, tok_mm[key])
                        ps = b_slots[slot]
                        if mi == 1:
                            s.copy(out=self.V_t[:, ts, qo * 512:(qo + 1) * 512],
                                   in_=ps).then_inc(cA.h, 1)
                        else:
                            if prev_stage is not None:
                                s.wait_ge(cL.h, tok_sp[prev_stage])
                                s.wait_ge(cV.h, tok_n2[prev_stage])
                            s.copy(out=self.stage[:, ts % 2, :], in_=ps).then_inc(cA.h, 1)
                        cA.n += 1
                        tok_evac[key] = cA.n
                    self.on("scalar", b_evac)

                    if mi != 1:
                        def b_v(v, key=key):
                            mi, qo, ts = key
                            dst = self.kn2 if mi == 0 else self.qn2
                            v.wait_ge(cA.h, tok_evac[key])
                            v.tensor_tensor(out=self.junkc[:, 0:512],
                                            in0=self.stage[:, ts % 2, :],
                                            in1=self.stage[:, ts % 2, :], op=ALU.mult)
                            v.engine_nop().then_inc(cV.h, 1)
                            cV.n += 1
                            v.wait_ge(cV.h, cV.n)
                            v.reduce_sum(
                                out=self.bstat[:].rearrange("p (c o) -> p c o", o=1),
                                in_=self.junkc[:, 0:512].rearrange("p (c f) -> p c f", f=128),
                                axis=AX.X)
                            v.engine_nop().then_inc(cV.h, 1)
                            cV.n += 1
                            v.wait_ge(cV.h, cV.n)
                            v.tensor_tensor(out=dst[:, qo * 4:qo * 4 + 4],
                                            in0=dst[:, qo * 4:qo * 4 + 4],
                                            in1=self.bstat[:],
                                            op=ALU.max).then_inc(cV.h, 1)
                            cV.n += 1
                            tok_n2[key] = cV.n
                        self.on("vector", b_v)

                        def b_spill(q, key=key):
                            mi, qo, ts = key
                            q.wait_ge(cA.h, tok_evac[key])
                            scr = self.kscr if mi == 0 else self.qscr
                            q.wait_ge(cL.h, cL.n)
                            q.dma_start(out=scr[qo, ts * 128:(ts + 1) * 128, :],
                                        in_=self.stage[:, ts % 2, :]).then_inc(cL.h, 16)
                            cL.n += 16
                            tok_sp[key] = cL.n
                        self.on("sync", b_spill)

        tok_lastn2 = cV.n
        self.tok_b_pe_done = cP.n
        self.tok_sp_last = max(tok_sp.values())
        if sub < 4:
            if self.debug:
                def b_dbg0(g):
                    g.wait_ge(cP.h, self.tok_b_pe_done)
                    g.wait_ge(cL.h, self.tok_sp_last)
                    if nmats >= 2:
                        g.wait_ge(cA.h, tok_evac[(1, 3, NT - 1)])
                        g.dma_start(out=self.dbg["d_V"][:], in_=self.V_t[:]).then_inc(cG.h, 16)
                        cG.n += 16
                    g.dma_start(out=self.dbg["d_K"][:], in_=self.kscr[:]).then_inc(cG.h, 16)
                    g.dma_start(out=self.dbg["d_bp"][:],
                                in_=self.bprow_bf[0:1, 0, :]).then_inc(cG.h, 16)
                    cG.n += 32
                    if nmats >= 3:
                        g.dma_start(out=self.dbg["d_Q"][:], in_=self.qscr[:]).then_inc(cG.h, 16)
                        cG.n += 16
                self.on("gpsimd", b_dbg0)
            self.flush()
            self.esAB.close()
            return
        tok_evac_last = tok_evac[(2, 3, OT - 1)]

        def bnd_settle(v):
            v.engine_nop().then_inc(cV.h, 1)
            cV.n += 1
            v.wait_ge(cV.h, cV.n)
            v.engine_nop().then_inc(cV.h, 1)
            cV.n += 1
        self.on("vector", bnd_settle)
        tok_lastn2 = cV.n

        def bnd_pe1(t):
            t.wait_ge(cV.h, tok_lastn2)
            t.wait_ge(cA.h, tok_evac_last)
            t.transpose(pcs[0:16, 0:128], self.kn2[:], self.ident_s[:])
            t.transpose(pd[0:16, 0:128], self.qn2[:], self.ident_s[:]).then_inc(cP.h, 1)
            cP.n += 1
        self.on("tensor", bnd_pe1)
        tok_btp = cP.n

        def bnd_v1(v):
            v.wait_ge(cP.h, tok_btp)
            v.reduce_max(out=self.nrow[0:16, 0:1], in_=pcs[0:16, 0:128], axis=AX.X)
            v.reduce_max(out=self.nrow[0:16, 1:2], in_=pd[0:16, 0:128], axis=AX.X)
            v.engine_nop().then_inc(cV.h, 1)
            cV.n += 1
            v.wait_ge(cV.h, cV.n)
            v.tensor_tensor(out=self.nrow[0:16, 2:3], in0=self.nrow[0:16, 0:1],
                            in1=self.nrow[0:16, 1:2], op=ALU.mult)
            v.engine_nop().then_inc(cV.h, 1)
            cV.n += 1
            v.wait_ge(cV.h, cV.n)
            v.engine_nop().then_inc(cV.h, 1)
            cV.n += 1
        self.on("vector", bnd_v1)
        tok_bv1 = cV.n

        def bnd_a1(s):
            s.wait_ge(cV.h, tok_bv1)
            s.activation(out=self.nrow[0:16, 4:5], in_=self.nrow[0:16, 2:3], func=AF.Ln)
            s.nop(cycle_cnt=1, nofuse=True).then_inc(cA.h, 1)
            cA.n += 1
            s.wait_ge(cA.h, cA.n)
            s.activation(out=self.nrow[0:16, 5:6], in_=self.nrow[0:16, 4:5], func=AF.Exp,
                         scale=0.5)
            s.nop(cycle_cnt=1, nofuse=True).then_inc(cA.h, 1)
            cA.n += 1
            s.wait_ge(cA.h, cA.n)
            s.activation(out=self.nrow[0:16, 12:13], in_=self.nrow[0:16, 5:6],
                         func=AF.Ln).then_inc(cA.h, 1)
            cA.n += 1
        self.on("scalar", bnd_a1)
        tok_ba1 = cA.n

        def bnd_v2(v):
            v.wait_ge(cA.h, tok_ba1)
            v.tensor_scalar(out=self.nrow[0:16, 8:9], in0=self.nrow[0:16, 5:6],
                            scalar1=-1.02 * SM_SCALE, scalar2=None, op0=ALU.mult)
            v.engine_nop().then_inc(cV.h, 1)
            cV.n += 1
            v.wait_ge(cV.h, cV.n)
            v.engine_nop().then_inc(cV.h, 1)
            cV.n += 1
        self.on("vector", bnd_v2)
        tok_bv2 = cV.n

        def bnd_pe2(t):
            t.wait_ge(cV.h, tok_bv2)
            t.transpose(pcs[0:1, 128:144], self.nrow[0:16, 8:9],
                        self.ident_s[0:16, 0:16]).then_inc(cP.h, 1)
            cP.n += 1
        self.on("tensor", bnd_pe2)
        tok_bt2 = cP.n

        def bnd_v3(v):
            v.wait_ge(cP.h, tok_bt2)
            v.tensor_copy(out=self.rowscr[0:1, 0:16], in_=pcs[0:1, 128:144])
            v.engine_nop().then_inc(cV.h, 1)
            cV.n += 1
            v.wait_ge(cV.h, cV.n)
            v.tensor_copy(out=self.rowbf[0:1, 0:16], in_=self.rowscr[0:1, 0:16])
            v.engine_nop().then_inc(cV.h, 1)
            cV.n += 1
            v.wait_ge(cV.h, cV.n)
            v.engine_nop().then_inc(cV.h, 1)
            cV.n += 1
        self.on("vector", bnd_v3)
        tok_bv3 = cV.n

        def bnd_pe3(t):
            t.wait_ge(cV.h, tok_bv3)
            t.matmul(pcs[:, 128:144], self.ones_r[:], self.rowbf[0:1, 0:16],
                     start=True, stop=True).then_inc(cP.h, 1)
            cP.n += 1
        self.on("tensor", bnd_pe3)
        tok_brep = cP.n

        def bnd_a2(s):
            s.wait_ge(cP.h, tok_brep)
            s.copy(out=self.negb_rep[:], in_=pcs[:, 128:144]).then_inc(cA.h, 1)
            cA.n += 1
        self.on("scalar", bnd_a2)
        self.tok_negb = cA.n

        if self.debug:
            def b_dbg(g):
                g.wait_ge(cA.h, self.tok_negb)
                g.dma_start(out=self.dbg["d_V"][:], in_=self.V_t[:]).then_inc(cG.h, 16)
                g.wait_ge(cL.h, tok_sp[(0, 3, NT - 1)])
                g.wait_ge(cL.h, tok_sp[(2, 3, OT - 1)])
                g.dma_start(out=self.dbg["d_K"][:], in_=self.kscr[:]).then_inc(cG.h, 16)
                g.dma_start(out=self.dbg["d_Q"][:], in_=self.qscr[:]).then_inc(cG.h, 16)
                cG.n += 48
            self.on("gpsimd", b_dbg)

            def b_dbg2(q):
                q.wait_ge(cA.h, self.tok_negb)
                q.wait_ge(cL.h, cL.n)
                q.dma_start(out=self.dbg["d_negb"][:],
                            in_=self.rowscr[0:1, 0:16]).then_inc(cL.h, 16)
                cL.n += 16
            self.on("sync", b_dbg2)

        self.flush()
        self.esAB.close()

    def phase_c(self):
        cL, cG, cP, cV, cA = self.cL, self.cG, self.cP, self.cV, self.cA
        pa, pb, pcs, pd = self.pa, self.pb, self.pcs, self.pd
        esC = contextlib.ExitStack()
        woq = [self.sbuf(esC, f"woq{i}", [128, NHC, 512], BF16) for i in range(2)]
        self.tri_s = self.sbuf(esC, "tri_s", [128, 16, 128], BF16)
        borow32 = self.sbuf(esC, "borow32", [1, D], F32)
        identb_s = self.sbuf(esC, "identb_s", [128, 128], BF16)
        borow_bf = self.sbuf(esC, "borow_bf", [1, D], BF16)
        ones512 = self.sbuf(esC, "ones512", [1, 512], BF16)

        def c_triload(q):
            q.wait_ge(cP.h, self.tok_b_pe_done)
            q.wait_ge(cL.h, self.tok_sp_last)
            q.wait_ge(cL.h, cL.n)
            q.dma_start(out=self.tri_s[:], in_=self.bmask[:]).then_inc(cL.h, 16)
            q.wait_ge(cL.h, cL.n)
            q.dma_start(out=borow32[:], in_=self.b_rows[3:4, :]).then_inc(cL.h, 16)
            q.wait_ge(cL.h, cL.n)
            q.dma_start(out=identb_s[:], in_=self.identb[:]).then_inc(cL.h, 16)
            cL.n += 48
        self.on("sync", c_triload)
        tok_tri = cL.n

        def c_boprep(v):
            v.wait_ge(cL.h, tok_tri)
            v.memset(ones512[:], 1.0)
            v.tensor_copy(out=borow_bf[:], in_=borow32[:])
            v.engine_nop().then_inc(cV.h, 1)
            cV.n += 1
            v.wait_ge(cV.h, cV.n)
            v.engine_nop().then_inc(cV.h, 1)
            cV.n += 1
        self.on("vector", c_boprep)
        tok_bo = cV.n
        PT = self.sbuf(esC, "PT", [128, 16, 512], BF16)
        AVn = self.sbuf(esC, "AVn", [128, H, 512], BF16)
        KT = self.sbuf(esC, "KT", [128, 2, 2048], BF16)
        QT = self.sbuf(esC, "QT", [128, 2, 512], BF16)
        X2h = self.sbuf(esC, "X2h", [128, 4, 2048], F32)
        ostage = self.sbuf(esC, "ostage", [128, 2, 512], F32)
        reprec = self.sbuf(esC, "reprec", [128, 512], F32)

        tok_sp = self.tok_sp
        tok_kt = [0, 0]
        tok_qt = [0, 0]
        tok_ktfree = [0, 0]
        tok_x2l = {}
        tok_x2sp = {}
        tok_wol = {}
        tok_wo_used = {}
        tok_oadd = {}
        self.tok_x2sp = tok_x2sp

        def kt_valid(kt, qh):
            imin = kt if kt < 8 else kt - 8
            loc = 128 * imin - 512 * qh
            if loc >= 512:
                return None
            return max(loc, 0), (0 <= loc)

        for qh in range(2):
            def c_xload(q, qh=qh):
                if qh == 0:
                    q.wait_ge(cA.h, self.tok_negb)
                else:
                    q.wait_ge(cL.h, tok_x2sp[(0, 3)])
                for ii in range(4):
                    q.dma_start(out=X2h[:, ii, :],
                                in_=self.x_in[(4 * qh + ii) * 128:(4 * qh + ii + 1) * 128, :]
                                ).then_inc(cL.h, 16)
                    cL.n += 16
                    tok_x2l[(qh, ii)] = cL.n
            self.on("sync", c_xload)

            tok_avn = {}
            slot_exp = {0: None, 1: None, 2: None, 3: None}

            def emit_c_load(qh, h):
                def c_load(q, h=h, qh=qh):
                    q.wait_ge(cP.h, tok_ktfree[h % 2])
                    q.wait_ge(cL.h, tok_sp[(0, h // 4, NT - 1)])
                    q.dma_start_transpose(
                        out=KT[:, h % 2, :],
                        in_=self.kscr[h // 4, :, (h % 4) * 128:(h % 4 + 1) * 128]
                        ).then_inc(cL.h, 16)
                    cL.n += 16
                    tok_kt[h % 2] = cL.n
                    q.wait_ge(cL.h, tok_sp[(2, h // 4, OT - 1)])
                    q.dma_start_transpose(
                        out=QT[:, h % 2, :],
                        in_=self.qscr[h // 4, 512 * qh:512 * (qh + 1),
                                      (h % 4) * 128:(h % 4 + 1) * 128]).then_inc(cL.h, 16)
                    cL.n += 16
                    tok_qt[h % 2] = cL.n
                self.on("sync", c_load)

            for h in range(H):
                if not (qh == 1 and h < 2):
                    emit_c_load(qh, h)

                kts = [kt for kt in range(16) if kt_valid(kt, qh) is not None]
                n_kt = len(kts)
                pe_base = cP.n
                pred_sc = {}
                pred_rav = {}
                NS = 4
                ev_list = [("S", i) for i in range(min(NS, n_kt))]
                for i in range(n_kt):
                    ev_list.append(("R", i))
                    if i + NS < n_kt:
                        ev_list.append(("S", i + NS))
                for off, (kind, i) in enumerate(ev_list):
                    if kind == "S":
                        pred_sc[i] = pe_base + off + 1
                    else:
                        pred_rav[i] = pe_base + off + 1
                exp_base = cA.n
                tok_exp = {kt: exp_base + i + 1 for i, kt in enumerate(kts)}

                score_slots = [pa[:, 0:512], pa[:, 512:1024], pb[:, 0:512], pb[:, 512:1024]]

                def c_exp_all(s, h=h, qh=qh, kts=kts, pred_sc=dict(pred_sc)):
                    for i, kt in enumerate(kts):
                        loc, _ = kt_valid(kt, qh)
                        sslot = score_slots[i % 4]
                        s.wait_ge(cP.h, pred_sc[i])
                        s.activation(out=PT[:, kt, loc:512], in_=sslot[:, loc:512],
                                     func=AF.Exp, bias=self.negb_rep[:, h:h + 1],
                                     scale=SM_SCALE).then_inc(cA.h, 1)
                        cA.n += 1
                self.on("scalar", c_exp_all)
                assert cA.n == exp_base + n_kt

                def c_pe_all(t, h=h, qh=qh, kts=kts, ev_list=list(ev_list)):
                    for kind, i in ev_list:
                        kt = kts[i]
                        loc, msk = kt_valid(kt, qh)
                        if kind == "S":
                            sslot = score_slots[i % 4]
                            t.wait_ge(cL.h, tok_kt[h % 2])
                            t.wait_ge(cL.h, tok_qt[h % 2])
                            if i >= 4:
                                t.wait_ge(cA.h, tok_exp[kts[i - 4]])
                            elif slot_exp[i % 4] is not None:
                                t.wait_ge(cA.h, slot_exp[i % 4])
                            if (qh, h) == (1, 0) and i < 4 and self._last_oev:
                                t.wait_ge(cA.h, self._last_oev)
                            inst = t.matmul(sslot[:, loc:512],
                                            KT[:, h % 2, kt * 128:(kt + 1) * 128],
                                            QT[:, h % 2, loc:512], start=True,
                                            stop=not msk)
                            if msk:
                                if (qh, h) == (0, 0):
                                    t.wait_ge(cL.h, tok_tri)
                                inst = t.matmul(sslot[:, loc:loc + 128],
                                                self.tri_s[:, kt, :], identb_s[:],
                                                start=False, stop=True)
                            inst.then_inc(cP.h, 1)
                            cP.n += 1
                            assert cP.n == pred_sc[i]
                        else:
                            first = i == 0
                            last = i == n_kt - 1
                            t.wait_ge(cA.h, tok_exp[kt])
                            av = pcs[:, 0:512] if h % 2 == 0 else pcs[:, 512:1024]
                            if first:
                                if h >= 2:
                                    t.wait_ge(cV.h, tok_avn[h - 2])
                                elif len(self.tp_ev):
                                    t.wait_ge(cV.h, self.tp_ev[-1])
                            t.matmul(pd[0:1, loc:512], self.ones_c[:], PT[:, kt, loc:512],
                                     start=first, stop=last)
                            t.matmul(av[:, loc:512],
                                     self.V_t[:, kt, h * 128:(h + 1) * 128],
                                     PT[:, kt, loc:512], start=first,
                                     stop=last).then_inc(cP.h, 1)
                            cP.n += 1
                            assert cP.n == pred_rav[i]
                self.on("tensor", c_pe_all)
                for i in range(n_kt):
                    slot_exp[i % 4] = tok_exp[kts[i]]

                tok_rav = cP.n
                tok_ktfree[h % 2] = cP.n

                def c_recip(s, tok_rav=tok_rav):
                    s.wait_ge(cP.h, tok_rav)
                    s.activation(out=self.rowscr[0:1, :], in_=pd[0:1, 0:512],
                                 func=AF.Ln)
                    s.nop(cycle_cnt=1, nofuse=True).then_inc(cA.h, 1)
                    cA.n += 1
                    s.wait_ge(cA.h, cA.n)
                    s.activation(out=self.rowbf[0:1, :], in_=self.rowscr[0:1, :],
                                 func=AF.Exp, scale=-1.0).then_inc(cA.h, 1)
                    cA.n += 1
                self.on("scalar", c_recip)
                tok_rc = cA.n

                def c_rep(t, tok_rc=tok_rc):
                    t.wait_ge(cA.h, tok_rc)
                    t.matmul(pd[:, 512:1024], self.ones_r[:], self.rowbf[0:1, :],
                             start=True, stop=True).then_inc(cP.h, 1)
                    cP.n += 1
                self.on("tensor", c_rep)
                tok_rep = cP.n

                def c_repc(s, tok_rep=tok_rep):
                    s.wait_ge(cP.h, tok_rep)
                    s.copy(out=reprec[:], in_=pd[:, 512:1024]).then_inc(cA.h, 1)
                    cA.n += 1
                self.on("scalar", c_repc)
                tok_repc = cA.n

                def c_avn(v, h=h, tok_repc=tok_repc):
                    v.wait_ge(cA.h, tok_repc)
                    av = pcs[:, 0:512] if h % 2 == 0 else pcs[:, 512:1024]
                    v.scalar_tensor_tensor(out=AVn[:, h, :], in0=av, scalar=1.0,
                                           in1=reprec[:], op0=ALU.mult,
                                           op1=ALU.mult).then_inc(cV.h, 1)
                    cV.n += 1
                    tok_avn[h] = cV.n
                self.on("vector", c_avn)
                tok_av_free = tok_avn[h]

            if self.debug and qh == 0:
                def c_dbg(g):
                    g.wait_ge(cV.h, tok_avn[H - 1])
                    g.dma_start(out=self.dbg["d_AVn"][:], in_=AVn[:]).then_inc(cG.h, 16)
                    cG.n += 16
                self.on("gpsimd", c_dbg)

            if qh == 0:
                emit_c_load(1, 0)
                emit_c_load(1, 1)

            tok_opj = {}
            tok_oev = {}
            for oq in range(4):
                widx = qh * 4 + oq

                def c_wol(g, qh=qh, oq=oq, widx=widx):
                    if widx >= 2:
                        g.wait_ge(cP.h, tok_wo_used.get(widx - 2, 0))
                    else:
                        g.wait_ge(cP.h, self.tok_b_pe_done)
                        g.wait_ge(cL.h, self.tok_sp_last)
                    g.dma_start(out=woq[oq % 2][:],
                                in_=self.wo[:, oq * 512:(oq + 1) * 512].rearrange(
                                    "(c p) o -> p c o", p=128)).then_inc(cG.h, 16)
                    cG.n += 16
                    tok_wol[widx] = cG.n
                self.on("gpsimd", c_wol)

                for oc4 in range(4):
                    oc = oq * 4 + oc4
                    oslot = pb[:, 0:512] if oc % 2 == 0 else pb[:, 512:1024]

                    def c_opj(t, qh=qh, oq=oq, oc4=oc4, oc=oc, oslot=oslot, widx=widx):
                        t.wait_ge(cG.h, tok_wol[widx])
                        t.wait_ge(cV.h, tok_avn[H - 1])
                        if oc >= 2:
                            t.wait_ge(cA.h, tok_oev[oc - 2])
                        t.wait_ge(cV.h, tok_bo)
                        for hc in range(H):
                            t.matmul(oslot,
                                     woq[oq % 2][:, hc, oc4 * 128:(oc4 + 1) * 128],
                                     AVn[:, hc, :], start=(hc == 0), stop=False)
                        inst = t.matmul(oslot, borow_bf[0:1, oc * 128:(oc + 1) * 128],
                                        ones512[:], start=False, stop=True)
                        inst.then_inc(cP.h, 1)
                        cP.n += 1
                        tok_opj[oc] = cP.n
                        tok_wo_used[widx] = cP.n
                    self.on("tensor", c_opj)

                    def c_oev(s, qh=qh, oc=oc, oslot=oslot):
                        s.wait_ge(cP.h, tok_opj[oc])
                        if oc >= 2:
                            s.wait_ge(cV.h, tok_oadd[(qh, oc - 2)])
                        s.copy(out=ostage[:, oc % 2, :], in_=oslot).then_inc(cA.h, 1)
                        cA.n += 1
                        tok_oev[oc] = cA.n
                    self.on("scalar", c_oev)

                    slots = []

                    def c_otp(t, qh=qh, oc=oc, slots=slots):
                        t.wait_ge(cA.h, tok_oev[oc])
                        for ii in range(4):
                            k = self.pe_transpose(
                                t, ostage[:, oc % 2, ii * 128:(ii + 1) * 128])
                            slots.append(k)
                    self.on("tensor", c_otp)

                    def c_oadd(v, qh=qh, oc=oc, slots=slots):
                        for ii in range(4):
                            k = slots[ii]
                            v.wait_ge(cP.h, self.tp_tok[k])
                            v.wait_ge(cL.h, tok_x2l[(qh, ii)])
                            v.scalar_tensor_tensor(
                                out=X2h[:, ii, oc * 128:(oc + 1) * 128],
                                in0=self.tp_slot[k % 4], scalar=1.0,
                                in1=X2h[:, ii, oc * 128:(oc + 1) * 128],
                                op0=ALU.mult, op1=ALU.add).then_inc(cV.h, 1)
                            cV.n += 1
                            self.tp_evac_done(k, cV.n)
                        tok_oadd[(qh, oc)] = cV.n
                    self.on("vector", c_oadd)

            self._last_oev = cA.n

            def c_x2spill(q, qh=qh):
                q.wait_ge(cV.h, tok_oadd[(qh, 15)])
                for ii in range(4):
                    q.dma_start(out=self.x2scr[4 * qh + ii],
                                in_=X2h[:, ii, :]).then_inc(cL.h, 16)
                    cL.n += 16
                    tok_x2sp[(qh, ii)] = cL.n
            self.on("sync", c_x2spill)

        if self.debug:
            def c_dbg2(q):
                for i in range(OT):
                    q.wait_ge(cL.h, tok_x2sp[(i // 4, i % 4)])
                    q.dma_start(out=self.dbg["d_x2"][i],
                                in_=self.x2scr[i]).then_inc(cL.h, 16)
                    cL.n += 16
            self.on("sync", c_dbg2)

        self.flush()
        esC.close()

    def phase_d(self, sub="d"):
        cL, cG, cP, cV, cA = self.cL, self.cG, self.cP, self.cV, self.cA
        pa, pb, pcs, pd = self.pa, self.pb, self.pcs, self.pd
        nc = self.nc
        tok_x2sp = self.tok_x2sp

        esD = contextlib.ExitStack()
        act = self.sbuf(esD, "act", [128, 44, 1024], BF16)
        esD0 = contextlib.ExitStack()
        xfT = self.sbuf(esD0, "xfT", [128, NHC, 1024], BF16)

        vb = self.vreg
        x2t = [self.vreg32[:, 0:2048], self.vreg32[:, 2048:4096]]
        xf32 = [self.vreg32[:, 4096:6144], self.vreg32[:, 6144:8192]]
        dec_b = [vb[:, 0:8192].rearrange("p (s o) -> p s o", s=8),
                 vb[:, 8192:16384].rearrange("p (s o) -> p s o", s=8)]
        pk = vb[:, 16384:20480].bitcast(I32).rearrange(
            "p (b w o) -> p b w o", b=2, w=2)
        silu4 = vb[:, 28672:32768].bitcast(F32).rearrange("p (s o) -> p s o", s=4)
        srep = vb[:, 25600:27648].rearrange("p (b w o) -> p b w o", b=2, w=2)
        pint = vb[:, 27648:28160].bitcast(I32)
        p1s = vb[:, 28160:28672].bitcast(I32)

        tok_x2 = [0] * OT
        tok_xf = [0] * OT
        tok_tp_last = [0] * OT
        for i in range(OT):
            pi = i % 2
            c0 = 16 * pi

            def d0_sync(q, i=i, pi=pi):
                q.wait_ge(cL.h, tok_x2sp[(i // 4, i % 4)])
                if i >= 2:
                    q.wait_ge(cV.h, tok_xf[i - 2])
                q.dma_start(out=x2t[pi], in_=self.x2scr[i]).then_inc(cL.h, 16)
                cL.n += 16
                tok_x2[i] = cL.n
            self.on("sync", d0_sync)

            tok_G = self.emit_ln(x2t[pi], c0, False, cL, tok_x2[i])()

            def d0_v2(v, i=i, pi=pi, c0=c0, tok_G=tok_G):
                v.wait_ge(cA.h, tok_G)
                if i >= 2:
                    v.wait_ge(cP.h, tok_tp_last[i - 2])
                v.tensor_scalar(out=xf32[pi], in0=x2t[pi],
                                scalar1=self.stats[:, c0 + 6:c0 + 7],
                                scalar2=self.stats[:, c0 + 13:c0 + 14],
                                op0=ALU.subtract, op1=ALU.mult).then_inc(cV.h, 1)
                cV.n += 1
                tok_xf[i] = cV.n
            self.on("vector", d0_v2)

            for idx in range(NHC):
                kbox = []

                def d0_pe(t, i=i, pi=pi, idx=idx, kbox=kbox):
                    if idx == 0:
                        t.wait_ge(cV.h, tok_xf[i])
                    src = xf32[pi].rearrange("p (a q j) -> p a j q", a=2, j=8)
                    kbox.append(self.pe_transpose(t, src[:, idx // 8, idx % 8, :]))
                    tok_tp_last[i] = cP.n
                self.on("tensor", d0_pe)

                def d0_v3(v, i=i, idx=idx, kbox=kbox):
                    k = kbox[0]
                    v.wait_ge(cP.h, self.tp_tok[k])
                    v.tensor_copy(out=xfT[:, idx, i * 128:(i + 1) * 128],
                                  in_=self.tp_slot[k % 4]).then_inc(cV.h, 1)
                    cV.n += 1
                    self.tp_evac_done(k, cV.n)
                self.on("vector", d0_v3)

        tok_xfT_done = cV.n

        if self.debug:
            def d0_dbg(g):
                g.wait_ge(cV.h, tok_xfT_done)
                g.dma_start(out=self.dbg["d_xfT"][:], in_=xfT[:]).then_inc(cG.h, 16)
                cG.n += 16
            self.on("gpsimd", d0_dbg)

        if sub == "d0":
            self.flush()
            esD0.close()
            esD.close()
            return
        pair_off = []
        off = 0
        for cc in range(48):
            w = CHUNKS[cc][2]
            pair_off.append((off, w))
            off += 2 * w
        assert off == 11264

        tok_pk = [0] * 48
        tok_gsc = [0] * 48
        tok_dec = [0] * 48
        tok_gumm = {}
        tok_actw = {}
        tok_actw_pred = {}
        gu_slots = [pa[:, 0:512], pa[:, 512:1024], pb[:, 0:512], pb[:, 512:1024]]
        gu_ctr = [0]
        gu_slot_hist = {}

        def gu_prefetch(g, cc):
            o0, w = pair_off[cc]
            span = 2 * w
            bi = cc % 2
            if cc >= 2:
                g.wait_ge(cV.h, tok_dec[cc - 2])
            else:
                g.wait_ge(cV.h, tok_xfT_done)
            g.dma_start(out=pk[:, bi, :, 0:span],
                        in_=self.gup_t[:, o0:o0 + span].rearrange(
                            "(c p) o -> p c o", p=128)).then_inc(cG.h, 16)
            cG.n += 16
            tok_pk[cc] = cG.n
            g.dma_start(out=srep[:, bi, :, 0:2 * span],
                        in_=self.gur[:, :, 2 * o0:2 * o0 + 2 * span].bitcast(F16)
                        ).then_inc(cG.h, 16)
            cG.n += 16
            tok_gsc[cc] = cG.n

        self.on("gpsimd", lambda g: gu_prefetch(g, 0))
        self.on("gpsimd", lambda g: gu_prefetch(g, 1))

        def emit_gu_dec(cc):
            bi = cc % 2
            o0, w = pair_off[cc]
            span = 2 * w
            dec = dec_b[bi]

            def gu_dec(v, cc=cc, bi=bi, span=span, dec=dec):
                v.wait_ge(cG.h, tok_gsc[cc])
                if cc >= 2:
                    v.wait_ge(cP.h, tok_gumm[(cc - 2, 1, 1)])
                for Wp in range(2):
                    for jp in range(4):
                        sh1 = 4 * jp - 6
                        if sh1 >= 0:
                            v.tensor_scalar(out=p1s[:, 0:span], in0=pk[:, bi, Wp, 0:span],
                                            scalar1=sh1, scalar2=0x01C001C0,
                                            op0=ALU.logical_shift_right, op1=ALU.bitwise_and)
                        else:
                            v.tensor_scalar(out=p1s[:, 0:span], in0=pk[:, bi, Wp, 0:span],
                                            scalar1=-sh1, scalar2=0x01C001C0,
                                            op0=ALU.logical_shift_left, op1=ALU.bitwise_and)
                        sh2 = 4 * jp - 12
                        v.tensor_scalar(out=pint[:, 0:span], in0=pk[:, bi, Wp, 0:span],
                                        scalar1=-sh2, scalar2=self.msign_s[:, 0:1],
                                        op0=ALU.logical_shift_left, op1=ALU.bitwise_and)
                        v.tensor_tensor(out=pint[:, 0:span], in0=p1s[:, 0:span],
                                        in1=pint[:, 0:span], op=ALU.bitwise_or)
                        v.scalar_tensor_tensor(out=dec[:, Wp * 4 + jp, 0:2 * span],
                                               in0=pint[:, 0:span].bitcast(BF16),
                                               scalar=P126,
                                               in1=srep[:, bi, Wp, 0:2 * span],
                                               op0=ALU.mult, op1=ALU.mult)
                v.engine_nop().then_inc(cV.h, 1)
                cV.n += 1
                tok_dec[cc] = cV.n
            self.on("vector", gu_dec)
            if cc + 2 < 48:
                self.on("gpsimd", lambda g, c=cc + 2: gu_prefetch(g, c))

        def emit_gu_mm(cc):
            W_d, j_d, wdt = CHUNKS[cc]
            bi = cc % 2
            dec = dec_b[bi]
            sl, po, _ = act_slot(cc)
            for qhf in range(2):
                for half in range(2):
                    k = gu_ctr[0]
                    slot = gu_slots[k % 4]
                    prevk = gu_slot_hist.get(k % 4)
                    gu_slot_hist[k % 4] = (cc, qhf, half)
                    gu_ctr[0] += 1

                    def gu_mm(t, cc=cc, qhf=qhf, half=half, slot=slot, prevk=prevk,
                              wdt=wdt, po=po, dec=dec):
                        t.wait_ge(cV.h, tok_dec[cc])
                        if prevk is not None:
                            t.wait_ge(cV.h, tok_actw[prevk[:2]])
                        decv = dec.rearrange("p s (o two) -> p s two o", two=2)
                        out = slot[po:po + wdt, :] if po else slot[0:wdt, :]
                        tp = (0, po) if po else None
                        for idx in range(NHC):
                            Wp, j = idx // 8, idx % 8
                            lhs = decv[:, Wp * 4 + (j % 4), j // 4,
                                       half * wdt:(half + 1) * wdt]
                            inst = t.matmul(out, lhs, xfT[:, idx, qhf * 512:(qhf + 1) * 512],
                                            start=(idx == 0), stop=(idx == NHC - 1),
                                            tile_position=tp)
                        inst.then_inc(cP.h, 1)
                        cP.n += 1
                        tok_gumm[(cc, qhf, half)] = cP.n
                    self.on("tensor", gu_mm)

                ss4 = (2 * cc + qhf) % 4

                def gu_silu(s, cc=cc, qhf=qhf, wdt=wdt, po=po, ss4=ss4):
                    gk = gu_ctr[0] - 2
                    gslot = gu_slots[gk % 4]
                    s.wait_ge(cP.h, tok_gumm[(cc, qhf, 0)])
                    if (cc, qhf) in tok_actw_pred:
                        s.wait_ge(cV.h, tok_actw_pred[(cc, qhf)])
                    src = gslot[po:po + wdt, :] if po else gslot[0:wdt, :]
                    dst = silu4[po:po + wdt, ss4, :] if po else silu4[0:wdt, ss4, :]
                    s.activation(out=dst, in_=src, func=AF.Silu).then_inc(cA.h, 1)
                    cA.n += 1
                self.on("scalar", gu_silu)
                tok_silu = cA.n

                def gu_act(v, cc=cc, qhf=qhf, wdt=wdt, po=po, sl=sl, tok_silu=tok_silu,
                           ss4=ss4):
                    uk = gu_ctr[0] - 1
                    uslot = gu_slots[uk % 4]
                    v.wait_ge(cA.h, tok_silu)
                    v.wait_ge(cP.h, tok_gumm[(cc, qhf, 1)])
                    src_u = uslot[po:po + wdt, :] if po else uslot[0:wdt, :]
                    src_s = silu4[po:po + wdt, ss4, :] if po else silu4[0:wdt, ss4, :]
                    dsta = act[po:po + wdt, sl, qhf * 512:(qhf + 1) * 512] if po else \
                        act[0:wdt, sl, qhf * 512:(qhf + 1) * 512]
                    v.scalar_tensor_tensor(out=dsta, in0=src_u, scalar=1.0, in1=src_s,
                                           op0=ALU.mult, op1=ALU.mult).then_inc(cV.h, 1)
                    cV.n += 1
                    tok_actw[(cc, qhf)] = cV.n
                self.on("vector", gu_act)
                nxt = (cc + 2, qhf) if qhf == 0 else (cc + 2, 1)
                tok_actw_pred[nxt] = tok_actw[(cc, qhf)]

        emit_gu_dec(0)
        for cc in range(48):
            if cc + 1 < 48:
                emit_gu_dec(cc + 1)
            emit_gu_mm(cc)

        tok_act_done = cV.n

        if self.debug:
            def gu_dbg(g):
                g.wait_ge(cV.h, tok_act_done)
                g.dma_start(out=self.dbg["d_act"][:], in_=act[:]).then_inc(cG.h, 16)
                cG.n += 16
            self.on("gpsimd", gu_dbg)

        self.flush()
        esD0.close()
        if sub == "dgu":
            esD.close()
            return

        dd3 = self.sbuf(esD, "dd3", [128, 3, 4, 1024], BF16)
        pintd = self.sbuf(esD, "pintd", [128, 512], I32)
        p1d = self.sbuf(esD, "p1d", [128, 512], I32)
        pkdw = self.sbuf(esD, "pkdw", [128, 2, 512], I32)
        srepd = self.sbuf(esD, "srepd", [128, 2, 1024], BF16)
        dd5c = self.sbuf(esD, "dd5c", [128, 4, 512], BF16)
        x2dn = self.vreg32[:, 0:2048]
        y_bf = self.vreg[:, 4096:20480].rearrange("p (t d) -> p t d", t=OT)

        y8 = [pa[:, 0:512], pa[:, 512:1024], pb[:, 0:512], pb[:, 512:1024],
              pcs[:, 0:512], pcs[:, 512:1024], pd[:, 0:512], pd[:, 512:1024]]
        tok_pkw = [0] * 24
        tok_dsc = [0] * 24
        tok_ddec = [0] * 24
        tok_wmm = [0] * 24
        tok_ymm = {}
        tok_yev = {}

        def dn_prefetch(g, k):
            hg, W = divmod(k, 6)
            if k >= 2:
                g.wait_ge(cV.h, tok_ddec[k - 2])
            else:
                g.wait_ge(cV.h, tok_act_done)
            if W < 5:
                g.dma_start(out=pkdw[:, k % 2, :],
                            in_=self.down_t[128 * W:128 * (W + 1), hg * 512:(hg + 1) * 512]
                            ).then_inc(cG.h, 16)
                cG.n += 16
            else:
                g.dma_start(out=pkdw[0:64, k % 2, :],
                            in_=self.down_t[640:704, hg * 512:(hg + 1) * 512]
                            ).then_inc(cG.h, 16)
                g.dma_start(out=pkdw[64:128, k % 2, :],
                            in_=self.down_t[640:704, hg * 512:(hg + 1) * 512]
                            ).then_inc(cG.h, 16)
                cG.n += 32
            tok_pkw[k] = cG.n
            g.dma_start(out=srepd[:, k % 2, :],
                        in_=self.dnr[:, W, hg * 1024:(hg + 1) * 1024].bitcast(F16)
                        ).then_inc(cG.h, 16)
            cG.n += 16
            tok_dsc[k] = cG.n

        self.on("gpsimd", lambda g: dn_prefetch(g, 0))
        self.on("gpsimd", lambda g: dn_prefetch(g, 1))

        def emit_ss(hg):
            for ts in range(OT):
                def dn_ss(v, hg=hg, ts=ts):
                    sc_ = 32 + 4 * (ts % 4)
                    ac_ = 4 * ts
                    v.wait_ge(cA.h, tok_yev[(hg, ts)])
                    yv = y_bf[:, ts, hg * 512:(hg + 1) * 512]
                    v.tensor_tensor(out=self.junkc[:, 0:512], in0=yv, in1=yv, op=ALU.mult)
                    v.engine_nop().then_inc(cV.h, 1)
                    cV.n += 1
                    v.wait_ge(cV.h, cV.n)
                    v.reduce_sum(out=self.ss[:, sc_:sc_ + 1],
                                 in_=self.junkc[:, 0:512], axis=AX.X)
                    v.engine_nop().then_inc(cV.h, 1)
                    cV.n += 1
                    v.wait_ge(cV.h, cV.n)
                    v.tensor_tensor(out=self.ss[:, ac_:ac_ + 1], in0=self.ss[:, ac_:ac_ + 1],
                                    in1=self.ss[:, sc_:sc_ + 1],
                                    op=ALU.add).then_inc(cV.h, 1)
                    cV.n += 1
                    self.tok_ssv[(hg, ts)] = cV.n
                self.on("vector", dn_ss)

        def emit_dn_dec(k):
            W = k % 6

            def dn_dec(v, k=k, W=W):
                v.wait_ge(cG.h, tok_dsc[k])
                if k >= 3:
                    v.wait_ge(cP.h, tok_wmm[k - 3])
                src = pkdw[:, k % 2, :]
                for jp in range(4):
                    sh1 = 4 * jp - 6
                    if sh1 >= 0:
                        v.tensor_scalar(out=p1d[:], in0=src, scalar1=sh1,
                                        scalar2=0x01C001C0,
                                        op0=ALU.logical_shift_right, op1=ALU.bitwise_and)
                    else:
                        v.tensor_scalar(out=p1d[:], in0=src, scalar1=-sh1,
                                        scalar2=0x01C001C0,
                                        op0=ALU.logical_shift_left, op1=ALU.bitwise_and)
                    sh2 = 4 * jp - 12
                    v.tensor_scalar(out=pintd[:], in0=src, scalar1=-sh2,
                                    scalar2=self.msign_s[:, 0:1],
                                    op0=ALU.logical_shift_left, op1=ALU.bitwise_and)
                    v.tensor_tensor(out=pintd[:], in0=p1d[:],
                                    in1=pintd[:], op=ALU.bitwise_or)
                    v.scalar_tensor_tensor(out=dd3[:, k % 3, jp, :],
                                           in0=pintd[:].bitcast(BF16), scalar=P126,
                                           in1=srepd[:, k % 2, :],
                                           op0=ALU.mult, op1=ALU.mult)
                    if W == 5:
                        dv = dd3[:, k % 3, jp, :].rearrange("p (h two) -> p two h", two=2)
                        v.tensor_copy(out=dd5c[0:64, jp, :], in_=dv[0:64, 0, :])
                        v.tensor_copy(out=dd5c[64:128, jp, :], in_=dv[64:128, 1, :])
                v.engine_nop().then_inc(cV.h, 1)
                cV.n += 1
                tok_ddec[k] = cV.n
            self.on("vector", dn_dec)
            if k + 2 < 24:
                self.on("gpsimd", lambda g, k2=k + 2: dn_prefetch(g, k2))

        def emit_dn_mm(k):
            hg, W = divmod(k, 6)

            def dn_mm(t, k=k, hg=hg, W=W):
                t.wait_ge(cV.h, tok_ddec[k])
                inst = None
                for ts in range(OT):
                    slot = y8[ts]
                    if W == 0:
                        if hg == 0:
                            t.wait_ge(cV.h, tok_act_done)
                        else:
                            t.wait_ge(cA.h, tok_yev[(hg - 1, ts)])
                    if W < 5:
                        for j in range(8):
                            cc = 8 * W + j
                            ddv = dd3[:, k % 3, j % 4, :].rearrange(
                                "p (o two) -> p two o", two=2)
                            inst = t.matmul(slot, act[0:128, cc, ts * 128:(ts + 1) * 128],
                                            ddv[:, j // 4, :],
                                            start=(W == 0 and j == 0), stop=False)
                    else:
                        for m in range(4):
                            inst = t.matmul(slot, act[0:128, 40 + m, ts * 128:(ts + 1) * 128],
                                            dd5c[:, m, :], start=False, stop=(m == 3))
                        inst.then_inc(cP.h, 1)
                        cP.n += 1
                        tok_ymm[(hg, ts)] = cP.n
                if W < 5:
                    inst.then_inc(cP.h, 1)
                    cP.n += 1
                tok_wmm[k] = cP.n
            self.on("tensor", dn_mm)

            if W == 5:
                for ts in range(OT):
                    def dn_yev(s, hg=hg, ts=ts):
                        s.wait_ge(cP.h, tok_ymm[(hg, ts)])
                        s.copy(out=y_bf[:, ts, hg * 512:(hg + 1) * 512],
                               in_=y8[ts]).then_inc(cA.h, 1)
                        cA.n += 1
                        tok_yev[(hg, ts)] = cA.n
                    self.on("scalar", dn_yev)
                emit_ss(hg)

        emit_dn_dec(0)
        for k in range(24):
            if k + 1 < 24:
                emit_dn_dec(k + 1)
            emit_dn_mm(k)
        tok_ssv_last = cV.n
        if self.debug:
            def dall_dbg(g):
                g.wait_ge(cV.h, tok_ssv_last)
                g.dma_start(out=self.dbg["d_ybf"][:],
                            in_=y_bf[:, :, :]).then_inc(cG.h, 16)
                for q_ in range(4):
                    g.dma_start(out=self.dbg["d_dd"][:, q_, :],
                                in_=dd3[:, 23 % 3, q_, :]).then_inc(cG.h, 16)
                g.dma_start(out=self.dbg["d_ss"][:], in_=self.ss[:]).then_inc(cG.h, 16)
                cG.n += 96
            self.on("gpsimd", dall_dbg)
        if sub == "dmm":
            if self.debug:
                def dmm_dbg(g):
                    g.wait_ge(cV.h, tok_ssv_last)
                    g.dma_start(out=self.dbg["d_ybf"][:],
                                in_=y_bf[:, :, :]).then_inc(cG.h, 16)
                    cG.n += 16
                self.on("gpsimd", dmm_dbg)
            self.flush()
            esD.close()
            return

        tok_x2d = [0] * OT
        tok_outs = [0] * OT
        for ts in range(OT):
            def ep_sync(q, ts=ts):
                q.wait_ge(cV.h, self.tok_ssv[(3, ts)])
                if ts >= 1:
                    q.wait_ge(cL.h, tok_outs[ts - 1])
                q.dma_start(out=x2dn, in_=self.x2scr[ts]).then_inc(cL.h, 16)
                cL.n += 16
                tok_x2d[ts] = cL.n
            self.on("sync", ep_sync)

            def ep_rms_v(v, ts=ts):
                p_ = 48 + 4 * (ts % 2)
                v.tensor_scalar(out=self.ss[:, p_:p_ + 1], in0=self.ss[:, 4 * ts:4 * ts + 1],
                                scalar1=1.0 / D, scalar2=1e-6, op0=ALU.mult,
                                op1=ALU.add).then_inc(cV.h, 1)
                cV.n += 1
            self.on("vector", ep_rms_v)
            tok_ms = cV.n

            def ep_rms_a(s, ts=ts, tok_ms=tok_ms):
                b_ = 56 + 4 * (ts % 2)
                m_ = 48 + 4 * (ts % 2)
                s.wait_ge(cV.h, tok_ms)
                s.activation(out=self.ss[:, b_:b_ + 1], in_=self.ss[:, m_:m_ + 1],
                             func=AF.Ln)
                s.nop(cycle_cnt=1, nofuse=True).then_inc(cA.h, 1)
                cA.n += 1
                s.wait_ge(cA.h, cA.n)
                s.activation(out=self.ss[:, b_ + 1:b_ + 2], in_=self.ss[:, b_:b_ + 1],
                             func=AF.Exp, scale=-0.5)
                s.nop(cycle_cnt=1, nofuse=True).then_inc(cA.h, 1)
                cA.n += 1
                s.wait_ge(cA.h, cA.n)
                s.activation(out=self.ss[:, b_ + 2:b_ + 3], in_=self.ss[:, b_ + 1:b_ + 2],
                             func=AF.Exp).then_inc(cA.h, 1)
            self.on("scalar", ep_rms_a)
            cA.n += 1
            tok_rr = cA.n

            def ep_out_v(v, ts=ts, tok_rr=tok_rr):
                v.wait_ge(cA.h, tok_rr)
                v.wait_ge(cL.h, tok_x2d[ts])
                v.scalar_tensor_tensor(out=x2dn, in0=y_bf[:, ts, :],
                                       scalar=self.ss[:, 57 + 4 * (ts % 2):58 + 4 * (ts % 2)],
                                       in1=x2dn, op0=ALU.mult,
                                       op1=ALU.add).then_inc(cV.h, 1)
                cV.n += 1
                self.tok_outv[ts] = cV.n
            self.on("vector", ep_out_v)

            def ep_store(q, ts=ts):
                q.wait_ge(cV.h, self.tok_outv[ts])
                q.dma_start(out=self.out_t[ts], in_=x2dn).then_inc(cL.h, 16)
                cL.n += 16
                tok_outs[ts] = cL.n
            self.on("sync", ep_store)

        self.tok_final = cL.n
        self.flush()
        esD.close()

    def finish(self):
        cL, cG = self.cL, self.cG
        tokL, tokG = cL.n, cG.n

        def fin(g):
            g.wait_ge(cL.h, tokL)
            g.wait_ge(cG.h, tokG)
        self.on("gpsimd", fin)


def _host_prep(inputs):
    X = np.ascontiguousarray(np.asarray(inputs["L0_attn_X"], dtype=np.float32))
    pr = {}
    pr["wq"] = np.ascontiguousarray(np.asarray(inputs["L0_attn_Wq"], np.float32))
    pr["wk"] = np.ascontiguousarray(np.asarray(inputs["L0_attn_Wk"], np.float32))
    pr["wv"] = np.ascontiguousarray(np.asarray(inputs["L0_attn_Wv"], np.float32))
    pr["wo"] = np.ascontiguousarray(np.asarray(inputs["L0_attn_Wo"], np.float32))
    g = np.asarray(inputs["L0_attn_ln_g"], np.float32)
    lb = np.asarray(inputs["L0_attn_ln_b"], np.float32)
    bo = np.asarray(inputs["L0_attn_bo"], np.float32)
    pr["g_cols"] = np.ascontiguousarray(g.reshape(16, 128).T)
    pr["lnb_cols"] = np.ascontiguousarray(lb.reshape(16, 128).T)
    pr["bo_cols"] = np.ascontiguousarray(bo.reshape(16, 128).T)
    pr["b_rows"] = np.ascontiguousarray(np.stack([
        np.asarray(inputs["L0_attn_bq"], np.float32),
        np.asarray(inputs["L0_attn_bk"], np.float32),
        np.asarray(inputs["L0_attn_bv"], np.float32),
        bo]))

    rows = []
    for cc in range(48):
        W, j, w = CHUNKS[cc]
        ilist = [8 * wd + j for wd in range(128 * W, 128 * W + w)]
        rows.extend(ilist)
        rows.extend([IFF + i for i in ilist])
    rows = np.array(rows)
    gup = np.asarray(inputs["L0_ffn_gate_up_packed"])
    pr["gup_t"] = np.ascontiguousarray(gup[rows].T)
    sc = np.asarray(inputs["L0_ffn_gate_up_scales"])
    scp = np.repeat(sc[:, rows], 2, axis=1)
    gur = np.broadcast_to(scp.reshape(2, 8, 1, 22528),
                          (2, 8, 16, 22528)).reshape(2, 128, 22528)
    pr["gur"] = np.ascontiguousarray(gur.transpose(1, 0, 2)).view(np.int16)
    dn = np.asarray(inputs["L0_ffn_down_packed"])
    pr["down_t"] = np.ascontiguousarray(dn.T)
    ds = np.asarray(inputs["L0_ffn_down_scales"])
    dsp = np.zeros((48, D), ds.dtype)
    dsp[:44] = ds
    dsd = np.repeat(dsp, 2, axis=1)
    base = dsd.reshape(6, 8, 2 * D)
    dnr = np.repeat(base, 16, axis=1)
    dnr = dnr.copy()
    dnr[5] = np.tile(np.repeat(base[5][:4], 16, axis=0), (2, 1))
    pr["dnr"] = np.ascontiguousarray(dnr.transpose(1, 0, 2)).view(np.int16)

    pr["identf"] = np.eye(128, dtype=np.float32)
    pr["ones_col"] = np.ones((128, 1), ml_dtypes.bfloat16)
    pr["ones_row"] = np.ones((1, 128), ml_dtypes.bfloat16)
    pr["m_sign"] = np.full((128, 1), np.uint32(0x80008000)).view(np.int32)
    pr["identb"] = np.eye(128, dtype=ml_dtypes.bfloat16)
    return X, pr


def _bmask(par):
    m = np.zeros((16, 128, 128), np.float32)
    kq = np.where(np.arange(128)[:, None] <= np.arange(128)[None, :], 0.0, -1e30)
    for kt in range(8):
        m[kt] = kq
    if par == 0:
        m[8:] = -1e30
    return np.ascontiguousarray(m.transpose(2, 0, 1).astype(ml_dtypes.bfloat16))


def kernel(**inputs):
    X, pr = _host_prep(inputs)
    nc, _ = Build(debug=False).build()
    in_maps = []
    for c in range(8):
        b, par = c // 2, c & 1
        permtok = np.concatenate(
            [np.arange(128) + 128 * (par + 2 * i) for i in range(8)]
            + [np.arange(128) + 128 * ((1 - par) + 2 * i) for i in range(8)])
        im = dict(pr)
        im["x"] = np.ascontiguousarray(X[b][permtok])
        im["bmask"] = _bmask(par)
        in_maps.append(im)
    res = run_bass_kernel_spmd(nc, in_maps, core_ids=list(range(8)))
    out = np.empty((4, S, D), np.float32)
    for c in range(8):
        b, par = c // 2, c & 1
        o = np.asarray(res.results[c]["out"]).reshape(OT, 128, D)
        for i in range(OT):
            out[b, 128 * (par + 2 * i):128 * (par + 2 * i) + 128] = o[i]
    return out
